# version 22
# baseline (speedup 1.0000x reference)
import numpy as np
import ml_dtypes
import concourse.bass as bass
import concourse.mybir as mybir
from concourse.bass_utils import run_bass_kernel_spmd

F32 = mybir.dt.float32
BF16 = mybir.dt.bfloat16
I32 = mybir.dt.int32
AX = mybir.AxisListType.X
AF = mybir.ActivationFunctionType
ALU = mybir.AluOpType

T = 2048
D = 1024
H = 16
INNER = 4096
TB = 256
NTB = T // TB
CMAG = float(np.float32(3 * 2 ** 22))
NEG = -1.0e30
PAR = ("vector", "gpsimd")



class Sched:

    ENGINES = ("sync", "scalar", "vector", "tensor", "gpsimd")
    CENG = ("scalar", "vector", "tensor", "gpsimd")
    DMAQ = ("sync", "scalar", "gpsimd")
    NLANE = 12

    def __init__(self, nc):
        self.nc = nc
        self.ops = []
        self.last_w = {}
        self.last_r = {}
        self.emitted = 0
        self.ccnt = {e: 0 for e in self.CENG}
        self.dtot = {e: 0 for e in self.DMAQ}
        self.dlcnt = {e: [0] * self.NLANE for e in self.DMAQ}
        self.waited = {e: {} for e in self.ENGINES}
        self.drained = {e: -1 for e in self.ENGINES}
        self.engpos = {e: 0 for e in self.ENGINES}
        self.snap = (dict(self.ccnt),
                     {e: list(v) for e, v in self.dlcnt.items()})
        self.csem = {}
        self.dsem = {}

    def make_sems(self, ent):
        for e in self.CENG:
            self.csem[e] = ent(self.nc.semaphore(f"c_{e}"))
        for e in self.DMAQ:
            self.dsem[e] = [ent(self.nc.semaphore(f"d_{e}{i}"))
                            for i in range(self.NLANE)]

    def op(self, engine, fn, reads=(), writes=(), dma=False):
        assert engine in self.ENGINES
        assert dma or engine != "sync"
        i = len(self.ops)
        deps = set()
        for b in reads:
            if b in self.last_w:
                deps.add(self.last_w[b])
        for b in writes:
            if b in self.last_w:
                deps.add(self.last_w[b])
            deps.update(self.last_r.get(b, ()))
        deps.discard(i)
        o = dict(idx=i, engine=engine, fn=fn, deps=sorted(deps),
                 dma=dma, inc=bool(dma), semval=None, pos=None, lane=None)
        if dma:
            o["lane"] = self.dtot[engine] % self.NLANE
            self.dtot[engine] += 1
        self.ops.append(o)
        for b in reads:
            self.last_r.setdefault(b, set()).add(i)
        for b in writes:
            self.last_w[b] = i
            self.last_r[b] = set()
        return i

    def dma(self, queue, out_ap, in_ap, reads=(), writes=(), transpose=False):
        self.op(queue,
                lambda e, o=out_ap, i=in_ap, tr=transpose:
                e.dma_start(o, i, transpose=True) if tr else e.dma_start(o, i),
                reads=reads, writes=writes, dma=True)

    def _fence_waits(self, e, eng, snapc, snapd):
        w = self.waited[e]
        for e2 in self.CENG:
            if e2 != e and snapc[e2] > 0 and w.get(("c", e2), 0) < snapc[e2]:
                eng.wait_ge(self.csem[e2], snapc[e2])
                w[("c", e2)] = snapc[e2]
        for e2 in self.DMAQ:
            for l in range(self.NLANE):
                v = 16 * snapd[e2][l]
                if v > 0 and w.get(("d", e2, l), 0) < v:
                    eng.wait_ge(self.dsem[e2][l], v)
                    w[("d", e2, l)] = v

    def fence(self):
        base = self.emitted
        seg = self.ops[base:]
        if not seg:
            return
        for o in seg:
            for d in o["deps"]:
                if d < base:
                    continue
                od = self.ops[d]
                if od["engine"] != o["engine"] and not od["dma"]:
                    od["inc"] = True
        eng_ops = {e: [] for e in self.ENGINES}
        for o in seg:
            o["pos"] = self.engpos[o["engine"]]
            self.engpos[o["engine"]] += 1
            eng_ops[o["engine"]].append(o)
        for e, lst in eng_ops.items():
            if lst and not lst[-1]["dma"]:
                lst[-1]["inc"] = True
        for o in seg:
            if o["dma"]:
                q, l = o["engine"], o["lane"]
                self.dlcnt[q][l] += 1
                o["semval"] = 16 * self.dlcnt[q][l]
            elif o["inc"]:
                self.ccnt[o["engine"]] += 1
                o["semval"] = self.ccnt[o["engine"]]

        snapc, snapd = self.snap

        def run(e, eng):
            w = self.waited[e]
            self._fence_waits(e, eng, snapc, snapd)
            for o in eng_ops[e]:
                need_drain = False
                for d in o["deps"]:
                    if d < base:
                        continue
                    od = self.ops[d]
                    if od["engine"] == e and not od["dma"]:
                        if od["pos"] > self.drained[e]:
                            need_drain = True
                        continue
                    assert od["inc"]
                    if od["dma"]:
                        k = ("d", od["engine"], od["lane"])
                        sem = self.dsem[od["engine"]][od["lane"]]
                    else:
                        k = ("c", od["engine"])
                        sem = self.csem[od["engine"]]
                    if w.get(k, 0) < od["semval"]:
                        eng.wait_ge(sem, od["semval"])
                        w[k] = od["semval"]
                if need_drain:
                    eng.drain()
                    self.drained[e] = o["pos"] - 1
                if o["dma"]:
                    k = ("d", e, o["lane"])
                    prev = o["semval"] - 16
                    if prev > 0 and w.get(k, 0) < prev:
                        eng.wait_ge(self.dsem[e][o["lane"]], prev)
                        w[k] = prev
                ins = o["fn"](eng)
                if o["dma"]:
                    ins.then_inc(self.dsem[e][o["lane"]], 16)
                elif o["inc"]:
                    ins.then_inc(self.csem[o["engine"]], 1)

        with self.nc.Block() as blk:
            for e in self.ENGINES:
                if eng_ops[e]:
                    getattr(blk, e)(lambda eng, e=e: run(e, eng))

        self.emitted = len(self.ops)
        self.snap = (dict(self.ccnt),
                     {e: list(v) for e, v in self.dlcnt.items()})
        self.last_w = {}
        self.last_r = {}
        for e in self.ENGINES:
            self.drained[e] = self.engpos[e] - 1

    def finalize(self):
        snapc = dict(self.ccnt)
        snapd = {e: list(v) for e, v in self.dlcnt.items()}
        with self.nc.Block() as blk:
            for e in self.ENGINES:
                getattr(blk, e)(
                    lambda eng, e=e: self._fence_waits(e, eng, snapc, snapd))




def build_nc():
    nc = bass.Bass("TRN2", target_bir_lowering=False)

    x_h = nc.dram_tensor("x", [T, D], F32, kind="ExternalInput")
    xg_h = nc.dram_tensor("xg", [16, D], F32, kind="ExternalInput")
    wq_h = nc.dram_tensor("wq", [D, D], F32, kind="ExternalInput")
    wk_h = nc.dram_tensor("wk", [D, D], F32, kind="ExternalInput")
    wv_h = nc.dram_tensor("wv", [D, D], F32, kind="ExternalInput")
    wo_h = nc.dram_tensor("wo", [D, D], F32, kind="ExternalInput")
    wg_h = nc.dram_tensor("wg", [INNER, D], F32, kind="ExternalInput")
    wu_h = nc.dram_tensor("wu", [INNER, D], F32, kind="ExternalInput")
    wd_h = nc.dram_tensor("wd", [D, INNER], F32, kind="ExternalInput")
    eye_h = nc.dram_tensor("eye", [128, 128], F32, kind="ExternalInput")
    onec_h = nc.dram_tensor("onec", [128, 1], F32, kind="ExternalInput")
    oner_h = nc.dram_tensor("oner", [1, 128], F32, kind="ExternalInput")
    maskn_h = nc.dram_tensor("maskn", [128, 2560], BF16, kind="ExternalInput")
    gmask_h = nc.dram_tensor("gmask", [128, 16], BF16, kind="ExternalInput")
    y_h = nc.dram_tensor("y", [T, D], F32, kind="ExternalOutput")
    ctx_scr = nc.dram_tensor("ctx_scr", [16 * D, 128], F32)
    a_scr = nc.dram_tensor("a_scr", [T, D], BF16)
    x1_scr = nc.dram_tensor("x1_scr", [T, D], F32)
    tg_scr = nc.dram_tensor("tg_scr", [INNER, D], BF16)
    tu_scr = nc.dram_tensor("tu_scr", [INNER, D], BF16)
    td_scr = nc.dram_tensor("td_scr", [D, INNER], BF16)
    h_scr = nc.dram_tensor("h_scr", [INNER, T], F32)
    row_scr = nc.dram_tensor("row_scr", [4, T], F32)

    from contextlib import ExitStack
    with ExitStack() as stack:
        ent = stack.enter_context
        cst = ent(nc.sbuf_tensor("cst", [128, 384], F32))
        sc = ent(nc.sbuf_tensor("sc", [128, 128], F32))
        i0b = ent(nc.sbuf_tensor("i0b", [128, 1024], I32))
        pb = [ent(nc.psum_tensor(f"pb{i}", [128, 512], F32)) for i in range(8)]
        pb6, pb7 = pb[6], pb[7]
        S = Sched(nc)
        S.make_sems(ent)

        eye = cst[:, 0:128]
        oner = cst[0:1, 128:256]
        onec = cst[:, 256:257]
        tau_v = [cst[:, 260 + j:261 + j] for j in range(3)]
        mg127_v = cst[:, 263:264]

        st6a = sc[:, 48:60]
        st2a = sc[:, 60:62]
        lnwa = sc[:, 62:64]
        maxh_nat = sc[:, 32:48]
        maxr_nat = sc[:, 64:80]

        def mm(out_ap, lhs_ap, rhs_ap, reads, writes, start=True, stop=True):
            S.op("tensor", lambda t, o=out_ap, l=lhs_ap, r=rhs_ap,
                 st=start, sp=stop: t.matmul(o, l, r, start=st, stop=sp),
                 reads=reads, writes=writes)

        def tpose(out_ap, in_ap, reads, writes, pb0_=0, pn=128):
            ey = eye[pb0_:pb0_ + pn, pb0_:pb0_ + pn]
            S.op("tensor", lambda t, o=out_ap, i=in_ap, e2=ey: t.transpose(o, i, e2),
                 reads=tuple(reads) + ("cst",), writes=writes)

        def cop(eng, dst, src, reads, writes):
            if eng == "scalar":
                S.op("scalar", lambda s, d=dst, s2=src: s.copy(d, s2),
                     reads=reads, writes=writes)
            else:
                S.op("vector", lambda v, d=dst, s2=src: v.tensor_copy(d, s2),
                     reads=reads, writes=writes)

        EV = ("vector", "scalar")

        def ln_tile(xap, xkeys, P=128):
            S.op("vector", lambda v, x=xap, p2=P: v.bn_stats(st6a[0:p2, 0:6],
                                                             x[:, 0:512]),
                 reads=xkeys, writes=("st6",))
            S.op("vector", lambda v, x=xap, p2=P: v.bn_stats(st6a[0:p2, 6:12],
                                                             x[:, 512:1024]),
                 reads=xkeys, writes=("st6",))
            S.op("vector", lambda v, p2=P: v.bn_aggr(st2a[0:p2, :], st6a[0:p2, :]),
                 reads=("st6",), writes=("st2",))
            S.op("vector", lambda v, p2=P: v.tensor_single_scalar(
                lnwa[0:p2, 0:1], st2a[0:p2, 1:2], 1e-5, ALU.add),
                 reads=("st2",), writes=("lnw",))
            S.op("scalar", lambda s, p2=P: s.activation(lnwa[0:p2, 1:2],
                                                        lnwa[0:p2, 0:1], AF.Ln),
                 reads=("lnw",), writes=("lnw",))
            S.op("vector", lambda v, p2=P: v.tensor_single_scalar(
                lnwa[0:p2, 0:1], lnwa[0:p2, 1:2], -0.5, ALU.mult),
                 reads=("lnw",), writes=("lnw",))
            S.op("scalar", lambda s, p2=P: s.activation(lnwa[0:p2, 0:1],
                                                        lnwa[0:p2, 0:1], AF.Exp),
                 reads=("lnw",), writes=("lnw",))
            return st2a[:, 0:1], lnwa[:, 0:1]

        def wt_prep(af, wh, wbase, wst, pitch, scale, keypfx):
            for r in range(8):
                p = r % 2
                buf = af[:, wst + p * 1024: wst + (p + 1) * 1024]
                S.dma("sync", buf, wh.ap()[r * 128:(r + 1) * 128, :],
                      (f"{keypfx}{p}",), (f"{keypfx}{p}",))
                for c in range(8):
                    bk = pb[4 + 2 * p + c // 4]
                    tpose(bk[:, (c % 4) * 128:(c % 4) * 128 + 128],
                          buf[:, c * 128:(c + 1) * 128],
                          (f"{keypfx}{p}",), (f"pb{4 + 2 * p + c // 4}",))
                for half in range(2):
                    bki = 4 + 2 * p + half
                    dst = bass.AP(af, wbase + (half * 4) * 1024 + r * 128,
                                  [[pitch, 128], [1024, 4], [1, 128]])
                    src = pb[bki][:, 0:512].rearrange("p (g k) -> p g k", g=4)
                    eng = EV[(r * 2 + half) % 2]
                    if scale is None:
                        cop(eng, dst, src, (f"pb{bki}",), ("wf",))
                    elif eng == "scalar":
                        S.op("scalar", lambda s, d=dst, s2=src, sc2=scale:
                             s.mul(d, s2, sc2),
                             reads=(f"pb{bki}",), writes=("wf",))
                    else:
                        S.op("vector", lambda v, d=dst, s2=src, sc2=scale:
                             v.tensor_single_scalar(d, s2, sc2, ALU.mult),
                             reads=(f"pb{bki}",), writes=("wf",))

        wtiles = []
        for wi, (wh, scr3) in enumerate(
                ((wg_h, tg_scr), (wu_h, tu_scr), (wd_h, td_scr))):
            if wi < 2:
                for r in range(32):
                    wtiles.append((wi, wh.ap()[r * 128:(r + 1) * 128, :],
                                   scr3.ap()[r * 128:(r + 1) * 128, :]))
            else:
                for r in range(8):
                    for q in range(4):
                        wtiles.append((wi,
                                       wh.ap()[r * 128:(r + 1) * 128,
                                               q * 1024:(q + 1) * 1024],
                                       scr3.ap()[r * 128:(r + 1) * 128,
                                                 q * 1024:(q + 1) * 1024]))
        absac = sc[:, 32:35]

        PITA = 44160
        with (
            nc.sbuf_tensor("afA", [128, PITA], F32) as afA,
            nc.sbuf_tensor("mk", [128, 2560], BF16) as mkT,
            nc.sbuf_tensor("gm", [128, 16], BF16) as gmT,
            nc.sbuf_tensor("afW1", [128, 2048], F32) as afW1,
        ):
            WQ, WK, WV = 0, 8192, 16384
            XNT = 24576
            QT0 = 26624
            KT0 = 28672
            VT0 = 31232
            SCB, PTB = 35328, 37888
            XT, XN = 39936, 41984
            KGT = 44032
            WST = SCB
            XG = PTB
            XGT = PTB + 1024

            S.dma("sync", cst[:, 0:128], eye_h.ap(), (), ("cst",))
            S.dma("sync", cst[0:1, 128:256], oner_h.ap(), (), ("cst",))
            S.dma("sync", cst[:, 256:257], onec_h.ap(), (), ("cst",))
            S.dma("sync", mkT[:, 0:2560], maskn_h.ap(), (), ("mk",))
            S.dma("sync", gmT[:, 0:16], gmask_h.ap(), (), ("mk",))
            S.op("vector", lambda v: v.memset(cst[:, 267:268], CMAG),
                 reads=(), writes=("cst",))
            S.op("vector", lambda v: v.memset(cst[:, 268:269], -CMAG),
                 reads=(), writes=("cst",))

            wt_prep(afA, wq_h, WQ, WST, PITA, 0.125, "wsa")
            wt_prep(afA, wk_h, WK, WST, PITA, None, "wsa")
            wt_prep(afA, wv_h, WV, WST, PITA, None, "wsa")

            S.dma("sync", afA[0:16, XG:XG + 1024], xg_h.ap(), (), ("xg",))
            mean, rstd = ln_tile(afA[0:16, XG:XG + 1024], ("xg",), P=16)
            xgn = afA[0:16, XN:XN + 1024]
            S.op("vector", lambda v, m=mean, r=rstd: v.tensor_scalar(
                xgn, afA[0:16, XG:XG + 1024], m[0:16, :], r[0:16, :],
                ALU.subtract, ALU.mult),
                 reads=("xg", "st2", "lnw"), writes=("xn",))
            for c in range(8):
                tpose(pb6[0:128, c * 16:(c + 1) * 16],
                      afA[0:16, XN + c * 128:XN + (c + 1) * 128], ("xn",), ("pb6",),
                      pn=16)
            S.op("vector", lambda v: v.tensor_copy(afA[:, XGT:XGT + 128],
                                                   pb6[:, 0:128]),
                 reads=("pb6",), writes=("xgt",))

            def xgt_c(c):
                return afA[:, XGT + c * 16:XGT + (c + 1) * 16]

            for n in range(2):
                for c in range(8):
                    mm(pb[n][0:16, 0:512], xgt_c(c),
                       afA[:, WK + c * 1024 + n * 512:WK + c * 1024 + (n + 1) * 512],
                       ("xgt", "wf"), (f"pb{n}",), start=(c == 0), stop=(c == 7))
            for n in range(2):
                S.op("vector", lambda v, n2=n: v.tensor_copy(
                    afA[0:16, XG + n2 * 512:XG + (n2 + 1) * 512],
                    pb[n2][0:16, 0:512]),
                     reads=(f"pb{n}",), writes=("xg",))
            for m in range(8):
                tpose(pb6[0:128, m * 16:(m + 1) * 16],
                      afA[0:16, XG + m * 128:XG + (m + 1) * 128], ("xg",), ("pb6",),
                      pn=16)
            S.op("vector", lambda v: v.tensor_copy(afA[:, KGT:KGT + 128],
                                                   pb6[:, 0:128]),
                 reads=("pb6",), writes=("kgt",))
            for n in range(2):
                for c in range(8):
                    mm(pb[2 + n][0:16, 0:512], xgt_c(c),
                       afA[:, WV + c * 1024 + n * 512:WV + c * 1024 + (n + 1) * 512],
                       ("xgt", "wf"), (f"pb{2 + n}",), start=(c == 0), stop=(c == 7))
            for n in range(2):
                S.op("vector", lambda v, n2=n: v.tensor_copy(
                    afA[0:16, XG + n2 * 512:XG + (n2 + 1) * 512],
                    pb[2 + n2][0:16, 0:512]),
                     reads=(f"pb{2 + n}",), writes=("xg",))
            for tg in range(4):
                S.dma("gpsimd", afA[64:80, VT0 + tg * 1024:VT0 + (tg + 1) * 1024],
                      afA[0:16, XG:XG + 1024], ("xg",), ("vg",))
            for m in range(8):
                for tg in range(4):
                    cop(EV[(m + tg) % 2],
                        afA[:, KT0 + m * 320 + tg * 80 + 64:
                            KT0 + m * 320 + tg * 80 + 80],
                        afA[:, KGT + m * 16:KGT + (m + 1) * 16],
                        ("kgt",), ("ktg",))
            for wi in range(3):
                S.op("vector", lambda v, w2=wi: v.memset(absac[:, w2:w2 + 1], 0.0),
                     reads=(), writes=(f"aa{wi}",))
            S.fence()

            def w_absum_tiles(lo, hi):
                for t3 in range(lo, hi):
                    wi, src_ap, _ = wtiles[t3]
                    p = t3 % 2
                    buf = afW1[:, p * 1024:(p + 1) * 1024]
                    S.dma("sync", buf, src_ap, (f"wpa{p}",), (f"wpa{p}",))
                    red = sc[:, 35 + p:36 + p]
                    S.op("vector", lambda v, b=buf, rd=red: v.reduce_sum(
                        rd, b, axis=AX, apply_absolute_value=True),
                         reads=(f"wpa{p}",), writes=(f"red{p}",))
                    S.op("vector", lambda v, rd=red, w2=wi: v.tensor_tensor(
                        absac[:, w2:w2 + 1], absac[:, w2:w2 + 1], rd, ALU.add),
                         reads=(f"red{p}", f"aa{wi}"), writes=(f"aa{wi}",))

            def ln_prefetch(tb2):
                for ti in range(2):
                    tok0 = tb2 * TB + ti * 128
                    xt = afA[:, XT + ti * 1024:XT + (ti + 1) * 1024]
                    xn = afA[:, XN + ti * 1024:XN + (ti + 1) * 1024]
                    S.dma("sync", xt, x_h.ap()[tok0:tok0 + 128, :],
                          (f"xta{ti}",), (f"xta{ti}",))
                    mean, rstd = ln_tile(xt, (f"xta{ti}",))
                    S.op("vector", lambda v, x=xt, o=xn, m=mean, r=rstd:
                         v.tensor_scalar(o, x, m, r, ALU.subtract, ALU.mult),
                         reads=(f"xta{ti}", "st2", "lnw"), writes=(f"xn{ti}",))

            for tb in range(NTB):
                if tb == 0:
                    ln_prefetch(0)
                for ti in range(2):
                    xn = afA[:, XN + ti * 1024:XN + (ti + 1) * 1024]
                    for c in range(8):
                        tpose(pb[6 + c // 4][:, (c % 4) * 128:(c % 4) * 128 + 128],
                              xn[:, c * 128:(c + 1) * 128],
                              (f"xn{ti}",), (f"pb{6 + c // 4}",))
                    for half in range(2):
                        dst = bass.AP(afA, XNT + (half * 4) * 256 + ti * 128,
                                      [[PITA, 128], [256, 4], [1, 128]])
                        src = pb[6 + half][:, 0:512].rearrange(
                            "p (g k) -> p g k", g=4)
                        cop("scalar", dst, src, (f"pb{6 + half}",), ("xnt",))

                if tb + 1 < NTB:
                    ln_prefetch(tb + 1)
                for wbase, dkey in ((WQ, "qt"), (WK, "kt")):
                    for m in range(8):
                        bk = pb[m % 4]
                        bcol = (m // 4) * 256
                        for c in range(8):
                            mm(bk[:, bcol:bcol + 256],
                               afA[:, wbase + c * 1024 + m * 128:
                                   wbase + c * 1024 + (m + 1) * 128],
                               afA[:, XNT + c * 256:XNT + (c + 1) * 256],
                               ("wf", "xnt"), (f"pb{m % 4}",),
                               start=(c == 0), stop=(c == 7))
                    for b2 in range(4):
                        if wbase == WQ:
                            dst = bass.AP(afA, QT0 + b2 * 256,
                                          [[PITA, 128], [1024, 2], [1, 256]])
                            src = pb[b2][:, 0:512].rearrange(
                                "p (g k) -> p g k", g=2)
                        else:
                            dst = bass.AP(afA, KT0 + b2 * 320,
                                          [[PITA, 128], [1280, 2], [80, 4], [1, 64]])
                            src = pb[b2][:, 0:512].rearrange(
                                "p (g t k) -> p g t k", g=2, t=4)
                        cop("scalar", dst, src, (f"pb{b2}",), (dkey,))
                for tg in range(4):
                    for n in range(2):
                        bki = 4 + (tg * 2 + n) % 4
                        for c in range(8):
                            mm(pb[bki][0:64, 0:512],
                               afA[:, XNT + c * 256 + tg * 64:
                                   XNT + c * 256 + tg * 64 + 64],
                               afA[:, WV + c * 1024 + n * 512:
                                   WV + c * 1024 + (n + 1) * 512],
                               ("wf", "xnt"), (f"pb{bki}",),
                               start=(c == 0), stop=(c == 7))
                        cop(EV[n],
                            afA[0:64, VT0 + tg * 1024 + n * 512:
                                VT0 + tg * 1024 + (n + 1) * 512],
                            pb[bki][0:64, 0:512],
                            (f"pb{bki}",), ("vt",))

                def scbuf(pr):
                    return afA[pr * 64:(pr + 1) * 64, SCB:SCB + 2560]

                for pr in range(2):
                    scb = scbuf(pr)
                    pk = f"scb{pr}"
                    for h2 in range(H):
                        qrow = (h2 % 2) * 64
                        for t2 in range(2):
                            ci = 2 * h2 + t2
                            bk = pb[ci // 6]
                            cc = (ci % 6) * 80
                            tg = pr * 2 + t2
                            lhs = afA[qrow:qrow + 64,
                                      QT0 + (h2 // 2) * 256 + tg * 64:
                                      QT0 + (h2 // 2) * 256 + tg * 64 + 64]
                            rhs = afA[qrow:qrow + 64,
                                      KT0 + (h2 // 2) * 320 + tg * 80:
                                      KT0 + (h2 // 2) * 320 + tg * 80 + 80]
                            mm(bk[0:64, cc:cc + 80], lhs, rhs,
                               ("qt", "kt", "ktg"), (f"pb{ci // 6}",))
                    for b2 in range(6):
                        w3 = 480 if b2 < 5 else 160
                        S.op("vector", lambda v, b=pb[b2], off=b2 * 480, w4=w3,
                             s4=scb, p2=pr:
                             v.tensor_tensor(s4[:, off:off + w4], b[0:64, 0:w4],
                                             mkT[p2 * 64:p2 * 64 + 64,
                                                 off:off + w4],
                                             ALU.add),
                             reads=(f"pb{b2}", "mk"), writes=(pk,))
                    if tb == 0 and pr == 0:
                        glob3 = bass.AP(afA, SCB + 64, [[PITA, 64], [160, 16],
                                                        [1, 16]])
                        gm3 = bass.AP(gmT, 0, [[16, 64], [0, 16], [1, 16]])
                        S.op("vector", lambda v, g3=glob3, g4=gm3: v.tensor_tensor(
                            g3, g3, g4, ALU.add),
                             reads=(pk, "mk"), writes=(pk,))
                    scb3 = scb.rearrange("p (h k) -> p h k", k=80)
                    mx3 = sc[pr * 64:pr * 64 + 64, 0:32].rearrange(
                        "p (h k) -> p h k", k=1)
                    S.op("vector", lambda v, o=mx3, i=scb3:
                         v.reduce_max(o, i, axis=AX),
                         reads=(pk,), writes=(f"mx{pr}",))
                    mxb = bass.AP(sc, pr * 64 * 128, [[128, 64], [1, 32], [0, 80]])
                    S.op("vector", lambda v, m3=mxb, s3=scb3: v.tensor_tensor(
                        s3, s3, m3, ALU.subtract),
                         reads=(pk, f"mx{pr}"), writes=(pk,))
                    S.op("scalar", lambda s, s3=scb: s.activation(s3, s3, AF.Exp),
                         reads=(pk,), writes=(pk,))
                    rs2 = sc[pr * 64:pr * 64 + 64, 96:128]
                    rs3 = rs2.rearrange("p (h k) -> p h k", k=1)
                    S.op("vector", lambda v, o=rs3, i=scb3:
                         v.reduce_sum(o, i, axis=AX),
                         reads=(pk,), writes=(f"rs{pr}",))
                    S.op("vector", lambda v, r=rs2: v.reciprocal(r, r),
                         reads=(f"rs{pr}",), writes=(f"rs{pr}",))
                    rsb = bass.AP(sc, pr * 64 * 128 + 96,
                                  [[128, 64], [1, 32], [0, 80]])
                    S.op("vector", lambda v, r=rsb, s3=scb3: v.tensor_tensor(
                        s3, s3, r, ALU.mult),
                         reads=(pk, f"rs{pr}"), writes=(pk,))

                PTBANKS = (6, 7, 0, 1)
                for pr in range(2):
                    scb = scbuf(pr)
                    pk = f"scb{pr}"
                    for ci in range(32):
                        bk = pb[PTBANKS[ci // 8]]
                        tpose(bk[0:80, (ci % 8) * 64:(ci % 8) * 64 + 64],
                              scb[:, ci * 80:ci * 80 + 80],
                              (pk,), (f"pb{PTBANKS[ci // 8]}",),
                              pb0_=pr * 64, pn=64)
                    for g2 in range(4):
                        S.op("scalar", lambda s, b=pb[PTBANKS[g2]], off=g2 * 512:
                             s.copy(afA[0:80, PTB + off:PTB + off + 512],
                                    b[0:80, 0:512]),
                             reads=(f"pb{PTBANKS[g2]}",), writes=("ptb",))
                    for h2 in range(H):
                        for t2 in range(2):
                            ci = 2 * h2 + t2
                            bk = pb[2 + ci // 8]
                            tg = pr * 2 + t2
                            lhsl = afA[0:80,
                                       VT0 + tg * 1024 + h2 * 64:
                                       VT0 + tg * 1024 + h2 * 64 + 64]
                            rhsl = afA[0:80, PTB + ci * 64:PTB + ci * 64 + 64]
                            mm(bk[0:64, (ci % 8) * 64:(ci % 8) * 64 + 64],
                               lhsl, rhsl, ("vt", "vg", "ptb"),
                               (f"pb{2 + ci // 8}",))
                    for g2 in range(4):
                        cop(EV[g2 % 2],
                            scb[:, g2 * 512:g2 * 512 + 512],
                            pb[2 + g2][0:64, 0:512],
                            (f"pb{2 + g2}",), (pk,))
                    pair = tb * 2 + pr
                    src = bass.AP(afA, pr * 64 * PITA + SCB,
                                  [[PITA, 64], [128, 16], [1, 128]])
                    dst = bass.AP(ctx_scr, pair * 131072,
                                  [[128, 64], [8192, 16], [1, 128]])
                    S.dma("gpsimd", dst, src, (pk,), ("ctxw",))
                w_absum_tiles(tb * 12, (tb + 1) * 12)
            S.fence()

        PITB = 34880
        with (
            nc.sbuf_tensor("afB", [128, PITB], F32) as afB,
            nc.sbuf_tensor("abB", [128, 12288], BF16) as abB,
            nc.sbuf_tensor("afW2", [128, 2048], F32) as afW2,
            nc.sbuf_tensor("abW", [128, 6144], BF16) as abW,
        ):
            WO, WST2 = 0, 8192
            CTXF, XT2, X1B, XN2, DQ = 10240, 14336, 18432, 22528, 26624
            GB2 = 30720
            LNS = 34816
            MD, ATI = 0, 8192

            mm(pb6[0:1, 0:3], onec, absac[:, 0:3], ("cst",), ("pb6",))
            S.op("vector", lambda v: v.tensor_copy(sc[0:1, 8:11], pb6[0:1, 0:3]),
                 reads=("pb6",), writes=("wm",))
            S.op("vector", lambda v: v.tensor_scalar(
                sc[0:1, 8:11], sc[0:1, 8:11], 1.0 / (INNER * D), 1e-5,
                ALU.mult, ALU.max),
                 reads=("wm",), writes=("wm",))
            S.op("vector", lambda v: v.tensor_single_scalar(
                sc[0:1, 11:14], sc[0:1, 8:11], 0.5, ALU.mult),
                 reads=("wm",), writes=("tau",))
            S.op("vector", lambda v: v.tensor_single_scalar(
                sc[0:1, 14:15], sc[0:1, 8:9], 1.0 / 254.0, ALU.mult),
                 reads=("wm",), writes=("tau",))
            S.op("vector", lambda v: v.reciprocal(sc[0:1, 17:18], sc[0:1, 8:9]),
                 reads=("wm",), writes=("mumg",))
            S.op("vector", lambda v: v.tensor_tensor(
                sc[0:1, 17:18], sc[0:1, 17:18], sc[0:1, 9:10], ALU.mult),
                 reads=("mumg", "wm"), writes=("mumg",))
            S.op("vector", lambda v: v.tensor_single_scalar(
                sc[0:1, 16:17], sc[0:1, 10:11], 1.0 / 254.0, ALU.mult),
                 reads=("wm",), writes=("md127",))
            mm(pb7[0:128, 0:4], oner, sc[0:1, 11:15], ("tau", "cst"), ("pb7",))
            S.op("vector", lambda v: v.tensor_copy(cst[:, 260:264], pb7[:, 0:4]),
                 reads=("pb7",), writes=("cst",))
            S.op("vector", lambda v: v.tensor_single_scalar(
                cst[:, 264:267], cst[:, 260:263], -1.0, ALU.mult),
                 reads=("cst",), writes=("cst",))

            def w_tern_tiles(lo, hi):
                for t3 in range(lo, hi):
                    wi, src_ap, dst_ap = wtiles[t3]
                    p = t3 % 2
                    buf = afW2[:, p * 1024:(p + 1) * 1024]
                    S.dma("sync", buf, src_ap,
                          (f"wpa{p}",), (f"wpa{p}",))
                    s1 = abW[:, p * 1024:(p + 1) * 1024]
                    s2b = abW[:, 2048 + p * 1024:2048 + (p + 1) * 1024]
                    tn = abW[:, 4096 + p * 1024:4096 + (p + 1) * 1024]
                    S.op("scalar", lambda s, o=s1, b=buf, w2=wi: s.sign(
                        o, b, bias=cst[:, 264 + w2:265 + w2]),
                         reads=(f"wpa{p}", "cst"), writes=(f"tp{p}",))
                    S.op("scalar", lambda s, o=s2b, b=buf, w2=wi: s.sign(
                        o, b, bias=cst[:, 260 + w2:261 + w2]),
                         reads=(f"wpa{p}", "cst"), writes=(f"tq{p}",))
                    S.op("gpsimd", lambda g, o=tn, a=s1, b=s2b: g.tensor_tensor(
                        o, a, b, ALU.add),
                         reads=(f"tp{p}", f"tq{p}"), writes=(f"tn{p}",))
                    S.dma("sync", dst_ap, tn, (f"tn{p}",), ("ts",))

            wt_prep(afB, wo_h, WO, WST2, PITB, None, "wsb")

            def btile(ti):
                p = ti % 4
                return dict(
                    p=p, tok0=ti * 128,
                    cf=afB[:, CTXF + p * 1024:CTXF + (p + 1) * 1024],
                    xt=afB[:, XT2 + p * 1024:XT2 + (p + 1) * 1024],
                    x1=afB[:, X1B + p * 1024:X1B + (p + 1) * 1024],
                    xn2=afB[:, XN2 + p * 1024:XN2 + (p + 1) * 1024],
                    ic=afB[:, DQ + p * 1024:DQ + (p + 1) * 1024],
                    gq=afB[:, GB2 + p * 1024:GB2 + (p + 1) * 1024],
                    mp=abB[:, MD + p * 1024:MD + (p + 1) * 1024],
                    mdn=abB[:, MD + 4096 + p * 1024:MD + 4096 + (p + 1) * 1024],
                    at=abB[:, ATI + p * 1024:ATI + (p + 1) * 1024],
                    st6=afB[:, LNS + p * 16:LNS + p * 16 + 12],
                    st2=afB[:, LNS + p * 16 + 12:LNS + p * 16 + 14],
                    lnw=afB[:, LNS + p * 16 + 14:LNS + p * 16 + 16],
                    mr=maxr_nat[:, ti:ti + 1], sx=sc[:, 80 + ti:81 + ti])

            for g in range(4):
                tis = range(g * 4, g * 4 + 4)
                bt = {ti: btile(ti) for ti in tis}
                w_tern_tiles(g * 24, (g + 1) * 24)
                for ti in tis:
                    b = bt[ti]
                    p = b["p"]
                    csrc = bass.AP(ctx_scr, ti * 131072,
                                   [[128, 128], [16384, 8], [1, 128]])
                    cdst = bass.AP(afB, CTXF + p * 1024,
                                   [[PITB, 128], [128, 8], [1, 128]])
                    S.dma("sync", cdst, csrc, (f"cf{p}",), (f"cf{p}",))
                    S.dma("sync", b["xt"], x_h.ap()[b["tok0"]:b["tok0"] + 128, :],
                          (f"xtb{p}",), (f"xtb{p}",))
                for ti in tis:
                    b = bt[ti]
                    p = b["p"]
                    wb0 = 4 + 2 * (ti % 2)
                    for n in range(2):
                        bk = pb[wb0 + n]
                        for c in range(8):
                            mm(bk[:, 0:512], b["cf"][:, c * 128:(c + 1) * 128],
                               afB[:, WO + c * 1024 + n * 512:
                                   WO + c * 1024 + (n + 1) * 512],
                               (f"cf{p}", "wf"), (f"pb{wb0 + n}",),
                               start=(c == 0), stop=(c == 7))
                    for n in range(2):
                        S.op("vector", lambda v, b2=pb[wb0 + n], n2=n,
                             x2=b["x1"], xx=b["xt"]:
                             v.tensor_tensor(x2[:, n2 * 512:(n2 + 1) * 512],
                                             b2[:, 0:512],
                                             xx[:, n2 * 512:(n2 + 1) * 512],
                                             ALU.add),
                             reads=(f"pb{wb0 + n}", f"xtb{p}"), writes=(f"x1{p}",))
                    S.dma("sync", x1_scr.ap()[b["tok0"]:b["tok0"] + 128, :],
                          b["x1"], (f"x1{p}",), ("x1s",))
                for ti in tis:
                    b = bt[ti]
                    p = b["p"]
                    S.op("vector", lambda v, x=b["x1"], o=b["st6"]: v.bn_stats(
                        o[:, 0:6], x[:, 0:512]),
                         reads=(f"x1{p}",), writes=(f"st6{p}",))
                    S.op("vector", lambda v, x=b["x1"], o=b["st6"]: v.bn_stats(
                        o[:, 6:12], x[:, 512:1024]),
                         reads=(f"x1{p}",), writes=(f"st6{p}",))
                    S.op("vector", lambda v, o=b["st2"], i=b["st6"]: v.bn_aggr(o, i),
                         reads=(f"st6{p}",), writes=(f"st2{p}",))
                    S.op("vector", lambda v, o=b["lnw"], i=b["st2"]:
                         v.tensor_single_scalar(o[:, 0:1], i[:, 1:2], 1e-5, ALU.add),
                         reads=(f"st2{p}",), writes=(f"lnw{p}",))
                    S.op("vector", lambda v, o=b["lnw"]:
                         v.reciprocal(o[:, 1:2], o[:, 0:1]),
                         reads=(f"lnw{p}",), writes=(f"lnw{p}",))
                for ti in tis:
                    b = bt[ti]
                    p = b["p"]
                    S.op("scalar", lambda s, o=b["lnw"]: s.sqrt(o[:, 0:1], o[:, 1:2]),
                         reads=(f"lnw{p}",), writes=(f"lnw{p}",))
                for ti in tis:
                    b = bt[ti]
                    p = b["p"]
                    S.op("vector", lambda v, x2=b["x1"], o=b["xn2"], m=b["st2"],
                         r=b["lnw"]:
                         v.tensor_scalar(o, x2, m[:, 0:1], r[:, 0:1],
                                         ALU.subtract, ALU.mult),
                         reads=(f"x1{p}", f"st2{p}", f"lnw{p}"), writes=(f"xn2{p}",))
                    S.op("vector", lambda v, m3=b["mr"], x2=b["xn2"]: v.reduce_max(
                        m3, x2, axis=AX, apply_absolute_value=True),
                         reads=(f"xn2{p}",), writes=("maxr",))
                    S.op("vector", lambda v, m3=b["mr"]: v.tensor_single_scalar(
                        m3, m3, 1e-5, ALU.max),
                         reads=("maxr",), writes=("maxr",))
                    S.op("vector", lambda v, m3=b["mr"], s3=b["sx"]:
                         v.reciprocal(s3, m3),
                         reads=("maxr",), writes=("sx",))
                    S.op("vector", lambda v, s3=b["sx"]: v.tensor_single_scalar(
                        s3, s3, 127.0, ALU.mult),
                         reads=("sx",), writes=("sx",))
                i0f = i0b[:, 0:1024].bitcast(F32)
                for ti in tis:
                    b = bt[ti]
                    p = b["p"]
                    S.op("scalar", lambda s, o=b["gq"], x2=b["xn2"], s3=b["sx"]:
                         s.mul(o, x2, s3),
                         reads=(f"xn2{p}", "sx"), writes=(f"gq{p}",))
                for ti in tis:
                    b = bt[ti]
                    p = b["p"]
                    S.op("scalar", lambda s, o=i0f, i=b["gq"]: s.add(
                        o, i, cst[:, 267:268]),
                         reads=(f"gq{p}", "cst", "i0"), writes=("i0",))
                    S.op("scalar", lambda s, o=b["ic"], i=i0f: s.add(
                        o, i, cst[:, 268:269]),
                         reads=("i0", "cst"), writes=(f"ic{p}",))
                for ti in tis:
                    b = bt[ti]
                    p = b["p"]
                    S.op(PAR[p % 2], lambda v, o=b["xt"], i=b["ic"], g2=b["gq"]:
                         v.tensor_sub(o, g2, i),
                         reads=(f"ic{p}", f"gq{p}"), writes=(f"xtb{p}",))
                for ti in tis:
                    b = bt[ti]
                    p = b["p"]
                    S.op("vector", lambda v, o=b["mp"], i=b["xt"]:
                         v.tensor_single_scalar(o, i, 0.5, ALU.is_gt),
                         reads=(f"xtb{p}",), writes=(f"mp{p}",))
                    S.op("vector", lambda v, o=b["mdn"], i=b["xt"], m3=b["mp"]:
                         v.scalar_tensor_tensor(
                             o, i, -0.5, m3, ALU.is_lt, ALU.subtract),
                         reads=(f"xtb{p}", f"mp{p}"), writes=(f"md{p}",))
                    S.op("vector", lambda v, o=b["at"], a=b["ic"], b2=b["mdn"]:
                         v.tensor_sub(o, a, b2),
                         reads=(f"ic{p}", f"md{p}"), writes=(f"at{p}",))
                    S.dma("sync", a_scr.ap()[b["tok0"]:b["tok0"] + 128, :],
                          b["at"], (f"at{p}",), ("as",))
            S.fence()

        with (
            nc.sbuf_tensor("abF", [128, 49152], BF16) as abF,
            nc.sbuf_tensor("afF", [128, 20736], F32) as afF,
        ):
            WG0, WU0, ATL = 0, 16384, 32768
            GUH, RMX0, RMX1 = 0, 8192, 10240
            ABG, ABU = 12288, 14336
            RA, RG, RU = 16384, 16512, 18560

            for c in range(8):
                S.dma("sync", abF[:, ATL + c * 2048:ATL + (c + 1) * 2048],
                      a_scr.ap()[0:T, c * 128:(c + 1) * 128],
                      (), ("atb",), transpose=True)
            S.op("vector", lambda v: v.tensor_scalar_mul(sc[:, 0:16], maxr_nat,
                                                         mg127_v),
                 reads=("cst",), writes=("ag",))
            tpose(pb6[0:16, 0:128], sc[:, 0:16], ("ag",), ("pb6",))
            S.op("vector", lambda v: v.tensor_copy(afF[0:16, RA:RA + 128],
                                                   pb6[0:16, 0:128]),
                 reads=("pb6",), writes=("ra",))
            S.dma("gpsimd",
                  row_scr.ap()[0:1, 0:T].rearrange("a (b c) -> (a b) c", c=128),
                  afF[0:16, RA:RA + 128], ("ra",), ("rowh",))
            S.dma("sync", afF[0:1, RG:RG + T], row_scr.ap()[0:1, 0:T],
                  ("rowh",), ("rg",))
            S.op("vector", lambda v: v.tensor_scalar_mul(
                afF[0:1, RU:RU + T], afF[0:1, RG:RG + T], sc[0:1, 17:18]),
                 reads=("rg",), writes=("ru",))
            for src_off, dst, rk in ((RG, ABG, "rg"), (RU, ABU, "ru")):
                for q in range(4):
                    bk = pb[2 + q % 2]
                    mm(bk[:, 0:512], oner,
                       afF[0:1, src_off + q * 512:src_off + (q + 1) * 512],
                       (rk, "cst"), (f"pb{2 + q % 2}",))
                    cop(EV[q % 2], afF[:, dst + q * 512:dst + (q + 1) * 512],
                        bk[:, 0:512], (f"pb{2 + q % 2}",), ("ab",))
            S.op("vector", lambda v: v.memset(afF[:, RMX0:RMX0 + 2048], 0.0),
                 reads=(), writes=("rm0",))
            S.op("gpsimd", lambda g: g.memset(afF[:, RMX1:RMX1 + 2048], 0.0),
                 reads=(), writes=("rm1",))

            for half in range(2):
                for wi, (scr3, dst) in enumerate(((tg_scr, WG0), (tu_scr, WU0))):
                    for c in range(8):
                        S.dma("scalar", abF[:, dst + c * 2048:dst + (c + 1) * 2048],
                              scr3.ap()[half * 2048:(half + 1) * 2048,
                                        c * 128:(c + 1) * 128],
                              ("wgw" if wi == 0 else "wuw",),
                              ("wgw" if wi == 0 else "wuw",), transpose=True)
                for kc in range(16):
                    for tb2 in range(4):
                        q = (kc * 4 + tb2) % 4
                        bg, bu = pb[q], pb[4 + q]
                        for c in range(8):
                            mm(bg[:, 0:512],
                               abF[:, WG0 + c * 2048 + kc * 128:
                                   WG0 + c * 2048 + (kc + 1) * 128],
                               abF[:, ATL + c * 2048 + tb2 * 512:
                                   ATL + c * 2048 + (tb2 + 1) * 512],
                               ("wgw", "atb"), (f"pb{q}",),
                               start=(c == 0), stop=(c == 7))
                            mm(bu[:, 0:512],
                               abF[:, WU0 + c * 2048 + kc * 128:
                                   WU0 + c * 2048 + (kc + 1) * 128],
                               abF[:, ATL + c * 2048 + tb2 * 512:
                                   ATL + c * 2048 + (tb2 + 1) * 512],
                               ("wuw", "atb"), (f"pb{4 + q}",),
                               start=(c == 0), stop=(c == 7))
                        gb = afF[:, GUH + q * 2048:GUH + q * 2048 + 512]
                        gs = afF[:, GUH + q * 2048 + 512:GUH + q * 2048 + 1024]
                        ub = afF[:, GUH + q * 2048 + 1024:GUH + q * 2048 + 1536]
                        hb = afF[:, GUH + q * 2048 + 1536:GUH + q * 2048 + 2048]
                        S.op("vector", lambda v, o=gb, b=bg, t4=tb2: v.tensor_tensor(
                            o, b[:, 0:512],
                            afF[:, ABG + t4 * 512:ABG + (t4 + 1) * 512], ALU.mult),
                             reads=(f"pb{q}", "ab"), writes=(f"gb{q}",))
                        S.op("scalar", lambda s, o=gs, i=gb:
                             s.activation(o, i, AF.Silu),
                             reads=(f"gb{q}",), writes=(f"gs{q}",))
                        S.op("vector", lambda v, o=ub, b=bu, t4=tb2: v.tensor_tensor(
                            o, b[:, 0:512],
                            afF[:, ABU + t4 * 512:ABU + (t4 + 1) * 512], ALU.mult),
                             reads=(f"pb{4 + q}", "ab"), writes=(f"ub{q}",))
                        S.op("gpsimd", lambda g, o=hb, a=gs, b=ub: g.tensor_tensor(
                            o, a, b, ALU.mult),
                             reads=(f"gs{q}", f"ub{q}"), writes=(f"hb{q}",))
                        rmx0 = afF[:, RMX0 + tb2 * 512:RMX0 + (tb2 + 1) * 512]
                        rmx1 = afF[:, RMX1 + tb2 * 512:RMX1 + (tb2 + 1) * 512]
                        S.op("vector", lambda v, o=rmx0, b=hb: v.tensor_tensor(
                            o, o, b, ALU.max),
                             reads=(f"hb{q}", "rm0"), writes=("rm0",))
                        S.op("vector", lambda v, o=rmx1, b=hb: v.tensor_tensor(
                            o, o, b, ALU.min),
                             reads=(f"hb{q}", "rm1"), writes=("rm1",))
                        kg = half * 16 + kc
                        S.dma("sync",
                              h_scr.ap()[kg * 128:(kg + 1) * 128,
                                         tb2 * 512:(tb2 + 1) * 512],
                              hb, (f"hb{q}",), ("hs",))
            S.op("vector", lambda v: v.tensor_single_scalar(
                afF[:, RMX1:RMX1 + 2048], afF[:, RMX1:RMX1 + 2048], -1.0, ALU.mult),
                 reads=("rm1",), writes=("rm1",))
            S.op("vector", lambda v: v.tensor_tensor(
                afF[:, RMX0:RMX0 + 2048], afF[:, RMX0:RMX0 + 2048],
                afF[:, RMX1:RMX1 + 2048], ALU.max),
                 reads=("rm0", "rm1"), writes=("rm0",))
            for sl in range(16):
                bk = pb[sl % 2]
                tpose(bk[:, 0:128], afF[:, RMX0 + sl * 128:RMX0 + (sl + 1) * 128],
                      ("rm0",), (f"pb{sl % 2}",))
                S.op("vector", lambda v, b=bk, s3=sl: v.reduce_max(
                    maxh_nat[:, s3:s3 + 1], b[:, 0:128], axis=AX),
                     reads=(f"pb{sl % 2}",), writes=("maxh",))
            S.op("vector", lambda v: v.tensor_single_scalar(
                maxh_nat, maxh_nat, 1e-5, ALU.max),
                 reads=("maxh",), writes=("maxh",))
            tpose(pb6[0:16, 0:128], maxh_nat, ("maxh",), ("pb6",))
            S.op("vector", lambda v: v.tensor_copy(afF[0:16, RA:RA + 128],
                                                   pb6[0:16, 0:128]),
                 reads=("pb6",), writes=("ra",))
            S.dma("gpsimd",
                  row_scr.ap()[2:3, 0:T].rearrange("a (b c) -> (a b) c", c=128),
                  afF[0:16, RA:RA + 128], ("ra",), ("rowh2",))
            S.fence()

        with (
            nc.sbuf_tensor("abG", [128, 38912], BF16) as abG,
            nc.sbuf_tensor("afG", [128, 24704], F32) as afG,
        ):
            WD0, HQ, MD2 = 0, 32768, 34816
            HT, G3, IC2, DD2 = 0, 2048, 4096, 6144
            SYB, BDB, FFT = 8192, 10240, 12288
            X1T, OUT = 16384, 18432
            ROWW, RMH, RROW = 20480, 20608, 22656

            for kc in range(32):
                S.dma(("scalar", "sync")[kc % 2],
                      abG[:, WD0 + kc * 1024:WD0 + (kc + 1) * 1024],
                      td_scr.ap()[0:D, kc * 128:(kc + 1) * 128],
                      (), ("wdt",), transpose=True)
            S.dma("sync", afG[0:1, RMH:RMH + T], row_scr.ap()[2:3, 0:T],
                  (), ("rmh",))
            S.op("vector", lambda v: v.reciprocal(afG[0:1, RROW:RROW + T],
                                                  afG[0:1, RMH:RMH + T]),
                 reads=("rmh",), writes=("rrow",))
            S.op("vector", lambda v: v.tensor_single_scalar(
                afG[0:1, RROW:RROW + T], afG[0:1, RROW:RROW + T], 127.0, ALU.mult),
                 reads=("rrow",), writes=("rrow",))
            for q in range(4):
                bk = pb[2 + q % 2]
                mm(bk[:, 0:512], oner, afG[0:1, RROW + q * 512:RROW + (q + 1) * 512],
                   ("rrow", "cst"), (f"pb{2 + q % 2}",))
                cop(EV[q % 2], afG[:, SYB + q * 512:SYB + (q + 1) * 512],
                    bk[:, 0:512], (f"pb{2 + q % 2}",), ("syb",))
            S.op("vector", lambda v: v.tensor_scalar_mul(
                afG[0:1, RROW:RROW + T], afG[0:1, RMH:RMH + T], sc[0:1, 16:17]),
                 reads=("rmh",), writes=("rrow",))
            for q in range(4):
                bk = pb[2 + q % 2]
                mm(bk[:, 0:512], oner, afG[0:1, RROW + q * 512:RROW + (q + 1) * 512],
                   ("rrow", "cst"), (f"pb{2 + q % 2}",))
                cop(EV[q % 2], afG[:, BDB + q * 512:BDB + (q + 1) * 512],
                    bk[:, 0:512], (f"pb{2 + q % 2}",), ("bdb",))

            for tb2 in range(4):
                for kc in range(32):
                    p = kc % 4
                    ht = afG[:, HT + p * 512:HT + (p + 1) * 512]
                    S.dma("sync", ht,
                          h_scr.ap()[kc * 128:(kc + 1) * 128,
                                     tb2 * 512:(tb2 + 1) * 512],
                          (f"ht{p}",), (f"ht{p}",))
                    g3 = afG[:, G3 + p * 512:G3 + (p + 1) * 512]
                    ic = afG[:, IC2 + p * 512:IC2 + (p + 1) * 512]
                    dd = afG[:, DD2 + p * 512:DD2 + (p + 1) * 512]
                    i0f = i0b[:, 0:512].bitcast(F32)
                    mp = abG[:, MD2 + p * 512:MD2 + (p + 1) * 512]
                    mdn = abG[:, MD2 + 1024 + p * 512:MD2 + 1024 + (p + 1) * 512]
                    hq = abG[:, HQ + p * 512:HQ + (p + 1) * 512]
                    S.op("gpsimd", lambda g, o=g3, a=ht, t4=tb2: g.tensor_tensor(
                        o, a, afG[:, SYB + t4 * 512:SYB + (t4 + 1) * 512], ALU.mult),
                         reads=(f"ht{p}", "syb"), writes=(f"g3{p}",))
                    S.op("scalar", lambda s, o=i0f, i=g3: s.add(
                        o, i, cst[:, 267:268]),
                         reads=(f"g3{p}", "cst", "i0"), writes=("i0",))
                    S.op("scalar", lambda s, o=ic, i=i0f: s.add(
                        o, i, cst[:, 268:269]),
                         reads=("i0", "cst"), writes=(f"ic{p}",))
                    S.op("gpsimd", lambda g, o=dd, a=ic, b=g3: g.tensor_sub(o, b, a),
                         reads=(f"ic{p}", f"g3{p}"), writes=(f"dd{p}",))
                    S.op("vector", lambda v, o=mp, i=dd: v.tensor_single_scalar(
                        o, i, 0.5, ALU.is_gt),
                         reads=(f"dd{p}",), writes=(f"mp{p}",))
                    S.op("vector", lambda v, o=mdn, i=dd, m3=mp:
                         v.scalar_tensor_tensor(
                             o, i, -0.5, m3, ALU.is_lt, ALU.subtract),
                         reads=(f"dd{p}", f"mp{p}",), writes=(f"md{p}",))
                    S.op("vector", lambda v, o=hq, a=ic, b=mdn: v.tensor_sub(
                        o, a, b),
                         reads=(f"ic{p}", f"md{p}"), writes=(f"hq{p}",))
                    for m2 in range(8):
                        mm(pb[m2][:, 0:512],
                           abG[:, WD0 + kc * 1024 + m2 * 128:
                               WD0 + kc * 1024 + (m2 + 1) * 128],
                           hq, ("wdt", f"hq{p}"), (f"pb{m2}",),
                           start=(kc == 0), stop=(kc == 31))
                for m2 in range(8):
                    S.op("vector", lambda v, b=pb[m2], m3=m2, t4=tb2:
                         v.tensor_tensor(
                             afG[:, FFT + m3 * 512:FFT + (m3 + 1) * 512],
                             b[:, 0:512],
                             afG[:, BDB + t4 * 512:BDB + (t4 + 1) * 512], ALU.mult),
                         reads=(f"pb{m2}", "bdb"), writes=("fft",))
                for sl in range(4):
                    tok0 = tb2 * 512 + sl * 128
                    sp = sl % 2
                    x1t = afG[:, X1T + sp * 1024:X1T + (sp + 1) * 1024]
                    S.dma("sync", x1t, x1_scr.ap()[tok0:tok0 + 128, :],
                          (f"x1t{sp}", "x1s"), (f"x1t{sp}",))
                    for m2 in range(8):
                        tpose(pb[m2 // 4][:, (m2 % 4) * 128:(m2 % 4) * 128 + 128],
                              afG[:, FFT + m2 * 512 + sl * 128:
                                  FFT + m2 * 512 + sl * 128 + 128],
                              ("fft",), (f"pb{m2 // 4}",))
                    ot = afG[:, OUT + sp * 1024:OUT + (sp + 1) * 1024]
                    for b2 in range(2):
                        S.op("vector", lambda v, b=pb[b2], b3=b2, o=ot, x2=x1t:
                             v.tensor_tensor(o[:, b3 * 512:(b3 + 1) * 512],
                                             b[:, 0:512],
                                             x2[:, b3 * 512:(b3 + 1) * 512], ALU.add),
                             reads=(f"pb{b2}", f"x1t{sp}"), writes=(f"ot{sp}",))
                    S.dma("scalar", y_h.ap()[tok0:tok0 + 128, :], ot,
                          (f"ot{sp}",), ("ys",))
            S.fence()
        S.finalize()

    return nc


_NC_CACHE = None


def mk_mask(first_pair):
    m = np.zeros((64, 2560), np.float32)
    j = np.arange(64)
    for h in range(H):
        for t2 in range(2):
            ci = 2 * h + t2
            for q in range(64):
                m[q, ci * 80:ci * 80 + 64][j > q] = NEG
            if first_pair and t2 == 0:
                m[:, ci * 80 + 64:ci * 80 + 80] = NEG
    return m.astype(ml_dtypes.bfloat16)


def kernel(L0_attn_hidden_states, L0_attn_Wq, L0_attn_Wk, L0_attn_Wv,
           L0_attn_Wo, L0_ffn_w_gate, L0_ffn_w_up, L0_ffn_w_down):
    global _NC_CACHE
    x = np.ascontiguousarray(L0_attn_hidden_states, dtype=np.float32)
    B, Sq, Dm = x.shape
    xf = x.reshape(B * Sq, Dm)

    eye = np.eye(128, dtype=np.float32)
    onec = np.ones((128, 1), np.float32)
    oner = np.ones((1, 128), np.float32)

    mask_n = np.vstack([mk_mask(False)] * 2)

    if _NC_CACHE is None:
        _NC_CACHE = build_nc()
    nc = _NC_CACHE

    in_maps = []
    for c in range(8):
        b = c // 2
        sh = c % 2
        in_maps.append({
            "x": np.ascontiguousarray(xf[c * T:(c + 1) * T]),
            "xg": np.ascontiguousarray(x[b, 0:16, :]),
            "wq": np.ascontiguousarray(L0_attn_Wq, np.float32),
            "wk": np.ascontiguousarray(L0_attn_Wk, np.float32),
            "wv": np.ascontiguousarray(L0_attn_Wv, np.float32),
            "wo": np.ascontiguousarray(L0_attn_Wo, np.float32),
            "wg": np.ascontiguousarray(L0_ffn_w_gate, np.float32),
            "wu": np.ascontiguousarray(L0_ffn_w_up, np.float32),
            "wd": np.ascontiguousarray(L0_ffn_w_down, np.float32),
            "eye": eye, "onec": onec, "oner": oner,
            "maskn": mask_n,
            "gmask": np.full((128, 16), NEG if sh == 0 else 0.0,
                             ml_dtypes.bfloat16),
        })
    res = run_bass_kernel_spmd(nc, in_maps, core_ids=list(range(8)))
    out = np.concatenate([res.results[c]["y"] for c in range(8)], axis=0)
    return out.reshape(B, Sq, Dm)


# version 23
# speedup vs baseline: 1.0630x; 1.0630x over previous
import numpy as np
import ml_dtypes
import concourse.bass as bass
import concourse.mybir as mybir
from concourse.bass_utils import run_bass_kernel_spmd

F32 = mybir.dt.float32
BF16 = mybir.dt.bfloat16
I32 = mybir.dt.int32
AX = mybir.AxisListType.X
AF = mybir.ActivationFunctionType
ALU = mybir.AluOpType

T = 2048
D = 1024
H = 16
INNER = 4096
TB = 256
NTB = T // TB
CMAG = float(np.float32(3 * 2 ** 22))
NEG = -1.0e30
PAR = ("vector", "gpsimd")



class Sched:

    ENGINES = ("sync", "scalar", "vector", "tensor", "gpsimd")
    CENG = ("scalar", "vector", "tensor", "gpsimd")
    DMAQ = ("sync", "scalar", "gpsimd")
    NLANE = 12

    def __init__(self, nc):
        self.nc = nc
        self.ops = []
        self.last_w = {}
        self.last_r = {}
        self.emitted = 0
        self.ccnt = {e: 0 for e in self.CENG}
        self.dtot = {e: 0 for e in self.DMAQ}
        self.dlcnt = {e: [0] * self.NLANE for e in self.DMAQ}
        self.waited = {e: {} for e in self.ENGINES}
        self.drained = {e: -1 for e in self.ENGINES}
        self.engpos = {e: 0 for e in self.ENGINES}
        self.snap = (dict(self.ccnt),
                     {e: list(v) for e, v in self.dlcnt.items()})
        self.csem = {}
        self.dsem = {}

    def make_sems(self, ent):
        for e in self.CENG:
            self.csem[e] = ent(self.nc.semaphore(f"c_{e}"))
        for e in self.DMAQ:
            self.dsem[e] = [ent(self.nc.semaphore(f"d_{e}{i}"))
                            for i in range(self.NLANE)]

    def op(self, engine, fn, reads=(), writes=(), dma=False):
        assert engine in self.ENGINES
        assert dma or engine != "sync"
        i = len(self.ops)
        deps = set()
        for b in reads:
            if b in self.last_w:
                deps.add(self.last_w[b])
        for b in writes:
            if b in self.last_w:
                deps.add(self.last_w[b])
            deps.update(self.last_r.get(b, ()))
        deps.discard(i)
        o = dict(idx=i, engine=engine, fn=fn, deps=sorted(deps),
                 dma=dma, inc=bool(dma), semval=None, pos=None, lane=None)
        if dma:
            o["lane"] = self.dtot[engine] % self.NLANE
            self.dtot[engine] += 1
        self.ops.append(o)
        for b in reads:
            self.last_r.setdefault(b, set()).add(i)
        for b in writes:
            self.last_w[b] = i
            self.last_r[b] = set()
        return i

    def dma(self, queue, out_ap, in_ap, reads=(), writes=(), transpose=False):
        self.op(queue,
                lambda e, o=out_ap, i=in_ap, tr=transpose:
                e.dma_start(o, i, transpose=True) if tr else e.dma_start(o, i),
                reads=reads, writes=writes, dma=True)

    def _fence_waits(self, e, eng, snapc, snapd):
        w = self.waited[e]
        for e2 in self.CENG:
            if e2 != e and snapc[e2] > 0 and w.get(("c", e2), 0) < snapc[e2]:
                eng.wait_ge(self.csem[e2], snapc[e2])
                w[("c", e2)] = snapc[e2]
        for e2 in self.DMAQ:
            for l in range(self.NLANE):
                v = 16 * snapd[e2][l]
                if v > 0 and w.get(("d", e2, l), 0) < v:
                    eng.wait_ge(self.dsem[e2][l], v)
                    w[("d", e2, l)] = v

    def fence(self):
        base = self.emitted
        seg = self.ops[base:]
        if not seg:
            return
        for o in seg:
            for d in o["deps"]:
                if d < base:
                    continue
                od = self.ops[d]
                if od["engine"] != o["engine"] and not od["dma"]:
                    od["inc"] = True
        eng_ops = {e: [] for e in self.ENGINES}
        for o in seg:
            o["pos"] = self.engpos[o["engine"]]
            self.engpos[o["engine"]] += 1
            eng_ops[o["engine"]].append(o)
        for e, lst in eng_ops.items():
            if lst and not lst[-1]["dma"]:
                lst[-1]["inc"] = True
        for o in seg:
            if o["dma"]:
                q, l = o["engine"], o["lane"]
                self.dlcnt[q][l] += 1
                o["semval"] = 16 * self.dlcnt[q][l]
            elif o["inc"]:
                self.ccnt[o["engine"]] += 1
                o["semval"] = self.ccnt[o["engine"]]

        snapc, snapd = self.snap

        def run(e, eng):
            w = self.waited[e]
            self._fence_waits(e, eng, snapc, snapd)
            for o in eng_ops[e]:
                need_drain = False
                for d in o["deps"]:
                    if d < base:
                        continue
                    od = self.ops[d]
                    if od["engine"] == e and not od["dma"]:
                        if od["pos"] > self.drained[e]:
                            need_drain = True
                        continue
                    assert od["inc"]
                    if od["dma"]:
                        k = ("d", od["engine"], od["lane"])
                        sem = self.dsem[od["engine"]][od["lane"]]
                    else:
                        k = ("c", od["engine"])
                        sem = self.csem[od["engine"]]
                    if w.get(k, 0) < od["semval"]:
                        eng.wait_ge(sem, od["semval"])
                        w[k] = od["semval"]
                if need_drain:
                    eng.drain()
                    self.drained[e] = o["pos"] - 1
                if o["dma"]:
                    k = ("d", e, o["lane"])
                    prev = o["semval"] - 16
                    if prev > 0 and w.get(k, 0) < prev:
                        eng.wait_ge(self.dsem[e][o["lane"]], prev)
                        w[k] = prev
                ins = o["fn"](eng)
                if o["dma"]:
                    ins.then_inc(self.dsem[e][o["lane"]], 16)
                elif o["inc"]:
                    ins.then_inc(self.csem[o["engine"]], 1)

        with self.nc.Block() as blk:
            for e in self.ENGINES:
                if eng_ops[e]:
                    getattr(blk, e)(lambda eng, e=e: run(e, eng))

        self.emitted = len(self.ops)
        self.snap = (dict(self.ccnt),
                     {e: list(v) for e, v in self.dlcnt.items()})
        self.last_w = {}
        self.last_r = {}
        for e in self.ENGINES:
            self.drained[e] = self.engpos[e] - 1

    def finalize(self):
        snapc = dict(self.ccnt)
        snapd = {e: list(v) for e, v in self.dlcnt.items()}
        with self.nc.Block() as blk:
            for e in self.ENGINES:
                getattr(blk, e)(
                    lambda eng, e=e: self._fence_waits(e, eng, snapc, snapd))




def build_nc():
    nc = bass.Bass("TRN2", target_bir_lowering=False)

    x_h = nc.dram_tensor("x", [T, D], F32, kind="ExternalInput")
    xg_h = nc.dram_tensor("xg", [16, D], F32, kind="ExternalInput")
    wq_h = nc.dram_tensor("wq", [D, D], F32, kind="ExternalInput")
    wk_h = nc.dram_tensor("wk", [D, D], F32, kind="ExternalInput")
    wv_h = nc.dram_tensor("wv", [D, D], F32, kind="ExternalInput")
    wo_h = nc.dram_tensor("wo", [D, D], F32, kind="ExternalInput")
    wg_h = nc.dram_tensor("wg", [INNER, D], F32, kind="ExternalInput")
    wu_h = nc.dram_tensor("wu", [INNER, D], F32, kind="ExternalInput")
    wd_h = nc.dram_tensor("wd", [D, INNER], F32, kind="ExternalInput")
    eye_h = nc.dram_tensor("eye", [128, 128], F32, kind="ExternalInput")
    onec_h = nc.dram_tensor("onec", [128, 1], F32, kind="ExternalInput")
    oner_h = nc.dram_tensor("oner", [1, 128], F32, kind="ExternalInput")
    maskn_h = nc.dram_tensor("maskn", [128, 2560], BF16, kind="ExternalInput")
    gmask_h = nc.dram_tensor("gmask", [128, 16], BF16, kind="ExternalInput")
    y_h = nc.dram_tensor("y", [T, D], F32, kind="ExternalOutput")
    ctx_scr = nc.dram_tensor("ctx_scr", [16 * D, 128], F32)
    a_scr = nc.dram_tensor("a_scr", [T, D], BF16)
    x1_scr = nc.dram_tensor("x1_scr", [T, D], F32)
    tg_scr = nc.dram_tensor("tg_scr", [INNER, D], BF16)
    tu_scr = nc.dram_tensor("tu_scr", [INNER, D], BF16)
    td_scr = nc.dram_tensor("td_scr", [D, INNER], BF16)
    h_scr = nc.dram_tensor("h_scr", [INNER, T], F32)
    row_scr = nc.dram_tensor("row_scr", [4, T], F32)

    from contextlib import ExitStack
    with ExitStack() as stack:
        ent = stack.enter_context
        cst = ent(nc.sbuf_tensor("cst", [128, 384], F32))
        sc = ent(nc.sbuf_tensor("sc", [128, 128], F32))
        i0b = ent(nc.sbuf_tensor("i0b", [128, 1024], I32))
        pb = [ent(nc.psum_tensor(f"pb{i}", [128, 512], F32)) for i in range(8)]
        pb6, pb7 = pb[6], pb[7]
        S = Sched(nc)
        S.make_sems(ent)

        eye = cst[:, 0:128]
        oner = cst[0:1, 128:256]
        onec = cst[:, 256:257]
        tau_v = [cst[:, 260 + j:261 + j] for j in range(3)]
        mg127_v = cst[:, 263:264]

        st6a = sc[:, 48:60]
        st2a = sc[:, 60:62]
        lnwa = sc[:, 62:64]
        maxh_nat = sc[:, 32:48]
        maxr_nat = sc[:, 64:80]

        def mm(out_ap, lhs_ap, rhs_ap, reads, writes, start=True, stop=True):
            S.op("tensor", lambda t, o=out_ap, l=lhs_ap, r=rhs_ap,
                 st=start, sp=stop: t.matmul(o, l, r, start=st, stop=sp),
                 reads=reads, writes=writes)

        def tpose(out_ap, in_ap, reads, writes, pb0_=0, pn=128):
            ey = eye[pb0_:pb0_ + pn, pb0_:pb0_ + pn]
            S.op("tensor", lambda t, o=out_ap, i=in_ap, e2=ey: t.transpose(o, i, e2),
                 reads=tuple(reads) + ("cst",), writes=writes)

        def cop(eng, dst, src, reads, writes):
            if eng == "scalar":
                S.op("scalar", lambda s, d=dst, s2=src: s.copy(d, s2),
                     reads=reads, writes=writes)
            else:
                S.op("vector", lambda v, d=dst, s2=src: v.tensor_copy(d, s2),
                     reads=reads, writes=writes)

        EV = ("vector", "scalar")

        def ln_tile(xap, xkeys, P=128):
            S.op("vector", lambda v, x=xap, p2=P: v.bn_stats(st6a[0:p2, 0:6],
                                                             x[:, 0:512]),
                 reads=xkeys, writes=("st6",))
            S.op("vector", lambda v, x=xap, p2=P: v.bn_stats(st6a[0:p2, 6:12],
                                                             x[:, 512:1024]),
                 reads=xkeys, writes=("st6",))
            S.op("vector", lambda v, p2=P: v.bn_aggr(st2a[0:p2, :], st6a[0:p2, :]),
                 reads=("st6",), writes=("st2",))
            S.op("vector", lambda v, p2=P: v.tensor_single_scalar(
                lnwa[0:p2, 0:1], st2a[0:p2, 1:2], 1e-5, ALU.add),
                 reads=("st2",), writes=("lnw",))
            S.op("scalar", lambda s, p2=P: s.activation(lnwa[0:p2, 1:2],
                                                        lnwa[0:p2, 0:1], AF.Ln),
                 reads=("lnw",), writes=("lnw",))
            S.op("vector", lambda v, p2=P: v.tensor_single_scalar(
                lnwa[0:p2, 0:1], lnwa[0:p2, 1:2], -0.5, ALU.mult),
                 reads=("lnw",), writes=("lnw",))
            S.op("scalar", lambda s, p2=P: s.activation(lnwa[0:p2, 0:1],
                                                        lnwa[0:p2, 0:1], AF.Exp),
                 reads=("lnw",), writes=("lnw",))
            return st2a[:, 0:1], lnwa[:, 0:1]

        def wt_prep(af, wh, wbase, wst, pitch, scale, keypfx):
            for r in range(8):
                p = r % 2
                buf = af[:, wst + p * 1024: wst + (p + 1) * 1024]
                S.dma("sync", buf, wh.ap()[r * 128:(r + 1) * 128, :],
                      (f"{keypfx}{p}",), (f"{keypfx}{p}",))
                for c in range(8):
                    bk = pb[4 + 2 * p + c // 4]
                    tpose(bk[:, (c % 4) * 128:(c % 4) * 128 + 128],
                          buf[:, c * 128:(c + 1) * 128],
                          (f"{keypfx}{p}",), (f"pb{4 + 2 * p + c // 4}",))
                for half in range(2):
                    bki = 4 + 2 * p + half
                    dst = bass.AP(af, wbase + (half * 4) * 1024 + r * 128,
                                  [[pitch, 128], [1024, 4], [1, 128]])
                    src = pb[bki][:, 0:512].rearrange("p (g k) -> p g k", g=4)
                    eng = EV[(r * 2 + half) % 2]
                    if scale is None:
                        cop(eng, dst, src, (f"pb{bki}",), ("wf",))
                    elif eng == "scalar":
                        S.op("scalar", lambda s, d=dst, s2=src, sc2=scale:
                             s.mul(d, s2, sc2),
                             reads=(f"pb{bki}",), writes=("wf",))
                    else:
                        S.op("vector", lambda v, d=dst, s2=src, sc2=scale:
                             v.tensor_single_scalar(d, s2, sc2, ALU.mult),
                             reads=(f"pb{bki}",), writes=("wf",))

        wtiles = []
        for wi, (wh, scr3) in enumerate(
                ((wg_h, tg_scr), (wu_h, tu_scr), (wd_h, td_scr))):
            if wi < 2:
                for r in range(32):
                    wtiles.append((wi, wh.ap()[r * 128:(r + 1) * 128, :],
                                   scr3.ap()[r * 128:(r + 1) * 128, :]))
            else:
                for r in range(8):
                    for q in range(4):
                        wtiles.append((wi,
                                       wh.ap()[r * 128:(r + 1) * 128,
                                               q * 1024:(q + 1) * 1024],
                                       scr3.ap()[r * 128:(r + 1) * 128,
                                                 q * 1024:(q + 1) * 1024]))
        absac = sc[:, 32:35]

        PITA = 44160
        with (
            nc.sbuf_tensor("afA", [128, PITA], F32) as afA,
            nc.sbuf_tensor("mk", [128, 2560], BF16) as mkT,
            nc.sbuf_tensor("gm", [128, 16], BF16) as gmT,
            nc.sbuf_tensor("afW1", [128, 2048], F32) as afW1,
        ):
            WQ, WK, WV = 0, 8192, 16384
            XNT = 24576
            QT0 = 26624
            KT0 = 28672
            VT0 = 31232
            SCB, PTB = 35328, 37888
            XT, XN = 39936, 41984
            KGT = 44032
            WST = SCB
            XG = PTB
            XGT = PTB + 1024

            S.dma("sync", cst[:, 0:128], eye_h.ap(), (), ("cst",))
            S.dma("sync", cst[0:1, 128:256], oner_h.ap(), (), ("cst",))
            S.dma("sync", cst[:, 256:257], onec_h.ap(), (), ("cst",))
            S.dma("sync", mkT[:, 0:2560], maskn_h.ap(), (), ("mk",))
            S.dma("sync", gmT[:, 0:16], gmask_h.ap(), (), ("mk",))
            S.op("vector", lambda v: v.memset(cst[:, 267:268], CMAG),
                 reads=(), writes=("cst",))
            S.op("vector", lambda v: v.memset(cst[:, 268:269], -CMAG),
                 reads=(), writes=("cst",))

            wt_prep(afA, wq_h, WQ, WST, PITA, 0.125, "wsa")
            wt_prep(afA, wk_h, WK, WST, PITA, None, "wsa")
            wt_prep(afA, wv_h, WV, WST, PITA, None, "wsa")

            S.dma("sync", afA[0:16, XG:XG + 1024], xg_h.ap(), (), ("xg",))
            mean, rstd = ln_tile(afA[0:16, XG:XG + 1024], ("xg",), P=16)
            xgn = afA[0:16, XN:XN + 1024]
            S.op("vector", lambda v, m=mean, r=rstd: v.tensor_scalar(
                xgn, afA[0:16, XG:XG + 1024], m[0:16, :], r[0:16, :],
                ALU.subtract, ALU.mult),
                 reads=("xg", "st2", "lnw"), writes=("xn",))
            for c in range(8):
                tpose(pb6[0:128, c * 16:(c + 1) * 16],
                      afA[0:16, XN + c * 128:XN + (c + 1) * 128], ("xn",), ("pb6",),
                      pn=16)
            S.op("vector", lambda v: v.tensor_copy(afA[:, XGT:XGT + 128],
                                                   pb6[:, 0:128]),
                 reads=("pb6",), writes=("xgt",))

            def xgt_c(c):
                return afA[:, XGT + c * 16:XGT + (c + 1) * 16]

            for n in range(2):
                for c in range(8):
                    mm(pb[n][0:16, 0:512], xgt_c(c),
                       afA[:, WK + c * 1024 + n * 512:WK + c * 1024 + (n + 1) * 512],
                       ("xgt", "wf"), (f"pb{n}",), start=(c == 0), stop=(c == 7))
            for n in range(2):
                S.op("vector", lambda v, n2=n: v.tensor_copy(
                    afA[0:16, XG + n2 * 512:XG + (n2 + 1) * 512],
                    pb[n2][0:16, 0:512]),
                     reads=(f"pb{n}",), writes=("xg",))
            for m in range(8):
                tpose(pb6[0:128, m * 16:(m + 1) * 16],
                      afA[0:16, XG + m * 128:XG + (m + 1) * 128], ("xg",), ("pb6",),
                      pn=16)
            S.op("vector", lambda v: v.tensor_copy(afA[:, KGT:KGT + 128],
                                                   pb6[:, 0:128]),
                 reads=("pb6",), writes=("kgt",))
            for n in range(2):
                for c in range(8):
                    mm(pb[2 + n][0:16, 0:512], xgt_c(c),
                       afA[:, WV + c * 1024 + n * 512:WV + c * 1024 + (n + 1) * 512],
                       ("xgt", "wf"), (f"pb{2 + n}",), start=(c == 0), stop=(c == 7))
            for n in range(2):
                S.op("vector", lambda v, n2=n: v.tensor_copy(
                    afA[0:16, XG + n2 * 512:XG + (n2 + 1) * 512],
                    pb[2 + n2][0:16, 0:512]),
                     reads=(f"pb{2 + n}",), writes=("xg",))
            for tg in range(4):
                S.dma("gpsimd", afA[64:80, VT0 + tg * 1024:VT0 + (tg + 1) * 1024],
                      afA[0:16, XG:XG + 1024], ("xg",), ("vg",))
            for m in range(8):
                for tg in range(4):
                    cop(EV[(m + tg) % 2],
                        afA[:, KT0 + m * 320 + tg * 80 + 64:
                            KT0 + m * 320 + tg * 80 + 80],
                        afA[:, KGT + m * 16:KGT + (m + 1) * 16],
                        ("kgt",), ("ktg",))
            for wi in range(3):
                S.op("vector", lambda v, w2=wi: v.memset(absac[:, w2:w2 + 1], 0.0),
                     reads=(), writes=(f"aa{wi}",))
            S.fence()

            def w_absum_tiles(lo, hi):
                for t3 in range(lo, hi):
                    wi, src_ap, _ = wtiles[t3]
                    p = t3 % 2
                    buf = afW1[:, p * 1024:(p + 1) * 1024]
                    S.dma("sync", buf, src_ap, (f"wpa{p}",), (f"wpa{p}",))
                    red = sc[:, 35 + p:36 + p]
                    S.op("vector", lambda v, b=buf, rd=red: v.reduce_sum(
                        rd, b, axis=AX, apply_absolute_value=True),
                         reads=(f"wpa{p}",), writes=(f"red{p}",))
                    S.op("vector", lambda v, rd=red, w2=wi: v.tensor_tensor(
                        absac[:, w2:w2 + 1], absac[:, w2:w2 + 1], rd, ALU.add),
                         reads=(f"red{p}", f"aa{wi}"), writes=(f"aa{wi}",))

            def ln_prefetch(tb2):
                for ti in range(2):
                    tok0 = tb2 * TB + ti * 128
                    xt = afA[:, XT + ti * 1024:XT + (ti + 1) * 1024]
                    xn = afA[:, XN + ti * 1024:XN + (ti + 1) * 1024]
                    S.dma("sync", xt, x_h.ap()[tok0:tok0 + 128, :],
                          (f"xta{ti}",), (f"xta{ti}",))
                    mean, rstd = ln_tile(xt, (f"xta{ti}",))
                    S.op("vector", lambda v, x=xt, o=xn, m=mean, r=rstd:
                         v.tensor_scalar(o, x, m, r, ALU.subtract, ALU.mult),
                         reads=(f"xta{ti}", "st2", "lnw"), writes=(f"xn{ti}",))

            for tb in range(NTB):
                if tb == 0:
                    ln_prefetch(0)
                for ti in range(2):
                    xn = afA[:, XN + ti * 1024:XN + (ti + 1) * 1024]
                    for c in range(8):
                        tpose(pb[6 + c // 4][:, (c % 4) * 128:(c % 4) * 128 + 128],
                              xn[:, c * 128:(c + 1) * 128],
                              (f"xn{ti}",), (f"pb{6 + c // 4}",))
                    for half in range(2):
                        dst = bass.AP(afA, XNT + (half * 4) * 256 + ti * 128,
                                      [[PITA, 128], [256, 4], [1, 128]])
                        src = pb[6 + half][:, 0:512].rearrange(
                            "p (g k) -> p g k", g=4)
                        cop("scalar", dst, src, (f"pb{6 + half}",), ("xnt",))

                if tb + 1 < NTB:
                    ln_prefetch(tb + 1)
                for wbase, dkey in ((WQ, "qt"), (WK, "kt")):
                    for m in range(8):
                        bk = pb[m % 4]
                        bcol = (m // 4) * 256
                        for c in range(8):
                            mm(bk[:, bcol:bcol + 256],
                               afA[:, wbase + c * 1024 + m * 128:
                                   wbase + c * 1024 + (m + 1) * 128],
                               afA[:, XNT + c * 256:XNT + (c + 1) * 256],
                               ("wf", "xnt"), (f"pb{m % 4}",),
                               start=(c == 0), stop=(c == 7))
                    for b2 in range(4):
                        if wbase == WQ:
                            dst = bass.AP(afA, QT0 + b2 * 256,
                                          [[PITA, 128], [1024, 2], [1, 256]])
                            src = pb[b2][:, 0:512].rearrange(
                                "p (g k) -> p g k", g=2)
                        else:
                            dst = bass.AP(afA, KT0 + b2 * 320,
                                          [[PITA, 128], [1280, 2], [80, 4], [1, 64]])
                            src = pb[b2][:, 0:512].rearrange(
                                "p (g t k) -> p g t k", g=2, t=4)
                        cop("scalar", dst, src, (f"pb{b2}",), (dkey,))
                for tg in range(4):
                    for n in range(2):
                        bki = 4 + (tg * 2 + n) % 4
                        for c in range(8):
                            mm(pb[bki][0:64, 0:512],
                               afA[:, XNT + c * 256 + tg * 64:
                                   XNT + c * 256 + tg * 64 + 64],
                               afA[:, WV + c * 1024 + n * 512:
                                   WV + c * 1024 + (n + 1) * 512],
                               ("wf", "xnt"), (f"pb{bki}",),
                               start=(c == 0), stop=(c == 7))
                        cop(EV[n],
                            afA[0:64, VT0 + tg * 1024 + n * 512:
                                VT0 + tg * 1024 + (n + 1) * 512],
                            pb[bki][0:64, 0:512],
                            (f"pb{bki}",), ("vt",))

                def scbuf(pr):
                    return afA[pr * 64:(pr + 1) * 64, SCB:SCB + 2560]

                for pr in range(2):
                    scb = scbuf(pr)
                    pk = f"scb{pr}"
                    for h2 in range(H):
                        qrow = (h2 % 2) * 64
                        for t2 in range(2):
                            ci = 2 * h2 + t2
                            bk = pb[ci // 6]
                            cc = (ci % 6) * 80
                            tg = pr * 2 + t2
                            lhs = afA[qrow:qrow + 64,
                                      QT0 + (h2 // 2) * 256 + tg * 64:
                                      QT0 + (h2 // 2) * 256 + tg * 64 + 64]
                            rhs = afA[qrow:qrow + 64,
                                      KT0 + (h2 // 2) * 320 + tg * 80:
                                      KT0 + (h2 // 2) * 320 + tg * 80 + 80]
                            mm(bk[0:64, cc:cc + 80], lhs, rhs,
                               ("qt", "kt", "ktg"), (f"pb{ci // 6}",))
                    for b2 in range(6):
                        w3 = 480 if b2 < 5 else 160
                        S.op("vector", lambda v, b=pb[b2], off=b2 * 480, w4=w3,
                             s4=scb, p2=pr:
                             v.tensor_tensor(s4[:, off:off + w4], b[0:64, 0:w4],
                                             mkT[p2 * 64:p2 * 64 + 64,
                                                 off:off + w4],
                                             ALU.add),
                             reads=(f"pb{b2}", "mk"), writes=(pk,))
                    if tb == 0 and pr == 0:
                        glob3 = bass.AP(afA, SCB + 64, [[PITA, 64], [160, 16],
                                                        [1, 16]])
                        gm3 = bass.AP(gmT, 0, [[16, 64], [0, 16], [1, 16]])
                        S.op("vector", lambda v, g3=glob3, g4=gm3: v.tensor_tensor(
                            g3, g3, g4, ALU.add),
                             reads=(pk, "mk"), writes=(pk,))
                    scb3 = scb.rearrange("p (h k) -> p h k", k=80)
                    mx3 = sc[pr * 64:pr * 64 + 64, 0:32].rearrange(
                        "p (h k) -> p h k", k=1)
                    S.op("vector", lambda v, o=mx3, i=scb3:
                         v.reduce_max(o, i, axis=AX),
                         reads=(pk,), writes=(f"mx{pr}",))
                    mxb = bass.AP(sc, pr * 64 * 128, [[128, 64], [1, 32], [0, 80]])
                    S.op("vector", lambda v, m3=mxb, s3=scb3: v.tensor_tensor(
                        s3, s3, m3, ALU.subtract),
                         reads=(pk, f"mx{pr}"), writes=(pk,))
                    S.op("scalar", lambda s, s3=scb: s.activation(s3, s3, AF.Exp),
                         reads=(pk,), writes=(pk,))
                    rs2 = sc[pr * 64:pr * 64 + 64, 96:128]
                    rs3 = rs2.rearrange("p (h k) -> p h k", k=1)
                    S.op("vector", lambda v, o=rs3, i=scb3:
                         v.reduce_sum(o, i, axis=AX),
                         reads=(pk,), writes=(f"rs{pr}",))
                    S.op("vector", lambda v, r=rs2: v.reciprocal(r, r),
                         reads=(f"rs{pr}",), writes=(f"rs{pr}",))
                    rsb = bass.AP(sc, pr * 64 * 128 + 96,
                                  [[128, 64], [1, 32], [0, 80]])
                    S.op("vector", lambda v, r=rsb, s3=scb3: v.tensor_tensor(
                        s3, s3, r, ALU.mult),
                         reads=(pk, f"rs{pr}"), writes=(pk,))

                PTBANKS = (6, 7, 0, 1)
                for pr in range(2):
                    scb = scbuf(pr)
                    pk = f"scb{pr}"
                    for ci in range(32):
                        bk = pb[PTBANKS[ci // 8]]
                        tpose(bk[0:80, (ci % 8) * 64:(ci % 8) * 64 + 64],
                              scb[:, ci * 80:ci * 80 + 80],
                              (pk,), (f"pb{PTBANKS[ci // 8]}",),
                              pb0_=pr * 64, pn=64)
                    for g2 in range(4):
                        S.op("scalar", lambda s, b=pb[PTBANKS[g2]], off=g2 * 512:
                             s.copy(afA[0:80, PTB + off:PTB + off + 512],
                                    b[0:80, 0:512]),
                             reads=(f"pb{PTBANKS[g2]}",), writes=("ptb",))
                    for h2 in range(H):
                        for t2 in range(2):
                            ci = 2 * h2 + t2
                            bk = pb[2 + ci // 8]
                            tg = pr * 2 + t2
                            lhsl = afA[0:80,
                                       VT0 + tg * 1024 + h2 * 64:
                                       VT0 + tg * 1024 + h2 * 64 + 64]
                            rhsl = afA[0:80, PTB + ci * 64:PTB + ci * 64 + 64]
                            mm(bk[0:64, (ci % 8) * 64:(ci % 8) * 64 + 64],
                               lhsl, rhsl, ("vt", "vg", "ptb"),
                               (f"pb{2 + ci // 8}",))
                    for g2 in range(4):
                        cop(EV[g2 % 2],
                            scb[:, g2 * 512:g2 * 512 + 512],
                            pb[2 + g2][0:64, 0:512],
                            (f"pb{2 + g2}",), (pk,))
                    pair = tb * 2 + pr
                    src = bass.AP(afA, pr * 64 * PITA + SCB,
                                  [[PITA, 64], [128, 16], [1, 128]])
                    dst = bass.AP(ctx_scr, pair * 131072,
                                  [[128, 64], [8192, 16], [1, 128]])
                    S.dma("gpsimd", dst, src, (pk,), ("ctxw",))
                w_absum_tiles(tb * 12, (tb + 1) * 12)
            S.fence()

        PITB = 34880
        with (
            nc.sbuf_tensor("afB", [128, PITB], F32) as afB,
            nc.sbuf_tensor("abB", [128, 12288], BF16) as abB,
            nc.sbuf_tensor("afW2", [128, 2048], F32) as afW2,
            nc.sbuf_tensor("abW", [128, 6144], BF16) as abW,
        ):
            WO, WST2 = 0, 8192
            CTXF, XT2, X1B, XN2, DQ = 10240, 14336, 18432, 22528, 26624
            GB2 = 30720
            LNS = 34816
            MD, ATI = 0, 8192

            mm(pb6[0:1, 0:3], onec, absac[:, 0:3], ("cst",), ("pb6",))
            S.op("vector", lambda v: v.tensor_copy(sc[0:1, 8:11], pb6[0:1, 0:3]),
                 reads=("pb6",), writes=("wm",))
            S.op("vector", lambda v: v.tensor_scalar(
                sc[0:1, 8:11], sc[0:1, 8:11], 1.0 / (INNER * D), 1e-5,
                ALU.mult, ALU.max),
                 reads=("wm",), writes=("wm",))
            S.op("vector", lambda v: v.tensor_single_scalar(
                sc[0:1, 11:14], sc[0:1, 8:11], 0.5, ALU.mult),
                 reads=("wm",), writes=("tau",))
            S.op("vector", lambda v: v.tensor_single_scalar(
                sc[0:1, 14:15], sc[0:1, 8:9], 1.0 / 254.0, ALU.mult),
                 reads=("wm",), writes=("tau",))
            S.op("vector", lambda v: v.reciprocal(sc[0:1, 17:18], sc[0:1, 8:9]),
                 reads=("wm",), writes=("mumg",))
            S.op("vector", lambda v: v.tensor_tensor(
                sc[0:1, 17:18], sc[0:1, 17:18], sc[0:1, 9:10], ALU.mult),
                 reads=("mumg", "wm"), writes=("mumg",))
            S.op("vector", lambda v: v.tensor_single_scalar(
                sc[0:1, 16:17], sc[0:1, 10:11], 1.0 / 254.0, ALU.mult),
                 reads=("wm",), writes=("md127",))
            mm(pb7[0:128, 0:4], oner, sc[0:1, 11:15], ("tau", "cst"), ("pb7",))
            S.op("vector", lambda v: v.tensor_copy(cst[:, 260:264], pb7[:, 0:4]),
                 reads=("pb7",), writes=("cst",))
            S.op("vector", lambda v: v.tensor_single_scalar(
                cst[:, 264:267], cst[:, 260:263], -1.0, ALU.mult),
                 reads=("cst",), writes=("cst",))

            def w_tern_tiles(lo, hi):
                for t3 in range(lo, hi):
                    wi, src_ap, dst_ap = wtiles[t3]
                    p = t3 % 2
                    buf = afW2[:, p * 1024:(p + 1) * 1024]
                    S.dma(("sync", "scalar")[p], buf, src_ap,
                          (f"wpa{p}",), (f"wpa{p}",))
                    s1 = abW[:, p * 1024:(p + 1) * 1024]
                    s2b = abW[:, 2048 + p * 1024:2048 + (p + 1) * 1024]
                    tn = abW[:, 4096 + p * 1024:4096 + (p + 1) * 1024]
                    S.op("scalar", lambda s, o=s1, b=buf, w2=wi: s.sign(
                        o, b, bias=cst[:, 264 + w2:265 + w2]),
                         reads=(f"wpa{p}", "cst"), writes=(f"tp{p}",))
                    S.op("scalar", lambda s, o=s2b, b=buf, w2=wi: s.sign(
                        o, b, bias=cst[:, 260 + w2:261 + w2]),
                         reads=(f"wpa{p}", "cst"), writes=(f"tq{p}",))
                    S.op("gpsimd", lambda g, o=tn, a=s1, b=s2b: g.tensor_tensor(
                        o, a, b, ALU.add),
                         reads=(f"tp{p}", f"tq{p}"), writes=(f"tn{p}",))
                    S.dma("sync", dst_ap, tn, (f"tn{p}",), ("ts",))

            wt_prep(afB, wo_h, WO, WST2, PITB, None, "wsb")

            def btile(ti):
                p = ti % 4
                return dict(
                    p=p, tok0=ti * 128,
                    cf=afB[:, CTXF + p * 1024:CTXF + (p + 1) * 1024],
                    xt=afB[:, XT2 + p * 1024:XT2 + (p + 1) * 1024],
                    x1=afB[:, X1B + p * 1024:X1B + (p + 1) * 1024],
                    xn2=afB[:, XN2 + p * 1024:XN2 + (p + 1) * 1024],
                    ic=afB[:, DQ + p * 1024:DQ + (p + 1) * 1024],
                    gq=afB[:, GB2 + p * 1024:GB2 + (p + 1) * 1024],
                    mp=abB[:, MD + p * 1024:MD + (p + 1) * 1024],
                    mdn=abB[:, MD + 4096 + p * 1024:MD + 4096 + (p + 1) * 1024],
                    at=abB[:, ATI + p * 1024:ATI + (p + 1) * 1024],
                    st6=afB[:, LNS + p * 16:LNS + p * 16 + 12],
                    st2=afB[:, LNS + p * 16 + 12:LNS + p * 16 + 14],
                    lnw=afB[:, LNS + p * 16 + 14:LNS + p * 16 + 16],
                    mr=maxr_nat[:, ti:ti + 1], sx=sc[:, 80 + ti:81 + ti])

            for g in range(4):
                tis = range(g * 4, g * 4 + 4)
                bt = {ti: btile(ti) for ti in tis}
                w_tern_tiles(g * 24, (g + 1) * 24)
                for ti in tis:
                    b = bt[ti]
                    p = b["p"]
                    csrc = bass.AP(ctx_scr, ti * 131072,
                                   [[128, 128], [16384, 8], [1, 128]])
                    cdst = bass.AP(afB, CTXF + p * 1024,
                                   [[PITB, 128], [128, 8], [1, 128]])
                    S.dma("sync", cdst, csrc, (f"cf{p}",), (f"cf{p}",))
                    S.dma("sync", b["xt"], x_h.ap()[b["tok0"]:b["tok0"] + 128, :],
                          (f"xtb{p}",), (f"xtb{p}",))
                for ti in tis:
                    b = bt[ti]
                    p = b["p"]
                    wb0 = 4 + 2 * (ti % 2)
                    for n in range(2):
                        bk = pb[wb0 + n]
                        for c in range(8):
                            mm(bk[:, 0:512], b["cf"][:, c * 128:(c + 1) * 128],
                               afB[:, WO + c * 1024 + n * 512:
                                   WO + c * 1024 + (n + 1) * 512],
                               (f"cf{p}", "wf"), (f"pb{wb0 + n}",),
                               start=(c == 0), stop=(c == 7))
                    for n in range(2):
                        S.op("vector", lambda v, b2=pb[wb0 + n], n2=n,
                             x2=b["x1"], xx=b["xt"]:
                             v.tensor_tensor(x2[:, n2 * 512:(n2 + 1) * 512],
                                             b2[:, 0:512],
                                             xx[:, n2 * 512:(n2 + 1) * 512],
                                             ALU.add),
                             reads=(f"pb{wb0 + n}", f"xtb{p}"), writes=(f"x1{p}",))
                    S.dma("scalar", x1_scr.ap()[b["tok0"]:b["tok0"] + 128, :],
                          b["x1"], (f"x1{p}",), ("x1s",))
                for ti in tis:
                    b = bt[ti]
                    p = b["p"]
                    S.op("vector", lambda v, x=b["x1"], o=b["st6"]: v.bn_stats(
                        o[:, 0:6], x[:, 0:512]),
                         reads=(f"x1{p}",), writes=(f"st6{p}",))
                    S.op("vector", lambda v, x=b["x1"], o=b["st6"]: v.bn_stats(
                        o[:, 6:12], x[:, 512:1024]),
                         reads=(f"x1{p}",), writes=(f"st6{p}",))
                    S.op("vector", lambda v, o=b["st2"], i=b["st6"]: v.bn_aggr(o, i),
                         reads=(f"st6{p}",), writes=(f"st2{p}",))
                    S.op("vector", lambda v, o=b["lnw"], i=b["st2"]:
                         v.tensor_single_scalar(o[:, 0:1], i[:, 1:2], 1e-5, ALU.add),
                         reads=(f"st2{p}",), writes=(f"lnw{p}",))
                    S.op("vector", lambda v, o=b["lnw"]:
                         v.reciprocal(o[:, 1:2], o[:, 0:1]),
                         reads=(f"lnw{p}",), writes=(f"lnw{p}",))
                for ti in tis:
                    b = bt[ti]
                    p = b["p"]
                    S.op("scalar", lambda s, o=b["lnw"]: s.sqrt(o[:, 0:1], o[:, 1:2]),
                         reads=(f"lnw{p}",), writes=(f"lnw{p}",))
                for ti in tis:
                    b = bt[ti]
                    p = b["p"]
                    S.op("vector", lambda v, x2=b["x1"], o=b["xn2"], m=b["st2"],
                         r=b["lnw"]:
                         v.tensor_scalar(o, x2, m[:, 0:1], r[:, 0:1],
                                         ALU.subtract, ALU.mult),
                         reads=(f"x1{p}", f"st2{p}", f"lnw{p}"), writes=(f"xn2{p}",))
                    S.op("vector", lambda v, m3=b["mr"], x2=b["xn2"]: v.reduce_max(
                        m3, x2, axis=AX, apply_absolute_value=True),
                         reads=(f"xn2{p}",), writes=("maxr",))
                    S.op("vector", lambda v, m3=b["mr"]: v.tensor_single_scalar(
                        m3, m3, 1e-5, ALU.max),
                         reads=("maxr",), writes=("maxr",))
                    S.op("vector", lambda v, m3=b["mr"], s3=b["sx"]:
                         v.reciprocal(s3, m3),
                         reads=("maxr",), writes=("sx",))
                    S.op("vector", lambda v, s3=b["sx"]: v.tensor_single_scalar(
                        s3, s3, 127.0, ALU.mult),
                         reads=("sx",), writes=("sx",))
                i0f = i0b[:, 0:1024].bitcast(F32)
                for ti in tis:
                    b = bt[ti]
                    p = b["p"]
                    S.op("scalar", lambda s, o=b["gq"], x2=b["xn2"], s3=b["sx"]:
                         s.mul(o, x2, s3),
                         reads=(f"xn2{p}", "sx"), writes=(f"gq{p}",))
                for ti in tis:
                    b = bt[ti]
                    p = b["p"]
                    S.op("scalar", lambda s, o=i0f, i=b["gq"]: s.add(
                        o, i, cst[:, 267:268]),
                         reads=(f"gq{p}", "cst", "i0"), writes=("i0",))
                    S.op("scalar", lambda s, o=b["ic"], i=i0f: s.add(
                        o, i, cst[:, 268:269]),
                         reads=("i0", "cst"), writes=(f"ic{p}",))
                for ti in tis:
                    b = bt[ti]
                    p = b["p"]
                    S.op(PAR[p % 2], lambda v, o=b["xt"], i=b["ic"], g2=b["gq"]:
                         v.tensor_sub(o, g2, i),
                         reads=(f"ic{p}", f"gq{p}"), writes=(f"xtb{p}",))
                for ti in tis:
                    b = bt[ti]
                    p = b["p"]
                    S.op("vector", lambda v, o=b["mp"], i=b["xt"]:
                         v.tensor_single_scalar(o, i, 0.5, ALU.is_gt),
                         reads=(f"xtb{p}",), writes=(f"mp{p}",))
                    S.op("vector", lambda v, o=b["mdn"], i=b["xt"], m3=b["mp"]:
                         v.scalar_tensor_tensor(
                             o, i, -0.5, m3, ALU.is_lt, ALU.subtract),
                         reads=(f"xtb{p}", f"mp{p}"), writes=(f"md{p}",))
                    S.op("vector", lambda v, o=b["at"], a=b["ic"], b2=b["mdn"]:
                         v.tensor_sub(o, a, b2),
                         reads=(f"ic{p}", f"md{p}"), writes=(f"at{p}",))
                    S.dma("gpsimd", a_scr.ap()[b["tok0"]:b["tok0"] + 128, :],
                          b["at"], (f"at{p}",), ("as",))
            S.fence()

        with (
            nc.sbuf_tensor("abF", [128, 49152], BF16) as abF,
            nc.sbuf_tensor("afF", [128, 20736], F32) as afF,
        ):
            WG0, WU0, ATL = 0, 16384, 32768
            GUH, RMX0, RMX1 = 0, 8192, 10240
            ABG, ABU = 12288, 14336
            RA, RG, RU = 16384, 16512, 18560

            for c in range(8):
                S.dma("sync", abF[:, ATL + c * 2048:ATL + (c + 1) * 2048],
                      a_scr.ap()[0:T, c * 128:(c + 1) * 128],
                      (), ("atb",), transpose=True)
            S.op("vector", lambda v: v.tensor_scalar_mul(sc[:, 0:16], maxr_nat,
                                                         mg127_v),
                 reads=("cst",), writes=("ag",))
            tpose(pb6[0:16, 0:128], sc[:, 0:16], ("ag",), ("pb6",))
            S.op("vector", lambda v: v.tensor_copy(afF[0:16, RA:RA + 128],
                                                   pb6[0:16, 0:128]),
                 reads=("pb6",), writes=("ra",))
            S.dma("gpsimd",
                  row_scr.ap()[0:1, 0:T].rearrange("a (b c) -> (a b) c", c=128),
                  afF[0:16, RA:RA + 128], ("ra",), ("rowh",))
            S.dma("sync", afF[0:1, RG:RG + T], row_scr.ap()[0:1, 0:T],
                  ("rowh",), ("rg",))
            S.op("vector", lambda v: v.tensor_scalar_mul(
                afF[0:1, RU:RU + T], afF[0:1, RG:RG + T], sc[0:1, 17:18]),
                 reads=("rg",), writes=("ru",))
            for src_off, dst, rk in ((RG, ABG, "rg"), (RU, ABU, "ru")):
                for q in range(4):
                    bk = pb[2 + q % 2]
                    mm(bk[:, 0:512], oner,
                       afF[0:1, src_off + q * 512:src_off + (q + 1) * 512],
                       (rk, "cst"), (f"pb{2 + q % 2}",))
                    cop(EV[q % 2], afF[:, dst + q * 512:dst + (q + 1) * 512],
                        bk[:, 0:512], (f"pb{2 + q % 2}",), ("ab",))
            S.op("vector", lambda v: v.memset(afF[:, RMX0:RMX0 + 2048], 0.0),
                 reads=(), writes=("rm0",))
            S.op("gpsimd", lambda g: g.memset(afF[:, RMX1:RMX1 + 2048], 0.0),
                 reads=(), writes=("rm1",))

            for half in range(2):
                for wi, (scr3, dst) in enumerate(((tg_scr, WG0), (tu_scr, WU0))):
                    for c in range(8):
                        S.dma("scalar", abF[:, dst + c * 2048:dst + (c + 1) * 2048],
                              scr3.ap()[half * 2048:(half + 1) * 2048,
                                        c * 128:(c + 1) * 128],
                              ("wgw" if wi == 0 else "wuw",),
                              ("wgw" if wi == 0 else "wuw",), transpose=True)
                for kc in range(16):
                    for tb2 in range(4):
                        q = (kc * 4 + tb2) % 4
                        bg, bu = pb[q], pb[4 + q]
                        for c in range(8):
                            mm(bg[:, 0:512],
                               abF[:, WG0 + c * 2048 + kc * 128:
                                   WG0 + c * 2048 + (kc + 1) * 128],
                               abF[:, ATL + c * 2048 + tb2 * 512:
                                   ATL + c * 2048 + (tb2 + 1) * 512],
                               ("wgw", "atb"), (f"pb{q}",),
                               start=(c == 0), stop=(c == 7))
                            mm(bu[:, 0:512],
                               abF[:, WU0 + c * 2048 + kc * 128:
                                   WU0 + c * 2048 + (kc + 1) * 128],
                               abF[:, ATL + c * 2048 + tb2 * 512:
                                   ATL + c * 2048 + (tb2 + 1) * 512],
                               ("wuw", "atb"), (f"pb{4 + q}",),
                               start=(c == 0), stop=(c == 7))
                        gb = afF[:, GUH + q * 2048:GUH + q * 2048 + 512]
                        gs = afF[:, GUH + q * 2048 + 512:GUH + q * 2048 + 1024]
                        ub = afF[:, GUH + q * 2048 + 1024:GUH + q * 2048 + 1536]
                        hb = afF[:, GUH + q * 2048 + 1536:GUH + q * 2048 + 2048]
                        S.op("vector", lambda v, o=gb, b=bg, t4=tb2: v.tensor_tensor(
                            o, b[:, 0:512],
                            afF[:, ABG + t4 * 512:ABG + (t4 + 1) * 512], ALU.mult),
                             reads=(f"pb{q}", "ab"), writes=(f"gb{q}",))
                        S.op("scalar", lambda s, o=gs, i=gb:
                             s.activation(o, i, AF.Silu),
                             reads=(f"gb{q}",), writes=(f"gs{q}",))
                        S.op("vector", lambda v, o=ub, b=bu, t4=tb2: v.tensor_tensor(
                            o, b[:, 0:512],
                            afF[:, ABU + t4 * 512:ABU + (t4 + 1) * 512], ALU.mult),
                             reads=(f"pb{4 + q}", "ab"), writes=(f"ub{q}",))
                        S.op("gpsimd", lambda g, o=hb, a=gs, b=ub: g.tensor_tensor(
                            o, a, b, ALU.mult),
                             reads=(f"gs{q}", f"ub{q}"), writes=(f"hb{q}",))
                        rmx0 = afF[:, RMX0 + tb2 * 512:RMX0 + (tb2 + 1) * 512]
                        rmx1 = afF[:, RMX1 + tb2 * 512:RMX1 + (tb2 + 1) * 512]
                        S.op("vector", lambda v, o=rmx0, b=hb: v.tensor_tensor(
                            o, o, b, ALU.max),
                             reads=(f"hb{q}", "rm0"), writes=("rm0",))
                        S.op("vector", lambda v, o=rmx1, b=hb: v.tensor_tensor(
                            o, o, b, ALU.min),
                             reads=(f"hb{q}", "rm1"), writes=("rm1",))
                        kg = half * 16 + kc
                        S.dma("sync",
                              h_scr.ap()[kg * 128:(kg + 1) * 128,
                                         tb2 * 512:(tb2 + 1) * 512],
                              hb, (f"hb{q}",), ("hs",))
            S.op("vector", lambda v: v.tensor_single_scalar(
                afF[:, RMX1:RMX1 + 2048], afF[:, RMX1:RMX1 + 2048], -1.0, ALU.mult),
                 reads=("rm1",), writes=("rm1",))
            S.op("vector", lambda v: v.tensor_tensor(
                afF[:, RMX0:RMX0 + 2048], afF[:, RMX0:RMX0 + 2048],
                afF[:, RMX1:RMX1 + 2048], ALU.max),
                 reads=("rm0", "rm1"), writes=("rm0",))
            for sl in range(16):
                bk = pb[sl % 2]
                tpose(bk[:, 0:128], afF[:, RMX0 + sl * 128:RMX0 + (sl + 1) * 128],
                      ("rm0",), (f"pb{sl % 2}",))
                S.op("vector", lambda v, b=bk, s3=sl: v.reduce_max(
                    maxh_nat[:, s3:s3 + 1], b[:, 0:128], axis=AX),
                     reads=(f"pb{sl % 2}",), writes=("maxh",))
            S.op("vector", lambda v: v.tensor_single_scalar(
                maxh_nat, maxh_nat, 1e-5, ALU.max),
                 reads=("maxh",), writes=("maxh",))
            tpose(pb6[0:16, 0:128], maxh_nat, ("maxh",), ("pb6",))
            S.op("vector", lambda v: v.tensor_copy(afF[0:16, RA:RA + 128],
                                                   pb6[0:16, 0:128]),
                 reads=("pb6",), writes=("ra",))
            S.dma("gpsimd",
                  row_scr.ap()[2:3, 0:T].rearrange("a (b c) -> (a b) c", c=128),
                  afF[0:16, RA:RA + 128], ("ra",), ("rowh2",))
            S.fence()

        with (
            nc.sbuf_tensor("abG", [128, 38912], BF16) as abG,
            nc.sbuf_tensor("afG", [128, 24704], F32) as afG,
        ):
            WD0, HQ, MD2 = 0, 32768, 34816
            HT, G3, IC2, DD2 = 0, 2048, 4096, 6144
            SYB, BDB, FFT = 8192, 10240, 12288
            X1T, OUT = 16384, 18432
            ROWW, RMH, RROW = 20480, 20608, 22656

            for kc in range(32):
                S.dma(("scalar", "sync")[kc % 2],
                      abG[:, WD0 + kc * 1024:WD0 + (kc + 1) * 1024],
                      td_scr.ap()[0:D, kc * 128:(kc + 1) * 128],
                      (), ("wdt",), transpose=True)
            S.dma("sync", afG[0:1, RMH:RMH + T], row_scr.ap()[2:3, 0:T],
                  (), ("rmh",))
            S.op("vector", lambda v: v.reciprocal(afG[0:1, RROW:RROW + T],
                                                  afG[0:1, RMH:RMH + T]),
                 reads=("rmh",), writes=("rrow",))
            S.op("vector", lambda v: v.tensor_single_scalar(
                afG[0:1, RROW:RROW + T], afG[0:1, RROW:RROW + T], 127.0, ALU.mult),
                 reads=("rrow",), writes=("rrow",))
            for q in range(4):
                bk = pb[2 + q % 2]
                mm(bk[:, 0:512], oner, afG[0:1, RROW + q * 512:RROW + (q + 1) * 512],
                   ("rrow", "cst"), (f"pb{2 + q % 2}",))
                cop(EV[q % 2], afG[:, SYB + q * 512:SYB + (q + 1) * 512],
                    bk[:, 0:512], (f"pb{2 + q % 2}",), ("syb",))
            S.op("vector", lambda v: v.tensor_scalar_mul(
                afG[0:1, RROW:RROW + T], afG[0:1, RMH:RMH + T], sc[0:1, 16:17]),
                 reads=("rmh",), writes=("rrow",))
            for q in range(4):
                bk = pb[2 + q % 2]
                mm(bk[:, 0:512], oner, afG[0:1, RROW + q * 512:RROW + (q + 1) * 512],
                   ("rrow", "cst"), (f"pb{2 + q % 2}",))
                cop(EV[q % 2], afG[:, BDB + q * 512:BDB + (q + 1) * 512],
                    bk[:, 0:512], (f"pb{2 + q % 2}",), ("bdb",))

            for tb2 in range(4):
                for kc in range(32):
                    p = kc % 4
                    ht = afG[:, HT + p * 512:HT + (p + 1) * 512]
                    S.dma("sync", ht,
                          h_scr.ap()[kc * 128:(kc + 1) * 128,
                                     tb2 * 512:(tb2 + 1) * 512],
                          (f"ht{p}",), (f"ht{p}",))
                    g3 = afG[:, G3 + p * 512:G3 + (p + 1) * 512]
                    ic = afG[:, IC2 + p * 512:IC2 + (p + 1) * 512]
                    dd = afG[:, DD2 + p * 512:DD2 + (p + 1) * 512]
                    i0f = i0b[:, 0:512].bitcast(F32)
                    mp = abG[:, MD2 + p * 512:MD2 + (p + 1) * 512]
                    mdn = abG[:, MD2 + 1024 + p * 512:MD2 + 1024 + (p + 1) * 512]
                    hq = abG[:, HQ + p * 512:HQ + (p + 1) * 512]
                    S.op("gpsimd", lambda g, o=g3, a=ht, t4=tb2: g.tensor_tensor(
                        o, a, afG[:, SYB + t4 * 512:SYB + (t4 + 1) * 512], ALU.mult),
                         reads=(f"ht{p}", "syb"), writes=(f"g3{p}",))
                    S.op("scalar", lambda s, o=i0f, i=g3: s.add(
                        o, i, cst[:, 267:268]),
                         reads=(f"g3{p}", "cst", "i0"), writes=("i0",))
                    S.op("scalar", lambda s, o=ic, i=i0f: s.add(
                        o, i, cst[:, 268:269]),
                         reads=("i0", "cst"), writes=(f"ic{p}",))
                    S.op("gpsimd", lambda g, o=dd, a=ic, b=g3: g.tensor_sub(o, b, a),
                         reads=(f"ic{p}", f"g3{p}"), writes=(f"dd{p}",))
                    S.op("vector", lambda v, o=mp, i=dd: v.tensor_single_scalar(
                        o, i, 0.5, ALU.is_gt),
                         reads=(f"dd{p}",), writes=(f"mp{p}",))
                    S.op("vector", lambda v, o=mdn, i=dd, m3=mp:
                         v.scalar_tensor_tensor(
                             o, i, -0.5, m3, ALU.is_lt, ALU.subtract),
                         reads=(f"dd{p}", f"mp{p}",), writes=(f"md{p}",))
                    S.op("vector", lambda v, o=hq, a=ic, b=mdn: v.tensor_sub(
                        o, a, b),
                         reads=(f"ic{p}", f"md{p}"), writes=(f"hq{p}",))
                    for m2 in range(8):
                        mm(pb[m2][:, 0:512],
                           abG[:, WD0 + kc * 1024 + m2 * 128:
                               WD0 + kc * 1024 + (m2 + 1) * 128],
                           hq, ("wdt", f"hq{p}"), (f"pb{m2}",),
                           start=(kc == 0), stop=(kc == 31))
                for m2 in range(8):
                    S.op("vector", lambda v, b=pb[m2], m3=m2, t4=tb2:
                         v.tensor_tensor(
                             afG[:, FFT + m3 * 512:FFT + (m3 + 1) * 512],
                             b[:, 0:512],
                             afG[:, BDB + t4 * 512:BDB + (t4 + 1) * 512], ALU.mult),
                         reads=(f"pb{m2}", "bdb"), writes=("fft",))
                for sl in range(4):
                    tok0 = tb2 * 512 + sl * 128
                    sp = sl % 2
                    x1t = afG[:, X1T + sp * 1024:X1T + (sp + 1) * 1024]
                    S.dma("sync", x1t, x1_scr.ap()[tok0:tok0 + 128, :],
                          (f"x1t{sp}", "x1s"), (f"x1t{sp}",))
                    for m2 in range(8):
                        tpose(pb[m2 // 4][:, (m2 % 4) * 128:(m2 % 4) * 128 + 128],
                              afG[:, FFT + m2 * 512 + sl * 128:
                                  FFT + m2 * 512 + sl * 128 + 128],
                              ("fft",), (f"pb{m2 // 4}",))
                    ot = afG[:, OUT + sp * 1024:OUT + (sp + 1) * 1024]
                    for b2 in range(2):
                        S.op("vector", lambda v, b=pb[b2], b3=b2, o=ot, x2=x1t:
                             v.tensor_tensor(o[:, b3 * 512:(b3 + 1) * 512],
                                             b[:, 0:512],
                                             x2[:, b3 * 512:(b3 + 1) * 512], ALU.add),
                             reads=(f"pb{b2}", f"x1t{sp}"), writes=(f"ot{sp}",))
                    S.dma("scalar", y_h.ap()[tok0:tok0 + 128, :], ot,
                          (f"ot{sp}",), ("ys",))
            S.fence()
        S.finalize()

    return nc


_NC_CACHE = None


def mk_mask(first_pair):
    m = np.zeros((64, 2560), np.float32)
    j = np.arange(64)
    for h in range(H):
        for t2 in range(2):
            ci = 2 * h + t2
            for q in range(64):
                m[q, ci * 80:ci * 80 + 64][j > q] = NEG
            if first_pair and t2 == 0:
                m[:, ci * 80 + 64:ci * 80 + 80] = NEG
    return m.astype(ml_dtypes.bfloat16)


def kernel(L0_attn_hidden_states, L0_attn_Wq, L0_attn_Wk, L0_attn_Wv,
           L0_attn_Wo, L0_ffn_w_gate, L0_ffn_w_up, L0_ffn_w_down):
    global _NC_CACHE
    x = np.ascontiguousarray(L0_attn_hidden_states, dtype=np.float32)
    B, Sq, Dm = x.shape
    xf = x.reshape(B * Sq, Dm)

    eye = np.eye(128, dtype=np.float32)
    onec = np.ones((128, 1), np.float32)
    oner = np.ones((1, 128), np.float32)

    mask_n = np.vstack([mk_mask(False)] * 2)

    if _NC_CACHE is None:
        _NC_CACHE = build_nc()
    nc = _NC_CACHE

    in_maps = []
    for c in range(8):
        b = c // 2
        sh = c % 2
        in_maps.append({
            "x": np.ascontiguousarray(xf[c * T:(c + 1) * T]),
            "xg": np.ascontiguousarray(x[b, 0:16, :]),
            "wq": np.ascontiguousarray(L0_attn_Wq, np.float32),
            "wk": np.ascontiguousarray(L0_attn_Wk, np.float32),
            "wv": np.ascontiguousarray(L0_attn_Wv, np.float32),
            "wo": np.ascontiguousarray(L0_attn_Wo, np.float32),
            "wg": np.ascontiguousarray(L0_ffn_w_gate, np.float32),
            "wu": np.ascontiguousarray(L0_ffn_w_up, np.float32),
            "wd": np.ascontiguousarray(L0_ffn_w_down, np.float32),
            "eye": eye, "onec": onec, "oner": oner,
            "maskn": mask_n,
            "gmask": np.full((128, 16), NEG if sh == 0 else 0.0,
                             ml_dtypes.bfloat16),
        })
    res = run_bass_kernel_spmd(nc, in_maps, core_ids=list(range(8)))
    out = np.concatenate([res.results[c]["y"] for c in range(8)], axis=0)
    return out.reshape(B, Sq, Dm)


# version 24
# speedup vs baseline: 1.0688x; 1.0054x over previous
import numpy as np
import ml_dtypes
import concourse.bass as bass
import concourse.mybir as mybir
from concourse.bass_utils import run_bass_kernel_spmd

F32 = mybir.dt.float32
BF16 = mybir.dt.bfloat16
I32 = mybir.dt.int32
AX = mybir.AxisListType.X
AF = mybir.ActivationFunctionType
ALU = mybir.AluOpType

T = 2048
D = 1024
H = 16
INNER = 4096
TB = 256
NTB = T // TB
CMAG = float(np.float32(3 * 2 ** 22))
NEG = -1.0e30
PAR = ("vector", "gpsimd")



class Sched:

    ENGINES = ("sync", "scalar", "vector", "tensor", "gpsimd")
    CENG = ("scalar", "vector", "tensor", "gpsimd")
    DMAQ = ("sync", "scalar", "gpsimd")
    NLANE = 12

    def __init__(self, nc):
        self.nc = nc
        self.ops = []
        self.last_w = {}
        self.last_r = {}
        self.emitted = 0
        self.ccnt = {e: 0 for e in self.CENG}
        self.dtot = {e: 0 for e in self.DMAQ}
        self.dlcnt = {e: [0] * self.NLANE for e in self.DMAQ}
        self.waited = {e: {} for e in self.ENGINES}
        self.drained = {e: -1 for e in self.ENGINES}
        self.engpos = {e: 0 for e in self.ENGINES}
        self.snap = (dict(self.ccnt),
                     {e: list(v) for e, v in self.dlcnt.items()})
        self.csem = {}
        self.dsem = {}

    def make_sems(self, ent):
        for e in self.CENG:
            self.csem[e] = ent(self.nc.semaphore(f"c_{e}"))
        for e in self.DMAQ:
            self.dsem[e] = [ent(self.nc.semaphore(f"d_{e}{i}"))
                            for i in range(self.NLANE)]

    def op(self, engine, fn, reads=(), writes=(), dma=False):
        assert engine in self.ENGINES
        assert dma or engine != "sync"
        i = len(self.ops)
        deps = set()
        for b in reads:
            if b in self.last_w:
                deps.add(self.last_w[b])
        for b in writes:
            if b in self.last_w:
                deps.add(self.last_w[b])
            deps.update(self.last_r.get(b, ()))
        deps.discard(i)
        o = dict(idx=i, engine=engine, fn=fn, deps=sorted(deps),
                 dma=dma, inc=bool(dma), semval=None, pos=None, lane=None)
        if dma:
            o["lane"] = self.dtot[engine] % self.NLANE
            self.dtot[engine] += 1
        self.ops.append(o)
        for b in reads:
            self.last_r.setdefault(b, set()).add(i)
        for b in writes:
            self.last_w[b] = i
            self.last_r[b] = set()
        return i

    def dma(self, queue, out_ap, in_ap, reads=(), writes=(), transpose=False):
        self.op(queue,
                lambda e, o=out_ap, i=in_ap, tr=transpose:
                e.dma_start(o, i, transpose=True) if tr else e.dma_start(o, i),
                reads=reads, writes=writes, dma=True)

    def _fence_waits(self, e, eng, snapc, snapd):
        w = self.waited[e]
        for e2 in self.CENG:
            if e2 != e and snapc[e2] > 0 and w.get(("c", e2), 0) < snapc[e2]:
                eng.wait_ge(self.csem[e2], snapc[e2])
                w[("c", e2)] = snapc[e2]
        for e2 in self.DMAQ:
            for l in range(self.NLANE):
                v = 16 * snapd[e2][l]
                if v > 0 and w.get(("d", e2, l), 0) < v:
                    eng.wait_ge(self.dsem[e2][l], v)
                    w[("d", e2, l)] = v

    def fence(self):
        base = self.emitted
        seg = self.ops[base:]
        if not seg:
            return
        for o in seg:
            for d in o["deps"]:
                if d < base:
                    continue
                od = self.ops[d]
                if od["engine"] != o["engine"] and not od["dma"]:
                    od["inc"] = True
        eng_ops = {e: [] for e in self.ENGINES}
        for o in seg:
            o["pos"] = self.engpos[o["engine"]]
            self.engpos[o["engine"]] += 1
            eng_ops[o["engine"]].append(o)
        for e, lst in eng_ops.items():
            if lst and not lst[-1]["dma"]:
                lst[-1]["inc"] = True
        for o in seg:
            if o["dma"]:
                q, l = o["engine"], o["lane"]
                self.dlcnt[q][l] += 1
                o["semval"] = 16 * self.dlcnt[q][l]
            elif o["inc"]:
                self.ccnt[o["engine"]] += 1
                o["semval"] = self.ccnt[o["engine"]]

        snapc, snapd = self.snap

        def run(e, eng):
            w = self.waited[e]
            self._fence_waits(e, eng, snapc, snapd)
            for o in eng_ops[e]:
                need_drain = False
                for d in o["deps"]:
                    if d < base:
                        continue
                    od = self.ops[d]
                    if od["engine"] == e and not od["dma"]:
                        if od["pos"] > self.drained[e]:
                            need_drain = True
                        continue
                    assert od["inc"]
                    if od["dma"]:
                        k = ("d", od["engine"], od["lane"])
                        sem = self.dsem[od["engine"]][od["lane"]]
                    else:
                        k = ("c", od["engine"])
                        sem = self.csem[od["engine"]]
                    if w.get(k, 0) < od["semval"]:
                        eng.wait_ge(sem, od["semval"])
                        w[k] = od["semval"]
                if need_drain:
                    eng.drain()
                    self.drained[e] = o["pos"] - 1
                if o["dma"]:
                    k = ("d", e, o["lane"])
                    prev = o["semval"] - 16
                    if prev > 0 and w.get(k, 0) < prev:
                        eng.wait_ge(self.dsem[e][o["lane"]], prev)
                        w[k] = prev
                ins = o["fn"](eng)
                if o["dma"]:
                    ins.then_inc(self.dsem[e][o["lane"]], 16)
                elif o["inc"]:
                    ins.then_inc(self.csem[o["engine"]], 1)

        with self.nc.Block() as blk:
            for e in self.ENGINES:
                if eng_ops[e]:
                    getattr(blk, e)(lambda eng, e=e: run(e, eng))

        self.emitted = len(self.ops)
        self.snap = (dict(self.ccnt),
                     {e: list(v) for e, v in self.dlcnt.items()})
        self.last_w = {}
        self.last_r = {}
        for e in self.ENGINES:
            self.drained[e] = self.engpos[e] - 1

    def finalize(self):
        snapc = dict(self.ccnt)
        snapd = {e: list(v) for e, v in self.dlcnt.items()}
        with self.nc.Block() as blk:
            for e in self.ENGINES:
                getattr(blk, e)(
                    lambda eng, e=e: self._fence_waits(e, eng, snapc, snapd))




def build_nc():
    nc = bass.Bass("TRN2", target_bir_lowering=False)

    x_h = nc.dram_tensor("x", [T, D], F32, kind="ExternalInput")
    xg_h = nc.dram_tensor("xg", [16, D], F32, kind="ExternalInput")
    wq_h = nc.dram_tensor("wq", [D, D], F32, kind="ExternalInput")
    wk_h = nc.dram_tensor("wk", [D, D], F32, kind="ExternalInput")
    wv_h = nc.dram_tensor("wv", [D, D], F32, kind="ExternalInput")
    wo_h = nc.dram_tensor("wo", [D, D], F32, kind="ExternalInput")
    wg_h = nc.dram_tensor("wg", [INNER, D], F32, kind="ExternalInput")
    wu_h = nc.dram_tensor("wu", [INNER, D], F32, kind="ExternalInput")
    wd_h = nc.dram_tensor("wd", [D, INNER], F32, kind="ExternalInput")
    eye_h = nc.dram_tensor("eye", [128, 128], F32, kind="ExternalInput")
    onec_h = nc.dram_tensor("onec", [128, 1], F32, kind="ExternalInput")
    oner_h = nc.dram_tensor("oner", [1, 128], F32, kind="ExternalInput")
    maskn_h = nc.dram_tensor("maskn", [128, 2560], BF16, kind="ExternalInput")
    gmask_h = nc.dram_tensor("gmask", [128, 16], BF16, kind="ExternalInput")
    y_h = nc.dram_tensor("y", [T, D], F32, kind="ExternalOutput")
    ctx_scr = nc.dram_tensor("ctx_scr", [16 * D, 128], F32)
    a_scr = nc.dram_tensor("a_scr", [T, D], BF16)
    x1_scr = nc.dram_tensor("x1_scr", [T, D], F32)
    tg_scr = nc.dram_tensor("tg_scr", [INNER, D], BF16)
    tu_scr = nc.dram_tensor("tu_scr", [INNER, D], BF16)
    td_scr = nc.dram_tensor("td_scr", [D, INNER], BF16)
    h_scr = nc.dram_tensor("h_scr", [INNER, T], F32)
    row_scr = nc.dram_tensor("row_scr", [4, T], F32)

    from contextlib import ExitStack
    with ExitStack() as stack:
        ent = stack.enter_context
        cst = ent(nc.sbuf_tensor("cst", [128, 384], F32))
        sc = ent(nc.sbuf_tensor("sc", [128, 128], F32))
        i0b = ent(nc.sbuf_tensor("i0b", [128, 1024], I32))
        pb = [ent(nc.psum_tensor(f"pb{i}", [128, 512], F32)) for i in range(8)]
        pb6, pb7 = pb[6], pb[7]
        S = Sched(nc)
        S.make_sems(ent)

        eye = cst[:, 0:128]
        oner = cst[0:1, 128:256]
        onec = cst[:, 256:257]
        tau_v = [cst[:, 260 + j:261 + j] for j in range(3)]
        mg127_v = cst[:, 263:264]

        st6a = sc[:, 48:60]
        st2a = sc[:, 60:62]
        lnwa = sc[:, 62:64]
        maxh_nat = sc[:, 32:48]
        maxr_nat = sc[:, 64:80]

        def mm(out_ap, lhs_ap, rhs_ap, reads, writes, start=True, stop=True):
            S.op("tensor", lambda t, o=out_ap, l=lhs_ap, r=rhs_ap,
                 st=start, sp=stop: t.matmul(o, l, r, start=st, stop=sp),
                 reads=reads, writes=writes)

        def tpose(out_ap, in_ap, reads, writes, pb0_=0, pn=128):
            ey = eye[pb0_:pb0_ + pn, pb0_:pb0_ + pn]
            S.op("tensor", lambda t, o=out_ap, i=in_ap, e2=ey: t.transpose(o, i, e2),
                 reads=tuple(reads) + ("cst",), writes=writes)

        def cop(eng, dst, src, reads, writes):
            if eng == "scalar":
                S.op("scalar", lambda s, d=dst, s2=src: s.copy(d, s2),
                     reads=reads, writes=writes)
            else:
                S.op("vector", lambda v, d=dst, s2=src: v.tensor_copy(d, s2),
                     reads=reads, writes=writes)

        EV = ("vector", "scalar")

        def ln_tile(xap, xkeys, P=128):
            S.op("vector", lambda v, x=xap, p2=P: v.bn_stats(st6a[0:p2, 0:6],
                                                             x[:, 0:512]),
                 reads=xkeys, writes=("st6",))
            S.op("vector", lambda v, x=xap, p2=P: v.bn_stats(st6a[0:p2, 6:12],
                                                             x[:, 512:1024]),
                 reads=xkeys, writes=("st6",))
            S.op("vector", lambda v, p2=P: v.bn_aggr(st2a[0:p2, :], st6a[0:p2, :]),
                 reads=("st6",), writes=("st2",))
            S.op("vector", lambda v, p2=P: v.tensor_single_scalar(
                lnwa[0:p2, 0:1], st2a[0:p2, 1:2], 1e-5, ALU.add),
                 reads=("st2",), writes=("lnw",))
            S.op("scalar", lambda s, p2=P: s.activation(lnwa[0:p2, 1:2],
                                                        lnwa[0:p2, 0:1], AF.Ln),
                 reads=("lnw",), writes=("lnw",))
            S.op("vector", lambda v, p2=P: v.tensor_single_scalar(
                lnwa[0:p2, 0:1], lnwa[0:p2, 1:2], -0.5, ALU.mult),
                 reads=("lnw",), writes=("lnw",))
            S.op("scalar", lambda s, p2=P: s.activation(lnwa[0:p2, 0:1],
                                                        lnwa[0:p2, 0:1], AF.Exp),
                 reads=("lnw",), writes=("lnw",))
            return st2a[:, 0:1], lnwa[:, 0:1]

        def wt_prep(af, wh, wbase, wst, pitch, scale, keypfx):
            for r in range(8):
                p = r % 2
                buf = af[:, wst + p * 1024: wst + (p + 1) * 1024]
                S.dma("sync", buf, wh.ap()[r * 128:(r + 1) * 128, :],
                      (f"{keypfx}{p}",), (f"{keypfx}{p}",))
                for c in range(8):
                    bk = pb[4 + 2 * p + c // 4]
                    tpose(bk[:, (c % 4) * 128:(c % 4) * 128 + 128],
                          buf[:, c * 128:(c + 1) * 128],
                          (f"{keypfx}{p}",), (f"pb{4 + 2 * p + c // 4}",))
                for half in range(2):
                    bki = 4 + 2 * p + half
                    dst = bass.AP(af, wbase + (half * 4) * 1024 + r * 128,
                                  [[pitch, 128], [1024, 4], [1, 128]])
                    src = pb[bki][:, 0:512].rearrange("p (g k) -> p g k", g=4)
                    eng = EV[(r * 2 + half) % 2]
                    if scale is None:
                        cop(eng, dst, src, (f"pb{bki}",), ("wf",))
                    elif eng == "scalar":
                        S.op("scalar", lambda s, d=dst, s2=src, sc2=scale:
                             s.mul(d, s2, sc2),
                             reads=(f"pb{bki}",), writes=("wf",))
                    else:
                        S.op("vector", lambda v, d=dst, s2=src, sc2=scale:
                             v.tensor_single_scalar(d, s2, sc2, ALU.mult),
                             reads=(f"pb{bki}",), writes=("wf",))

        wtiles = []
        for wi, (wh, scr3) in enumerate(
                ((wg_h, tg_scr), (wu_h, tu_scr), (wd_h, td_scr))):
            if wi < 2:
                for r in range(32):
                    wtiles.append((wi, wh.ap()[r * 128:(r + 1) * 128, :],
                                   scr3.ap()[r * 128:(r + 1) * 128, :]))
            else:
                for r in range(8):
                    for q in range(4):
                        wtiles.append((wi,
                                       wh.ap()[r * 128:(r + 1) * 128,
                                               q * 1024:(q + 1) * 1024],
                                       scr3.ap()[r * 128:(r + 1) * 128,
                                                 q * 1024:(q + 1) * 1024]))
        absac = sc[:, 32:35]

        PITA = 44160
        with (
            nc.sbuf_tensor("afA", [128, PITA], F32) as afA,
            nc.sbuf_tensor("mk", [128, 2560], BF16) as mkT,
            nc.sbuf_tensor("gm", [128, 16], BF16) as gmT,
            nc.sbuf_tensor("afW1", [128, 2048], F32) as afW1,
        ):
            WQ, WK, WV = 0, 8192, 16384
            XNT = 24576
            QT0 = 26624
            KT0 = 28672
            VT0 = 31232
            SCB, PTB = 35328, 37888
            XT, XN = 39936, 41984
            KGT = 44032
            WST = SCB
            XG = PTB
            XGT = PTB + 1024

            S.dma("sync", cst[:, 0:128], eye_h.ap(), (), ("cst",))
            S.dma("sync", cst[0:1, 128:256], oner_h.ap(), (), ("cst",))
            S.dma("sync", cst[:, 256:257], onec_h.ap(), (), ("cst",))
            S.dma("sync", mkT[:, 0:2560], maskn_h.ap(), (), ("mk",))
            S.dma("sync", gmT[:, 0:16], gmask_h.ap(), (), ("mk",))
            S.op("vector", lambda v: v.memset(cst[:, 267:268], CMAG),
                 reads=(), writes=("cst",))
            S.op("vector", lambda v: v.memset(cst[:, 268:269], -CMAG),
                 reads=(), writes=("cst",))

            wt_prep(afA, wq_h, WQ, WST, PITA, 0.125, "wsa")
            wt_prep(afA, wk_h, WK, WST, PITA, None, "wsa")
            wt_prep(afA, wv_h, WV, WST, PITA, None, "wsa")

            S.dma("sync", afA[0:16, XG:XG + 1024], xg_h.ap(), (), ("xg",))
            mean, rstd = ln_tile(afA[0:16, XG:XG + 1024], ("xg",), P=16)
            xgn = afA[0:16, XN:XN + 1024]
            S.op("vector", lambda v, m=mean, r=rstd: v.tensor_scalar(
                xgn, afA[0:16, XG:XG + 1024], m[0:16, :], r[0:16, :],
                ALU.subtract, ALU.mult),
                 reads=("xg", "st2", "lnw"), writes=("xn",))
            for c in range(8):
                tpose(pb6[0:128, c * 16:(c + 1) * 16],
                      afA[0:16, XN + c * 128:XN + (c + 1) * 128], ("xn",), ("pb6",),
                      pn=16)
            S.op("vector", lambda v: v.tensor_copy(afA[:, XGT:XGT + 128],
                                                   pb6[:, 0:128]),
                 reads=("pb6",), writes=("xgt",))

            def xgt_c(c):
                return afA[:, XGT + c * 16:XGT + (c + 1) * 16]

            for n in range(2):
                for c in range(8):
                    mm(pb[n][0:16, 0:512], xgt_c(c),
                       afA[:, WK + c * 1024 + n * 512:WK + c * 1024 + (n + 1) * 512],
                       ("xgt", "wf"), (f"pb{n}",), start=(c == 0), stop=(c == 7))
            for n in range(2):
                S.op("vector", lambda v, n2=n: v.tensor_copy(
                    afA[0:16, XG + n2 * 512:XG + (n2 + 1) * 512],
                    pb[n2][0:16, 0:512]),
                     reads=(f"pb{n}",), writes=("xg",))
            for m in range(8):
                tpose(pb6[0:128, m * 16:(m + 1) * 16],
                      afA[0:16, XG + m * 128:XG + (m + 1) * 128], ("xg",), ("pb6",),
                      pn=16)
            S.op("vector", lambda v: v.tensor_copy(afA[:, KGT:KGT + 128],
                                                   pb6[:, 0:128]),
                 reads=("pb6",), writes=("kgt",))
            for n in range(2):
                for c in range(8):
                    mm(pb[2 + n][0:16, 0:512], xgt_c(c),
                       afA[:, WV + c * 1024 + n * 512:WV + c * 1024 + (n + 1) * 512],
                       ("xgt", "wf"), (f"pb{2 + n}",), start=(c == 0), stop=(c == 7))
            for n in range(2):
                S.op("vector", lambda v, n2=n: v.tensor_copy(
                    afA[0:16, XG + n2 * 512:XG + (n2 + 1) * 512],
                    pb[2 + n2][0:16, 0:512]),
                     reads=(f"pb{2 + n}",), writes=("xg",))
            for tg in range(4):
                S.dma("gpsimd", afA[64:80, VT0 + tg * 1024:VT0 + (tg + 1) * 1024],
                      afA[0:16, XG:XG + 1024], ("xg",), ("vg",))
            for m in range(8):
                for tg in range(4):
                    cop(EV[(m + tg) % 2],
                        afA[:, KT0 + m * 320 + tg * 80 + 64:
                            KT0 + m * 320 + tg * 80 + 80],
                        afA[:, KGT + m * 16:KGT + (m + 1) * 16],
                        ("kgt",), ("ktg",))
            for wi in range(3):
                S.op("vector", lambda v, w2=wi: v.memset(absac[:, w2:w2 + 1], 0.0),
                     reads=(), writes=(f"aa{wi}",))
            S.fence()

            def w_absum_tiles(lo, hi):
                for t3 in range(lo, hi):
                    wi, src_ap, _ = wtiles[t3]
                    p = t3 % 2
                    buf = afW1[:, p * 1024:(p + 1) * 1024]
                    S.dma("sync", buf, src_ap, (f"wpa{p}",), (f"wpa{p}",))
                    red = sc[:, 35 + p:36 + p]
                    S.op("vector", lambda v, b=buf, rd=red: v.reduce_sum(
                        rd, b, axis=AX, apply_absolute_value=True),
                         reads=(f"wpa{p}",), writes=(f"red{p}",))
                    S.op("vector", lambda v, rd=red, w2=wi: v.tensor_tensor(
                        absac[:, w2:w2 + 1], absac[:, w2:w2 + 1], rd, ALU.add),
                         reads=(f"red{p}", f"aa{wi}"), writes=(f"aa{wi}",))

            def ln_prefetch(tb2):
                for ti in range(2):
                    tok0 = tb2 * TB + ti * 128
                    xt = afA[:, XT + ti * 1024:XT + (ti + 1) * 1024]
                    xn = afA[:, XN + ti * 1024:XN + (ti + 1) * 1024]
                    S.dma("sync", xt, x_h.ap()[tok0:tok0 + 128, :],
                          (f"xta{ti}",), (f"xta{ti}",))
                    mean, rstd = ln_tile(xt, (f"xta{ti}",))
                    S.op("vector", lambda v, x=xt, o=xn, m=mean, r=rstd:
                         v.tensor_scalar(o, x, m, r, ALU.subtract, ALU.mult),
                         reads=(f"xta{ti}", "st2", "lnw"), writes=(f"xn{ti}",))

            for tb in range(NTB):
                if tb == 0:
                    ln_prefetch(0)
                for ti in range(2):
                    tb0_ = 6 - 2 * ti
                    xn = afA[:, XN + ti * 1024:XN + (ti + 1) * 1024]
                    for c in range(8):
                        tpose(pb[tb0_ + c // 4][:, (c % 4) * 128:(c % 4) * 128 + 128],
                              xn[:, c * 128:(c + 1) * 128],
                              (f"xn{ti}",), (f"pb{tb0_ + c // 4}",))
                    for half in range(2):
                        dst = bass.AP(afA, XNT + (half * 4) * 256 + ti * 128,
                                      [[PITA, 128], [256, 4], [1, 128]])
                        src = pb[tb0_ + half][:, 0:512].rearrange(
                            "p (g k) -> p g k", g=4)
                        cop("scalar", dst, src, (f"pb{tb0_ + half}",), ("xnt",))

                if tb + 1 < NTB:
                    ln_prefetch(tb + 1)
                for wbase, dkey in ((WQ, "qt"), (WK, "kt")):
                    for m in range(8):
                        bk = pb[m % 4]
                        bcol = (m // 4) * 256
                        for c in range(8):
                            mm(bk[:, bcol:bcol + 256],
                               afA[:, wbase + c * 1024 + m * 128:
                                   wbase + c * 1024 + (m + 1) * 128],
                               afA[:, XNT + c * 256:XNT + (c + 1) * 256],
                               ("wf", "xnt"), (f"pb{m % 4}",),
                               start=(c == 0), stop=(c == 7))
                    for b2 in range(4):
                        if wbase == WQ:
                            dst = bass.AP(afA, QT0 + b2 * 256,
                                          [[PITA, 128], [1024, 2], [1, 256]])
                            src = pb[b2][:, 0:512].rearrange(
                                "p (g k) -> p g k", g=2)
                        else:
                            dst = bass.AP(afA, KT0 + b2 * 320,
                                          [[PITA, 128], [1280, 2], [80, 4], [1, 64]])
                            src = pb[b2][:, 0:512].rearrange(
                                "p (g t k) -> p g t k", g=2, t=4)
                        cop("scalar", dst, src, (f"pb{b2}",), (dkey,))
                for tg in range(4):
                    for n in range(2):
                        bki = 4 + (tg * 2 + n) % 4
                        for c in range(8):
                            mm(pb[bki][0:64, 0:512],
                               afA[:, XNT + c * 256 + tg * 64:
                                   XNT + c * 256 + tg * 64 + 64],
                               afA[:, WV + c * 1024 + n * 512:
                                   WV + c * 1024 + (n + 1) * 512],
                               ("wf", "xnt"), (f"pb{bki}",),
                               start=(c == 0), stop=(c == 7))
                        cop(EV[n],
                            afA[0:64, VT0 + tg * 1024 + n * 512:
                                VT0 + tg * 1024 + (n + 1) * 512],
                            pb[bki][0:64, 0:512],
                            (f"pb{bki}",), ("vt",))

                def scbuf(pr):
                    return afA[pr * 64:(pr + 1) * 64, SCB:SCB + 2560]

                for pr in range(2):
                    scb = scbuf(pr)
                    pk = f"scb{pr}"
                    for h2 in range(H):
                        qrow = (h2 % 2) * 64
                        for t2 in range(2):
                            ci = 2 * h2 + t2
                            bk = pb[ci // 6]
                            cc = (ci % 6) * 80
                            tg = pr * 2 + t2
                            lhs = afA[qrow:qrow + 64,
                                      QT0 + (h2 // 2) * 256 + tg * 64:
                                      QT0 + (h2 // 2) * 256 + tg * 64 + 64]
                            rhs = afA[qrow:qrow + 64,
                                      KT0 + (h2 // 2) * 320 + tg * 80:
                                      KT0 + (h2 // 2) * 320 + tg * 80 + 80]
                            mm(bk[0:64, cc:cc + 80], lhs, rhs,
                               ("qt", "kt", "ktg"), (f"pb{ci // 6}",))
                    for b2 in range(6):
                        w3 = 480 if b2 < 5 else 160
                        S.op("vector", lambda v, b=pb[b2], off=b2 * 480, w4=w3,
                             s4=scb, p2=pr:
                             v.tensor_tensor(s4[:, off:off + w4], b[0:64, 0:w4],
                                             mkT[p2 * 64:p2 * 64 + 64,
                                                 off:off + w4],
                                             ALU.add),
                             reads=(f"pb{b2}", "mk"), writes=(pk,))
                    if tb == 0 and pr == 0:
                        glob3 = bass.AP(afA, SCB + 64, [[PITA, 64], [160, 16],
                                                        [1, 16]])
                        gm3 = bass.AP(gmT, 0, [[16, 64], [0, 16], [1, 16]])
                        S.op("vector", lambda v, g3=glob3, g4=gm3: v.tensor_tensor(
                            g3, g3, g4, ALU.add),
                             reads=(pk, "mk"), writes=(pk,))
                    scb3 = scb.rearrange("p (h k) -> p h k", k=80)
                    mx3 = sc[pr * 64:pr * 64 + 64, 0:32].rearrange(
                        "p (h k) -> p h k", k=1)
                    S.op("vector", lambda v, o=mx3, i=scb3:
                         v.reduce_max(o, i, axis=AX),
                         reads=(pk,), writes=(f"mx{pr}",))
                    mxb = bass.AP(sc, pr * 64 * 128, [[128, 64], [1, 32], [0, 80]])
                    S.op("vector", lambda v, m3=mxb, s3=scb3: v.tensor_tensor(
                        s3, s3, m3, ALU.subtract),
                         reads=(pk, f"mx{pr}"), writes=(pk,))
                    S.op("scalar", lambda s, s3=scb: s.activation(s3, s3, AF.Exp),
                         reads=(pk,), writes=(pk,))
                    rs2 = sc[pr * 64:pr * 64 + 64, 96:128]
                    rs3 = rs2.rearrange("p (h k) -> p h k", k=1)
                    S.op("vector", lambda v, o=rs3, i=scb3:
                         v.reduce_sum(o, i, axis=AX),
                         reads=(pk,), writes=(f"rs{pr}",))
                    S.op("vector", lambda v, r=rs2: v.reciprocal(r, r),
                         reads=(f"rs{pr}",), writes=(f"rs{pr}",))
                    rsb = bass.AP(sc, pr * 64 * 128 + 96,
                                  [[128, 64], [1, 32], [0, 80]])
                    S.op("vector", lambda v, r=rsb, s3=scb3: v.tensor_tensor(
                        s3, s3, r, ALU.mult),
                         reads=(pk, f"rs{pr}"), writes=(pk,))

                PTBANKS = (6, 7, 0, 1)
                for pr in range(2):
                    scb = scbuf(pr)
                    pk = f"scb{pr}"
                    for ci in range(32):
                        bk = pb[PTBANKS[ci // 8]]
                        tpose(bk[0:80, (ci % 8) * 64:(ci % 8) * 64 + 64],
                              scb[:, ci * 80:ci * 80 + 80],
                              (pk,), (f"pb{PTBANKS[ci // 8]}",),
                              pb0_=pr * 64, pn=64)
                    for g2 in range(4):
                        S.op("scalar", lambda s, b=pb[PTBANKS[g2]], off=g2 * 512:
                             s.copy(afA[0:80, PTB + off:PTB + off + 512],
                                    b[0:80, 0:512]),
                             reads=(f"pb{PTBANKS[g2]}",), writes=("ptb",))
                    for h2 in range(H):
                        for t2 in range(2):
                            ci = 2 * h2 + t2
                            bk = pb[2 + ci // 8]
                            tg = pr * 2 + t2
                            lhsl = afA[0:80,
                                       VT0 + tg * 1024 + h2 * 64:
                                       VT0 + tg * 1024 + h2 * 64 + 64]
                            rhsl = afA[0:80, PTB + ci * 64:PTB + ci * 64 + 64]
                            mm(bk[0:64, (ci % 8) * 64:(ci % 8) * 64 + 64],
                               lhsl, rhsl, ("vt", "vg", "ptb"),
                               (f"pb{2 + ci // 8}",))
                    for g2 in range(4):
                        cop(EV[g2 % 2],
                            scb[:, g2 * 512:g2 * 512 + 512],
                            pb[2 + g2][0:64, 0:512],
                            (f"pb{2 + g2}",), (pk,))
                    pair = tb * 2 + pr
                    src = bass.AP(afA, pr * 64 * PITA + SCB,
                                  [[PITA, 64], [128, 16], [1, 128]])
                    dst = bass.AP(ctx_scr, pair * 131072,
                                  [[128, 64], [8192, 16], [1, 128]])
                    S.dma("gpsimd", dst, src, (pk,), ("ctxw",))
                w_absum_tiles(tb * 12, (tb + 1) * 12)
            S.fence()

        PITB = 34880
        with (
            nc.sbuf_tensor("afB", [128, PITB], F32) as afB,
            nc.sbuf_tensor("abB", [128, 12288], BF16) as abB,
            nc.sbuf_tensor("afW2", [128, 2048], F32) as afW2,
            nc.sbuf_tensor("abW", [128, 6144], BF16) as abW,
        ):
            WO, WST2 = 0, 8192
            CTXF, XT2, X1B, XN2, DQ = 10240, 14336, 18432, 22528, 26624
            GB2 = 30720
            LNS = 34816
            MD, ATI = 0, 8192

            mm(pb6[0:1, 0:3], onec, absac[:, 0:3], ("cst",), ("pb6",))
            S.op("vector", lambda v: v.tensor_copy(sc[0:1, 8:11], pb6[0:1, 0:3]),
                 reads=("pb6",), writes=("wm",))
            S.op("vector", lambda v: v.tensor_scalar(
                sc[0:1, 8:11], sc[0:1, 8:11], 1.0 / (INNER * D), 1e-5,
                ALU.mult, ALU.max),
                 reads=("wm",), writes=("wm",))
            S.op("vector", lambda v: v.tensor_single_scalar(
                sc[0:1, 11:14], sc[0:1, 8:11], 0.5, ALU.mult),
                 reads=("wm",), writes=("tau",))
            S.op("vector", lambda v: v.tensor_single_scalar(
                sc[0:1, 14:15], sc[0:1, 8:9], 1.0 / 254.0, ALU.mult),
                 reads=("wm",), writes=("tau",))
            S.op("vector", lambda v: v.reciprocal(sc[0:1, 17:18], sc[0:1, 8:9]),
                 reads=("wm",), writes=("mumg",))
            S.op("vector", lambda v: v.tensor_tensor(
                sc[0:1, 17:18], sc[0:1, 17:18], sc[0:1, 9:10], ALU.mult),
                 reads=("mumg", "wm"), writes=("mumg",))
            S.op("vector", lambda v: v.tensor_single_scalar(
                sc[0:1, 16:17], sc[0:1, 10:11], 1.0 / 254.0, ALU.mult),
                 reads=("wm",), writes=("md127",))
            mm(pb7[0:128, 0:4], oner, sc[0:1, 11:15], ("tau", "cst"), ("pb7",))
            S.op("vector", lambda v: v.tensor_copy(cst[:, 260:264], pb7[:, 0:4]),
                 reads=("pb7",), writes=("cst",))
            S.op("vector", lambda v: v.tensor_single_scalar(
                cst[:, 264:267], cst[:, 260:263], -1.0, ALU.mult),
                 reads=("cst",), writes=("cst",))

            def w_tern_tiles(lo, hi):
                for t3 in range(lo, hi):
                    wi, src_ap, dst_ap = wtiles[t3]
                    p = t3 % 2
                    buf = afW2[:, p * 1024:(p + 1) * 1024]
                    S.dma(("sync", "scalar")[p], buf, src_ap,
                          (f"wpa{p}",), (f"wpa{p}",))
                    s1 = abW[:, p * 1024:(p + 1) * 1024]
                    s2b = abW[:, 2048 + p * 1024:2048 + (p + 1) * 1024]
                    tn = abW[:, 4096 + p * 1024:4096 + (p + 1) * 1024]
                    S.op("scalar", lambda s, o=s1, b=buf, w2=wi: s.sign(
                        o, b, bias=cst[:, 264 + w2:265 + w2]),
                         reads=(f"wpa{p}", "cst"), writes=(f"tp{p}",))
                    S.op("scalar", lambda s, o=s2b, b=buf, w2=wi: s.sign(
                        o, b, bias=cst[:, 260 + w2:261 + w2]),
                         reads=(f"wpa{p}", "cst"), writes=(f"tq{p}",))
                    S.op("gpsimd", lambda g, o=tn, a=s1, b=s2b: g.tensor_tensor(
                        o, a, b, ALU.add),
                         reads=(f"tp{p}", f"tq{p}"), writes=(f"tn{p}",))
                    S.dma("sync", dst_ap, tn, (f"tn{p}",), ("ts",))

            wt_prep(afB, wo_h, WO, WST2, PITB, None, "wsb")

            def btile(ti):
                p = ti % 4
                return dict(
                    p=p, tok0=ti * 128,
                    cf=afB[:, CTXF + p * 1024:CTXF + (p + 1) * 1024],
                    xt=afB[:, XT2 + p * 1024:XT2 + (p + 1) * 1024],
                    x1=afB[:, X1B + p * 1024:X1B + (p + 1) * 1024],
                    xn2=afB[:, XN2 + p * 1024:XN2 + (p + 1) * 1024],
                    ic=afB[:, DQ + p * 1024:DQ + (p + 1) * 1024],
                    gq=afB[:, GB2 + p * 1024:GB2 + (p + 1) * 1024],
                    mp=abB[:, MD + p * 1024:MD + (p + 1) * 1024],
                    mdn=abB[:, MD + 4096 + p * 1024:MD + 4096 + (p + 1) * 1024],
                    at=abB[:, ATI + p * 1024:ATI + (p + 1) * 1024],
                    st6=afB[:, LNS + p * 16:LNS + p * 16 + 12],
                    st2=afB[:, LNS + p * 16 + 12:LNS + p * 16 + 14],
                    lnw=afB[:, LNS + p * 16 + 14:LNS + p * 16 + 16],
                    mr=maxr_nat[:, ti:ti + 1], sx=sc[:, 80 + ti:81 + ti])

            for g in range(4):
                tis = range(g * 4, g * 4 + 4)
                bt = {ti: btile(ti) for ti in tis}
                w_tern_tiles(g * 24, (g + 1) * 24)
                for ti in tis:
                    b = bt[ti]
                    p = b["p"]
                    csrc = bass.AP(ctx_scr, ti * 131072,
                                   [[128, 128], [16384, 8], [1, 128]])
                    cdst = bass.AP(afB, CTXF + p * 1024,
                                   [[PITB, 128], [128, 8], [1, 128]])
                    S.dma("sync", cdst, csrc, (f"cf{p}",), (f"cf{p}",))
                    S.dma("sync", b["xt"], x_h.ap()[b["tok0"]:b["tok0"] + 128, :],
                          (f"xtb{p}",), (f"xtb{p}",))
                for ti in tis:
                    b = bt[ti]
                    p = b["p"]
                    wb0 = 4 + 2 * (ti % 2)
                    for n in range(2):
                        bk = pb[wb0 + n]
                        for c in range(8):
                            mm(bk[:, 0:512], b["cf"][:, c * 128:(c + 1) * 128],
                               afB[:, WO + c * 1024 + n * 512:
                                   WO + c * 1024 + (n + 1) * 512],
                               (f"cf{p}", "wf"), (f"pb{wb0 + n}",),
                               start=(c == 0), stop=(c == 7))
                    for n in range(2):
                        S.op("vector", lambda v, b2=pb[wb0 + n], n2=n,
                             x2=b["x1"], xx=b["xt"]:
                             v.tensor_tensor(x2[:, n2 * 512:(n2 + 1) * 512],
                                             b2[:, 0:512],
                                             xx[:, n2 * 512:(n2 + 1) * 512],
                                             ALU.add),
                             reads=(f"pb{wb0 + n}", f"xtb{p}"), writes=(f"x1{p}",))
                    S.dma("scalar", x1_scr.ap()[b["tok0"]:b["tok0"] + 128, :],
                          b["x1"], (f"x1{p}",), ("x1s",))
                for ti in tis:
                    b = bt[ti]
                    p = b["p"]
                    S.op("vector", lambda v, x=b["x1"], o=b["st6"]: v.bn_stats(
                        o[:, 0:6], x[:, 0:512]),
                         reads=(f"x1{p}",), writes=(f"st6{p}",))
                    S.op("vector", lambda v, x=b["x1"], o=b["st6"]: v.bn_stats(
                        o[:, 6:12], x[:, 512:1024]),
                         reads=(f"x1{p}",), writes=(f"st6{p}",))
                    S.op("vector", lambda v, o=b["st2"], i=b["st6"]: v.bn_aggr(o, i),
                         reads=(f"st6{p}",), writes=(f"st2{p}",))
                    S.op("vector", lambda v, o=b["lnw"], i=b["st2"]:
                         v.tensor_single_scalar(o[:, 0:1], i[:, 1:2], 1e-5, ALU.add),
                         reads=(f"st2{p}",), writes=(f"lnw{p}",))
                    S.op("vector", lambda v, o=b["lnw"]:
                         v.reciprocal(o[:, 1:2], o[:, 0:1]),
                         reads=(f"lnw{p}",), writes=(f"lnw{p}",))
                for ti in tis:
                    b = bt[ti]
                    p = b["p"]
                    S.op("scalar", lambda s, o=b["lnw"]: s.sqrt(o[:, 0:1], o[:, 1:2]),
                         reads=(f"lnw{p}",), writes=(f"lnw{p}",))
                for ti in tis:
                    b = bt[ti]
                    p = b["p"]
                    S.op("vector", lambda v, x2=b["x1"], o=b["xn2"], m=b["st2"],
                         r=b["lnw"]:
                         v.tensor_scalar(o, x2, m[:, 0:1], r[:, 0:1],
                                         ALU.subtract, ALU.mult),
                         reads=(f"x1{p}", f"st2{p}", f"lnw{p}"), writes=(f"xn2{p}",))
                    S.op("vector", lambda v, m3=b["mr"], x2=b["xn2"]: v.reduce_max(
                        m3, x2, axis=AX, apply_absolute_value=True),
                         reads=(f"xn2{p}",), writes=("maxr",))
                    S.op("vector", lambda v, m3=b["mr"]: v.tensor_single_scalar(
                        m3, m3, 1e-5, ALU.max),
                         reads=("maxr",), writes=("maxr",))
                    S.op("vector", lambda v, m3=b["mr"], s3=b["sx"]:
                         v.reciprocal(s3, m3),
                         reads=("maxr",), writes=("sx",))
                    S.op("vector", lambda v, s3=b["sx"]: v.tensor_single_scalar(
                        s3, s3, 127.0, ALU.mult),
                         reads=("sx",), writes=("sx",))
                i0f = i0b[:, 0:1024].bitcast(F32)
                for ti in tis:
                    b = bt[ti]
                    p = b["p"]
                    S.op("scalar", lambda s, o=b["gq"], x2=b["xn2"], s3=b["sx"]:
                         s.mul(o, x2, s3),
                         reads=(f"xn2{p}", "sx"), writes=(f"gq{p}",))
                for ti in tis:
                    b = bt[ti]
                    p = b["p"]
                    S.op("scalar", lambda s, o=i0f, i=b["gq"]: s.add(
                        o, i, cst[:, 267:268]),
                         reads=(f"gq{p}", "cst", "i0"), writes=("i0",))
                    S.op("scalar", lambda s, o=b["ic"], i=i0f: s.add(
                        o, i, cst[:, 268:269]),
                         reads=("i0", "cst"), writes=(f"ic{p}",))
                for ti in tis:
                    b = bt[ti]
                    p = b["p"]
                    S.op(PAR[p % 2], lambda v, o=b["xt"], i=b["ic"], g2=b["gq"]:
                         v.tensor_sub(o, g2, i),
                         reads=(f"ic{p}", f"gq{p}"), writes=(f"xtb{p}",))
                for ti in tis:
                    b = bt[ti]
                    p = b["p"]
                    S.op("vector", lambda v, o=b["mp"], i=b["xt"]:
                         v.tensor_single_scalar(o, i, 0.5, ALU.is_gt),
                         reads=(f"xtb{p}",), writes=(f"mp{p}",))
                    S.op("vector", lambda v, o=b["mdn"], i=b["xt"], m3=b["mp"]:
                         v.scalar_tensor_tensor(
                             o, i, -0.5, m3, ALU.is_lt, ALU.subtract),
                         reads=(f"xtb{p}", f"mp{p}"), writes=(f"md{p}",))
                    S.op("vector", lambda v, o=b["at"], a=b["ic"], b2=b["mdn"]:
                         v.tensor_sub(o, a, b2),
                         reads=(f"ic{p}", f"md{p}"), writes=(f"at{p}",))
                    S.dma("gpsimd", a_scr.ap()[b["tok0"]:b["tok0"] + 128, :],
                          b["at"], (f"at{p}",), ("as",))
            S.fence()

        with (
            nc.sbuf_tensor("abF", [128, 49152], BF16) as abF,
            nc.sbuf_tensor("afF", [128, 20736], F32) as afF,
        ):
            WG0, WU0, ATL = 0, 16384, 32768
            GUH, RMX0, RMX1 = 0, 8192, 10240
            ABG, ABU = 12288, 14336
            RA, RG, RU = 16384, 16512, 18560

            for c in range(8):
                S.dma("sync", abF[:, ATL + c * 2048:ATL + (c + 1) * 2048],
                      a_scr.ap()[0:T, c * 128:(c + 1) * 128],
                      (), ("atb",), transpose=True)
            S.op("vector", lambda v: v.tensor_scalar_mul(sc[:, 0:16], maxr_nat,
                                                         mg127_v),
                 reads=("cst",), writes=("ag",))
            tpose(pb6[0:16, 0:128], sc[:, 0:16], ("ag",), ("pb6",))
            S.op("vector", lambda v: v.tensor_copy(afF[0:16, RA:RA + 128],
                                                   pb6[0:16, 0:128]),
                 reads=("pb6",), writes=("ra",))
            S.dma("gpsimd",
                  row_scr.ap()[0:1, 0:T].rearrange("a (b c) -> (a b) c", c=128),
                  afF[0:16, RA:RA + 128], ("ra",), ("rowh",))
            S.dma("sync", afF[0:1, RG:RG + T], row_scr.ap()[0:1, 0:T],
                  ("rowh",), ("rg",))
            S.op("vector", lambda v: v.tensor_scalar_mul(
                afF[0:1, RU:RU + T], afF[0:1, RG:RG + T], sc[0:1, 17:18]),
                 reads=("rg",), writes=("ru",))
            for src_off, dst, rk in ((RG, ABG, "rg"), (RU, ABU, "ru")):
                for q in range(4):
                    bk = pb[2 + q % 2]
                    mm(bk[:, 0:512], oner,
                       afF[0:1, src_off + q * 512:src_off + (q + 1) * 512],
                       (rk, "cst"), (f"pb{2 + q % 2}",))
                    cop(EV[q % 2], afF[:, dst + q * 512:dst + (q + 1) * 512],
                        bk[:, 0:512], (f"pb{2 + q % 2}",), ("ab",))
            S.op("vector", lambda v: v.memset(afF[:, RMX0:RMX0 + 2048], 0.0),
                 reads=(), writes=("rm0",))
            S.op("gpsimd", lambda g: g.memset(afF[:, RMX1:RMX1 + 2048], 0.0),
                 reads=(), writes=("rm1",))

            for half in range(2):
                for wi, (scr3, dst) in enumerate(((tg_scr, WG0), (tu_scr, WU0))):
                    for c in range(8):
                        S.dma("scalar", abF[:, dst + c * 2048:dst + (c + 1) * 2048],
                              scr3.ap()[half * 2048:(half + 1) * 2048,
                                        c * 128:(c + 1) * 128],
                              ("wgw" if wi == 0 else "wuw",),
                              ("wgw" if wi == 0 else "wuw",), transpose=True)
                for kc in range(16):
                    for tb2 in range(4):
                        q = (kc * 4 + tb2) % 4
                        bg, bu = pb[q], pb[4 + q]
                        for c in range(8):
                            mm(bg[:, 0:512],
                               abF[:, WG0 + c * 2048 + kc * 128:
                                   WG0 + c * 2048 + (kc + 1) * 128],
                               abF[:, ATL + c * 2048 + tb2 * 512:
                                   ATL + c * 2048 + (tb2 + 1) * 512],
                               ("wgw", "atb"), (f"pb{q}",),
                               start=(c == 0), stop=(c == 7))
                            mm(bu[:, 0:512],
                               abF[:, WU0 + c * 2048 + kc * 128:
                                   WU0 + c * 2048 + (kc + 1) * 128],
                               abF[:, ATL + c * 2048 + tb2 * 512:
                                   ATL + c * 2048 + (tb2 + 1) * 512],
                               ("wuw", "atb"), (f"pb{4 + q}",),
                               start=(c == 0), stop=(c == 7))
                        gb = afF[:, GUH + q * 2048:GUH + q * 2048 + 512]
                        gs = afF[:, GUH + q * 2048 + 512:GUH + q * 2048 + 1024]
                        ub = afF[:, GUH + q * 2048 + 1024:GUH + q * 2048 + 1536]
                        hb = afF[:, GUH + q * 2048 + 1536:GUH + q * 2048 + 2048]
                        S.op("vector", lambda v, o=gb, b=bg, t4=tb2: v.tensor_tensor(
                            o, b[:, 0:512],
                            afF[:, ABG + t4 * 512:ABG + (t4 + 1) * 512], ALU.mult),
                             reads=(f"pb{q}", "ab"), writes=(f"gb{q}",))
                        S.op("scalar", lambda s, o=gs, i=gb:
                             s.activation(o, i, AF.Silu),
                             reads=(f"gb{q}",), writes=(f"gs{q}",))
                        S.op("vector", lambda v, o=ub, b=bu, t4=tb2: v.tensor_tensor(
                            o, b[:, 0:512],
                            afF[:, ABU + t4 * 512:ABU + (t4 + 1) * 512], ALU.mult),
                             reads=(f"pb{4 + q}", "ab"), writes=(f"ub{q}",))
                        S.op("gpsimd", lambda g, o=hb, a=gs, b=ub: g.tensor_tensor(
                            o, a, b, ALU.mult),
                             reads=(f"gs{q}", f"ub{q}"), writes=(f"hb{q}",))
                        rmx0 = afF[:, RMX0 + tb2 * 512:RMX0 + (tb2 + 1) * 512]
                        rmx1 = afF[:, RMX1 + tb2 * 512:RMX1 + (tb2 + 1) * 512]
                        S.op("vector", lambda v, o=rmx0, b=hb: v.tensor_tensor(
                            o, o, b, ALU.max),
                             reads=(f"hb{q}", "rm0"), writes=("rm0",))
                        S.op("vector", lambda v, o=rmx1, b=hb: v.tensor_tensor(
                            o, o, b, ALU.min),
                             reads=(f"hb{q}", "rm1"), writes=("rm1",))
                        kg = half * 16 + kc
                        S.dma("sync",
                              h_scr.ap()[kg * 128:(kg + 1) * 128,
                                         tb2 * 512:(tb2 + 1) * 512],
                              hb, (f"hb{q}",), ("hs",))
            S.op("vector", lambda v: v.tensor_single_scalar(
                afF[:, RMX1:RMX1 + 2048], afF[:, RMX1:RMX1 + 2048], -1.0, ALU.mult),
                 reads=("rm1",), writes=("rm1",))
            S.op("vector", lambda v: v.tensor_tensor(
                afF[:, RMX0:RMX0 + 2048], afF[:, RMX0:RMX0 + 2048],
                afF[:, RMX1:RMX1 + 2048], ALU.max),
                 reads=("rm0", "rm1"), writes=("rm0",))
            for sl in range(16):
                bk = pb[sl % 2]
                tpose(bk[:, 0:128], afF[:, RMX0 + sl * 128:RMX0 + (sl + 1) * 128],
                      ("rm0",), (f"pb{sl % 2}",))
                S.op("vector", lambda v, b=bk, s3=sl: v.reduce_max(
                    maxh_nat[:, s3:s3 + 1], b[:, 0:128], axis=AX),
                     reads=(f"pb{sl % 2}",), writes=("maxh",))
            S.op("vector", lambda v: v.tensor_single_scalar(
                maxh_nat, maxh_nat, 1e-5, ALU.max),
                 reads=("maxh",), writes=("maxh",))
            tpose(pb6[0:16, 0:128], maxh_nat, ("maxh",), ("pb6",))
            S.op("vector", lambda v: v.tensor_copy(afF[0:16, RA:RA + 128],
                                                   pb6[0:16, 0:128]),
                 reads=("pb6",), writes=("ra",))
            S.dma("gpsimd",
                  row_scr.ap()[2:3, 0:T].rearrange("a (b c) -> (a b) c", c=128),
                  afF[0:16, RA:RA + 128], ("ra",), ("rowh2",))
            S.fence()

        with (
            nc.sbuf_tensor("abG", [128, 38912], BF16) as abG,
            nc.sbuf_tensor("afG", [128, 24704], F32) as afG,
        ):
            WD0, HQ, MD2 = 0, 32768, 34816
            HT, G3, IC2, DD2 = 0, 2048, 4096, 6144
            SYB, BDB, FFT = 8192, 10240, 12288
            X1T, OUT = 16384, 18432
            ROWW, RMH, RROW = 20480, 20608, 22656

            for kc in range(32):
                S.dma(("scalar", "sync")[kc % 2],
                      abG[:, WD0 + kc * 1024:WD0 + (kc + 1) * 1024],
                      td_scr.ap()[0:D, kc * 128:(kc + 1) * 128],
                      (), ("wdt",), transpose=True)
            S.dma("sync", afG[0:1, RMH:RMH + T], row_scr.ap()[2:3, 0:T],
                  (), ("rmh",))
            S.op("vector", lambda v: v.reciprocal(afG[0:1, RROW:RROW + T],
                                                  afG[0:1, RMH:RMH + T]),
                 reads=("rmh",), writes=("rrow",))
            S.op("vector", lambda v: v.tensor_single_scalar(
                afG[0:1, RROW:RROW + T], afG[0:1, RROW:RROW + T], 127.0, ALU.mult),
                 reads=("rrow",), writes=("rrow",))
            for q in range(4):
                bk = pb[2 + q % 2]
                mm(bk[:, 0:512], oner, afG[0:1, RROW + q * 512:RROW + (q + 1) * 512],
                   ("rrow", "cst"), (f"pb{2 + q % 2}",))
                cop(EV[q % 2], afG[:, SYB + q * 512:SYB + (q + 1) * 512],
                    bk[:, 0:512], (f"pb{2 + q % 2}",), ("syb",))
            S.op("vector", lambda v: v.tensor_scalar_mul(
                afG[0:1, RROW:RROW + T], afG[0:1, RMH:RMH + T], sc[0:1, 16:17]),
                 reads=("rmh",), writes=("rrow",))
            for q in range(4):
                bk = pb[2 + q % 2]
                mm(bk[:, 0:512], oner, afG[0:1, RROW + q * 512:RROW + (q + 1) * 512],
                   ("rrow", "cst"), (f"pb{2 + q % 2}",))
                cop(EV[q % 2], afG[:, BDB + q * 512:BDB + (q + 1) * 512],
                    bk[:, 0:512], (f"pb{2 + q % 2}",), ("bdb",))

            for tb2 in range(4):
                for kc in range(32):
                    p = kc % 4
                    ht = afG[:, HT + p * 512:HT + (p + 1) * 512]
                    S.dma("sync", ht,
                          h_scr.ap()[kc * 128:(kc + 1) * 128,
                                     tb2 * 512:(tb2 + 1) * 512],
                          (f"ht{p}",), (f"ht{p}",))
                    g3 = afG[:, G3 + p * 512:G3 + (p + 1) * 512]
                    ic = afG[:, IC2 + p * 512:IC2 + (p + 1) * 512]
                    dd = afG[:, DD2 + p * 512:DD2 + (p + 1) * 512]
                    i0f = i0b[:, 0:512].bitcast(F32)
                    mp = abG[:, MD2 + p * 512:MD2 + (p + 1) * 512]
                    mdn = abG[:, MD2 + 1024 + p * 512:MD2 + 1024 + (p + 1) * 512]
                    hq = abG[:, HQ + p * 512:HQ + (p + 1) * 512]
                    S.op("gpsimd", lambda g, o=g3, a=ht, t4=tb2: g.tensor_tensor(
                        o, a, afG[:, SYB + t4 * 512:SYB + (t4 + 1) * 512], ALU.mult),
                         reads=(f"ht{p}", "syb"), writes=(f"g3{p}",))
                    S.op("scalar", lambda s, o=i0f, i=g3: s.add(
                        o, i, cst[:, 267:268]),
                         reads=(f"g3{p}", "cst", "i0"), writes=("i0",))
                    S.op("scalar", lambda s, o=ic, i=i0f: s.add(
                        o, i, cst[:, 268:269]),
                         reads=("i0", "cst"), writes=(f"ic{p}",))
                    S.op("gpsimd", lambda g, o=dd, a=ic, b=g3: g.tensor_sub(o, b, a),
                         reads=(f"ic{p}", f"g3{p}"), writes=(f"dd{p}",))
                    S.op("vector", lambda v, o=mp, i=dd: v.tensor_single_scalar(
                        o, i, 0.5, ALU.is_gt),
                         reads=(f"dd{p}",), writes=(f"mp{p}",))
                    S.op("vector", lambda v, o=mdn, i=dd, m3=mp:
                         v.scalar_tensor_tensor(
                             o, i, -0.5, m3, ALU.is_lt, ALU.subtract),
                         reads=(f"dd{p}", f"mp{p}",), writes=(f"md{p}",))
                    S.op("vector", lambda v, o=hq, a=ic, b=mdn: v.tensor_sub(
                        o, a, b),
                         reads=(f"ic{p}", f"md{p}"), writes=(f"hq{p}",))
                    for m2 in range(8):
                        mm(pb[m2][:, 0:512],
                           abG[:, WD0 + kc * 1024 + m2 * 128:
                               WD0 + kc * 1024 + (m2 + 1) * 128],
                           hq, ("wdt", f"hq{p}"), (f"pb{m2}",),
                           start=(kc == 0), stop=(kc == 31))
                for m2 in range(8):
                    S.op("vector", lambda v, b=pb[m2], m3=m2, t4=tb2:
                         v.tensor_tensor(
                             afG[:, FFT + m3 * 512:FFT + (m3 + 1) * 512],
                             b[:, 0:512],
                             afG[:, BDB + t4 * 512:BDB + (t4 + 1) * 512], ALU.mult),
                         reads=(f"pb{m2}", "bdb"), writes=("fft",))
                for sl in range(4):
                    tok0 = tb2 * 512 + sl * 128
                    sp = sl % 2
                    x1t = afG[:, X1T + sp * 1024:X1T + (sp + 1) * 1024]
                    S.dma("sync", x1t, x1_scr.ap()[tok0:tok0 + 128, :],
                          (f"x1t{sp}", "x1s"), (f"x1t{sp}",))
                    for m2 in range(8):
                        tpose(pb[m2 // 4][:, (m2 % 4) * 128:(m2 % 4) * 128 + 128],
                              afG[:, FFT + m2 * 512 + sl * 128:
                                  FFT + m2 * 512 + sl * 128 + 128],
                              ("fft",), (f"pb{m2 // 4}",))
                    ot = afG[:, OUT + sp * 1024:OUT + (sp + 1) * 1024]
                    for b2 in range(2):
                        S.op("vector", lambda v, b=pb[b2], b3=b2, o=ot, x2=x1t:
                             v.tensor_tensor(o[:, b3 * 512:(b3 + 1) * 512],
                                             b[:, 0:512],
                                             x2[:, b3 * 512:(b3 + 1) * 512], ALU.add),
                             reads=(f"pb{b2}", f"x1t{sp}"), writes=(f"ot{sp}",))
                    S.dma("scalar", y_h.ap()[tok0:tok0 + 128, :], ot,
                          (f"ot{sp}",), ("ys",))
            S.fence()
        S.finalize()

    return nc


_NC_CACHE = None


def mk_mask(first_pair):
    m = np.zeros((64, 2560), np.float32)
    j = np.arange(64)
    for h in range(H):
        for t2 in range(2):
            ci = 2 * h + t2
            for q in range(64):
                m[q, ci * 80:ci * 80 + 64][j > q] = NEG
            if first_pair and t2 == 0:
                m[:, ci * 80 + 64:ci * 80 + 80] = NEG
    return m.astype(ml_dtypes.bfloat16)


def kernel(L0_attn_hidden_states, L0_attn_Wq, L0_attn_Wk, L0_attn_Wv,
           L0_attn_Wo, L0_ffn_w_gate, L0_ffn_w_up, L0_ffn_w_down):
    global _NC_CACHE
    x = np.ascontiguousarray(L0_attn_hidden_states, dtype=np.float32)
    B, Sq, Dm = x.shape
    xf = x.reshape(B * Sq, Dm)

    eye = np.eye(128, dtype=np.float32)
    onec = np.ones((128, 1), np.float32)
    oner = np.ones((1, 128), np.float32)

    mask_n = np.vstack([mk_mask(False)] * 2)

    if _NC_CACHE is None:
        _NC_CACHE = build_nc()
    nc = _NC_CACHE

    in_maps = []
    for c in range(8):
        b = c // 2
        sh = c % 2
        in_maps.append({
            "x": np.ascontiguousarray(xf[c * T:(c + 1) * T]),
            "xg": np.ascontiguousarray(x[b, 0:16, :]),
            "wq": np.ascontiguousarray(L0_attn_Wq, np.float32),
            "wk": np.ascontiguousarray(L0_attn_Wk, np.float32),
            "wv": np.ascontiguousarray(L0_attn_Wv, np.float32),
            "wo": np.ascontiguousarray(L0_attn_Wo, np.float32),
            "wg": np.ascontiguousarray(L0_ffn_w_gate, np.float32),
            "wu": np.ascontiguousarray(L0_ffn_w_up, np.float32),
            "wd": np.ascontiguousarray(L0_ffn_w_down, np.float32),
            "eye": eye, "onec": onec, "oner": oner,
            "maskn": mask_n,
            "gmask": np.full((128, 16), NEG if sh == 0 else 0.0,
                             ml_dtypes.bfloat16),
        })
    res = run_bass_kernel_spmd(nc, in_maps, core_ids=list(range(8)))
    out = np.concatenate([res.results[c]["y"] for c in range(8)], axis=0)
    return out.reshape(B, Sq, Dm)


# version 25
# speedup vs baseline: 1.0699x; 1.0011x over previous
import numpy as np
import ml_dtypes
import concourse.bass as bass
import concourse.mybir as mybir
from concourse.bass_utils import run_bass_kernel_spmd

F32 = mybir.dt.float32
BF16 = mybir.dt.bfloat16
I32 = mybir.dt.int32
AX = mybir.AxisListType.X
AF = mybir.ActivationFunctionType
ALU = mybir.AluOpType

T = 2048
D = 1024
H = 16
INNER = 4096
TB = 256
NTB = T // TB
CMAG = float(np.float32(3 * 2 ** 22))
NEG = -1.0e30
PAR = ("vector", "gpsimd")



class Sched:

    ENGINES = ("sync", "scalar", "vector", "tensor", "gpsimd")
    CENG = ("scalar", "vector", "tensor", "gpsimd")
    DMAQ = ("sync", "scalar", "gpsimd")
    NLANE = 12

    def __init__(self, nc):
        self.nc = nc
        self.ops = []
        self.last_w = {}
        self.last_r = {}
        self.emitted = 0
        self.ccnt = {e: 0 for e in self.CENG}
        self.dtot = {e: 0 for e in self.DMAQ}
        self.dlcnt = {e: [0] * self.NLANE for e in self.DMAQ}
        self.waited = {e: {} for e in self.ENGINES}
        self.drained = {e: -1 for e in self.ENGINES}
        self.engpos = {e: 0 for e in self.ENGINES}
        self.snap = (dict(self.ccnt),
                     {e: list(v) for e, v in self.dlcnt.items()})
        self.csem = {}
        self.dsem = {}

    def make_sems(self, ent):
        for e in self.CENG:
            self.csem[e] = ent(self.nc.semaphore(f"c_{e}"))
        for e in self.DMAQ:
            self.dsem[e] = [ent(self.nc.semaphore(f"d_{e}{i}"))
                            for i in range(self.NLANE)]

    def op(self, engine, fn, reads=(), writes=(), dma=False):
        assert engine in self.ENGINES
        assert dma or engine != "sync"
        i = len(self.ops)
        deps = set()
        for b in reads:
            if b in self.last_w:
                deps.add(self.last_w[b])
        for b in writes:
            if b in self.last_w:
                deps.add(self.last_w[b])
            deps.update(self.last_r.get(b, ()))
        deps.discard(i)
        o = dict(idx=i, engine=engine, fn=fn, deps=sorted(deps),
                 dma=dma, inc=bool(dma), semval=None, pos=None, lane=None)
        if dma:
            o["lane"] = self.dtot[engine] % self.NLANE
            self.dtot[engine] += 1
        self.ops.append(o)
        for b in reads:
            self.last_r.setdefault(b, set()).add(i)
        for b in writes:
            self.last_w[b] = i
            self.last_r[b] = set()
        return i

    def dma(self, queue, out_ap, in_ap, reads=(), writes=(), transpose=False):
        self.op(queue,
                lambda e, o=out_ap, i=in_ap, tr=transpose:
                e.dma_start(o, i, transpose=True) if tr else e.dma_start(o, i),
                reads=reads, writes=writes, dma=True)

    def _fence_waits(self, e, eng, snapc, snapd):
        w = self.waited[e]
        for e2 in self.CENG:
            if e2 != e and snapc[e2] > 0 and w.get(("c", e2), 0) < snapc[e2]:
                eng.wait_ge(self.csem[e2], snapc[e2])
                w[("c", e2)] = snapc[e2]
        for e2 in self.DMAQ:
            for l in range(self.NLANE):
                v = 16 * snapd[e2][l]
                if v > 0 and w.get(("d", e2, l), 0) < v:
                    eng.wait_ge(self.dsem[e2][l], v)
                    w[("d", e2, l)] = v

    def fence(self):
        base = self.emitted
        seg = self.ops[base:]
        if not seg:
            return
        for o in seg:
            for d in o["deps"]:
                if d < base:
                    continue
                od = self.ops[d]
                if od["engine"] != o["engine"] and not od["dma"]:
                    od["inc"] = True
        eng_ops = {e: [] for e in self.ENGINES}
        for o in seg:
            o["pos"] = self.engpos[o["engine"]]
            self.engpos[o["engine"]] += 1
            eng_ops[o["engine"]].append(o)
        for e, lst in eng_ops.items():
            if lst and not lst[-1]["dma"]:
                lst[-1]["inc"] = True
        for o in seg:
            if o["dma"]:
                q, l = o["engine"], o["lane"]
                self.dlcnt[q][l] += 1
                o["semval"] = 16 * self.dlcnt[q][l]
            elif o["inc"]:
                self.ccnt[o["engine"]] += 1
                o["semval"] = self.ccnt[o["engine"]]

        snapc, snapd = self.snap

        def run(e, eng):
            w = self.waited[e]
            self._fence_waits(e, eng, snapc, snapd)
            for o in eng_ops[e]:
                need_drain = False
                for d in o["deps"]:
                    if d < base:
                        continue
                    od = self.ops[d]
                    if od["engine"] == e and not od["dma"]:
                        if od["pos"] > self.drained[e]:
                            need_drain = True
                        continue
                    assert od["inc"]
                    if od["dma"]:
                        k = ("d", od["engine"], od["lane"])
                        sem = self.dsem[od["engine"]][od["lane"]]
                    else:
                        k = ("c", od["engine"])
                        sem = self.csem[od["engine"]]
                    if w.get(k, 0) < od["semval"]:
                        eng.wait_ge(sem, od["semval"])
                        w[k] = od["semval"]
                if need_drain:
                    eng.drain()
                    self.drained[e] = o["pos"] - 1
                if o["dma"]:
                    k = ("d", e, o["lane"])
                    prev = o["semval"] - 16
                    if prev > 0 and w.get(k, 0) < prev:
                        eng.wait_ge(self.dsem[e][o["lane"]], prev)
                        w[k] = prev
                ins = o["fn"](eng)
                if o["dma"]:
                    ins.then_inc(self.dsem[e][o["lane"]], 16)
                elif o["inc"]:
                    ins.then_inc(self.csem[o["engine"]], 1)

        with self.nc.Block() as blk:
            for e in self.ENGINES:
                if eng_ops[e]:
                    getattr(blk, e)(lambda eng, e=e: run(e, eng))

        self.emitted = len(self.ops)
        self.snap = (dict(self.ccnt),
                     {e: list(v) for e, v in self.dlcnt.items()})
        self.last_w = {}
        self.last_r = {}
        for e in self.ENGINES:
            self.drained[e] = self.engpos[e] - 1

    def finalize(self):
        snapc = dict(self.ccnt)
        snapd = {e: list(v) for e, v in self.dlcnt.items()}
        with self.nc.Block() as blk:
            for e in self.ENGINES:
                getattr(blk, e)(
                    lambda eng, e=e: self._fence_waits(e, eng, snapc, snapd))




def build_nc():
    nc = bass.Bass("TRN2", target_bir_lowering=False)

    x_h = nc.dram_tensor("x", [T, D], F32, kind="ExternalInput")
    xg_h = nc.dram_tensor("xg", [16, D], F32, kind="ExternalInput")
    wq_h = nc.dram_tensor("wq", [D, D], F32, kind="ExternalInput")
    wk_h = nc.dram_tensor("wk", [D, D], F32, kind="ExternalInput")
    wv_h = nc.dram_tensor("wv", [D, D], F32, kind="ExternalInput")
    wo_h = nc.dram_tensor("wo", [D, D], F32, kind="ExternalInput")
    wg_h = nc.dram_tensor("wg", [INNER, D], F32, kind="ExternalInput")
    wu_h = nc.dram_tensor("wu", [INNER, D], F32, kind="ExternalInput")
    wd_h = nc.dram_tensor("wd", [D, INNER], F32, kind="ExternalInput")
    eye_h = nc.dram_tensor("eye", [128, 128], F32, kind="ExternalInput")
    onec_h = nc.dram_tensor("onec", [128, 1], F32, kind="ExternalInput")
    oner_h = nc.dram_tensor("oner", [1, 128], F32, kind="ExternalInput")
    maskn_h = nc.dram_tensor("maskn", [128, 2560], BF16, kind="ExternalInput")
    gmask_h = nc.dram_tensor("gmask", [128, 16], BF16, kind="ExternalInput")
    y_h = nc.dram_tensor("y", [T, D], F32, kind="ExternalOutput")
    ctx_scr = nc.dram_tensor("ctx_scr", [16 * D, 128], F32)
    a_scr = nc.dram_tensor("a_scr", [T, D], BF16)
    x1_scr = nc.dram_tensor("x1_scr", [T, D], F32)
    tg_scr = nc.dram_tensor("tg_scr", [INNER, D], BF16)
    tu_scr = nc.dram_tensor("tu_scr", [INNER, D], BF16)
    td_scr = nc.dram_tensor("td_scr", [D, INNER], BF16)
    h_scr = nc.dram_tensor("h_scr", [INNER, T], F32)
    row_scr = nc.dram_tensor("row_scr", [4, T], F32)

    from contextlib import ExitStack
    with ExitStack() as stack:
        ent = stack.enter_context
        cst = ent(nc.sbuf_tensor("cst", [128, 384], F32))
        sc = ent(nc.sbuf_tensor("sc", [128, 128], F32))
        i0b = ent(nc.sbuf_tensor("i0b", [128, 1024], I32))
        pb = [ent(nc.psum_tensor(f"pb{i}", [128, 512], F32)) for i in range(8)]
        pb6, pb7 = pb[6], pb[7]
        S = Sched(nc)
        S.make_sems(ent)

        eye = cst[:, 0:128]
        oner = cst[0:1, 128:256]
        onec = cst[:, 256:257]
        tau_v = [cst[:, 260 + j:261 + j] for j in range(3)]
        mg127_v = cst[:, 263:264]

        st6a = sc[:, 48:60]
        st2a = sc[:, 60:62]
        lnwa = sc[:, 62:64]
        maxh_nat = sc[:, 32:48]
        maxr_nat = sc[:, 64:80]

        def mm(out_ap, lhs_ap, rhs_ap, reads, writes, start=True, stop=True):
            S.op("tensor", lambda t, o=out_ap, l=lhs_ap, r=rhs_ap,
                 st=start, sp=stop: t.matmul(o, l, r, start=st, stop=sp),
                 reads=reads, writes=writes)

        def tpose(out_ap, in_ap, reads, writes, pb0_=0, pn=128):
            ey = eye[pb0_:pb0_ + pn, pb0_:pb0_ + pn]
            S.op("tensor", lambda t, o=out_ap, i=in_ap, e2=ey: t.transpose(o, i, e2),
                 reads=tuple(reads) + ("cst",), writes=writes)

        def cop(eng, dst, src, reads, writes):
            if eng == "scalar":
                S.op("scalar", lambda s, d=dst, s2=src: s.copy(d, s2),
                     reads=reads, writes=writes)
            else:
                S.op("vector", lambda v, d=dst, s2=src: v.tensor_copy(d, s2),
                     reads=reads, writes=writes)

        EV = ("vector", "scalar")

        def ln_tile(xap, xkeys, P=128):
            S.op("vector", lambda v, x=xap, p2=P: v.bn_stats(st6a[0:p2, 0:6],
                                                             x[:, 0:512]),
                 reads=xkeys, writes=("st6",))
            S.op("vector", lambda v, x=xap, p2=P: v.bn_stats(st6a[0:p2, 6:12],
                                                             x[:, 512:1024]),
                 reads=xkeys, writes=("st6",))
            S.op("vector", lambda v, p2=P: v.bn_aggr(st2a[0:p2, :], st6a[0:p2, :]),
                 reads=("st6",), writes=("st2",))
            S.op("vector", lambda v, p2=P: v.tensor_single_scalar(
                lnwa[0:p2, 0:1], st2a[0:p2, 1:2], 1e-5, ALU.add),
                 reads=("st2",), writes=("lnw",))
            S.op("scalar", lambda s, p2=P: s.activation(lnwa[0:p2, 1:2],
                                                        lnwa[0:p2, 0:1], AF.Ln),
                 reads=("lnw",), writes=("lnw",))
            S.op("vector", lambda v, p2=P: v.tensor_single_scalar(
                lnwa[0:p2, 0:1], lnwa[0:p2, 1:2], -0.5, ALU.mult),
                 reads=("lnw",), writes=("lnw",))
            S.op("scalar", lambda s, p2=P: s.activation(lnwa[0:p2, 0:1],
                                                        lnwa[0:p2, 0:1], AF.Exp),
                 reads=("lnw",), writes=("lnw",))
            return st2a[:, 0:1], lnwa[:, 0:1]

        def wt_prep(af, wh, wbase, wst, pitch, scale, keypfx):
            for r in range(8):
                p = r % 2
                buf = af[:, wst + p * 1024: wst + (p + 1) * 1024]
                S.dma("sync", buf, wh.ap()[r * 128:(r + 1) * 128, :],
                      (f"{keypfx}{p}",), (f"{keypfx}{p}",))
                for c in range(8):
                    bk = pb[4 + 2 * p + c // 4]
                    tpose(bk[:, (c % 4) * 128:(c % 4) * 128 + 128],
                          buf[:, c * 128:(c + 1) * 128],
                          (f"{keypfx}{p}",), (f"pb{4 + 2 * p + c // 4}",))
                for half in range(2):
                    bki = 4 + 2 * p + half
                    dst = bass.AP(af, wbase + (half * 4) * 1024 + r * 128,
                                  [[pitch, 128], [1024, 4], [1, 128]])
                    src = pb[bki][:, 0:512].rearrange("p (g k) -> p g k", g=4)
                    eng = EV[(r * 2 + half) % 2]
                    if scale is None:
                        cop(eng, dst, src, (f"pb{bki}",), ("wf",))
                    elif eng == "scalar":
                        S.op("scalar", lambda s, d=dst, s2=src, sc2=scale:
                             s.mul(d, s2, sc2),
                             reads=(f"pb{bki}",), writes=("wf",))
                    else:
                        S.op("vector", lambda v, d=dst, s2=src, sc2=scale:
                             v.tensor_single_scalar(d, s2, sc2, ALU.mult),
                             reads=(f"pb{bki}",), writes=("wf",))

        wtiles = []
        for wi, (wh, scr3) in enumerate(
                ((wg_h, tg_scr), (wu_h, tu_scr), (wd_h, td_scr))):
            if wi < 2:
                for r in range(32):
                    wtiles.append((wi, wh.ap()[r * 128:(r + 1) * 128, :],
                                   scr3.ap()[r * 128:(r + 1) * 128, :]))
            else:
                for r in range(8):
                    for q in range(4):
                        wtiles.append((wi,
                                       wh.ap()[r * 128:(r + 1) * 128,
                                               q * 1024:(q + 1) * 1024],
                                       scr3.ap()[r * 128:(r + 1) * 128,
                                                 q * 1024:(q + 1) * 1024]))
        absac = sc[:, 32:35]

        PITA = 44160
        with (
            nc.sbuf_tensor("afA", [128, PITA], F32) as afA,
            nc.sbuf_tensor("mk", [128, 2560], BF16) as mkT,
            nc.sbuf_tensor("gm", [128, 16], BF16) as gmT,
            nc.sbuf_tensor("afW1", [128, 2048], F32) as afW1,
        ):
            WQ, WK, WV = 0, 8192, 16384
            XNT = 24576
            QT0 = 26624
            KT0 = 28672
            VT0 = 31232
            SCB, PTB = 35328, 37888
            XT, XN = 39936, 41984
            KGT = 44032
            WST = SCB
            XG = PTB
            XGT = PTB + 1024

            S.dma("sync", cst[:, 0:128], eye_h.ap(), (), ("cst",))
            S.dma("sync", cst[0:1, 128:256], oner_h.ap(), (), ("cst",))
            S.dma("sync", cst[:, 256:257], onec_h.ap(), (), ("cst",))
            S.dma("sync", mkT[:, 0:2560], maskn_h.ap(), (), ("mk",))
            S.dma("sync", gmT[:, 0:16], gmask_h.ap(), (), ("mk",))
            S.op("vector", lambda v: v.memset(cst[:, 267:268], CMAG),
                 reads=(), writes=("cst",))
            S.op("vector", lambda v: v.memset(cst[:, 268:269], -CMAG),
                 reads=(), writes=("cst",))

            wt_prep(afA, wq_h, WQ, WST, PITA, 0.125, "wsa")
            wt_prep(afA, wk_h, WK, WST, PITA, None, "wsa")
            wt_prep(afA, wv_h, WV, WST, PITA, None, "wsa")

            S.dma("sync", afA[0:16, XG:XG + 1024], xg_h.ap(), (), ("xg",))
            mean, rstd = ln_tile(afA[0:16, XG:XG + 1024], ("xg",), P=16)
            xgn = afA[0:16, XN:XN + 1024]
            S.op("vector", lambda v, m=mean, r=rstd: v.tensor_scalar(
                xgn, afA[0:16, XG:XG + 1024], m[0:16, :], r[0:16, :],
                ALU.subtract, ALU.mult),
                 reads=("xg", "st2", "lnw"), writes=("xn",))
            for c in range(8):
                tpose(pb6[0:128, c * 16:(c + 1) * 16],
                      afA[0:16, XN + c * 128:XN + (c + 1) * 128], ("xn",), ("pb6",),
                      pn=16)
            S.op("vector", lambda v: v.tensor_copy(afA[:, XGT:XGT + 128],
                                                   pb6[:, 0:128]),
                 reads=("pb6",), writes=("xgt",))

            def xgt_c(c):
                return afA[:, XGT + c * 16:XGT + (c + 1) * 16]

            for n in range(2):
                for c in range(8):
                    mm(pb[n][0:16, 0:512], xgt_c(c),
                       afA[:, WK + c * 1024 + n * 512:WK + c * 1024 + (n + 1) * 512],
                       ("xgt", "wf"), (f"pb{n}",), start=(c == 0), stop=(c == 7))
            for n in range(2):
                S.op("vector", lambda v, n2=n: v.tensor_copy(
                    afA[0:16, XG + n2 * 512:XG + (n2 + 1) * 512],
                    pb[n2][0:16, 0:512]),
                     reads=(f"pb{n}",), writes=("xg",))
            for m in range(8):
                tpose(pb6[0:128, m * 16:(m + 1) * 16],
                      afA[0:16, XG + m * 128:XG + (m + 1) * 128], ("xg",), ("pb6",),
                      pn=16)
            S.op("vector", lambda v: v.tensor_copy(afA[:, KGT:KGT + 128],
                                                   pb6[:, 0:128]),
                 reads=("pb6",), writes=("kgt",))
            for n in range(2):
                for c in range(8):
                    mm(pb[2 + n][0:16, 0:512], xgt_c(c),
                       afA[:, WV + c * 1024 + n * 512:WV + c * 1024 + (n + 1) * 512],
                       ("xgt", "wf"), (f"pb{2 + n}",), start=(c == 0), stop=(c == 7))
            for n in range(2):
                S.op("vector", lambda v, n2=n: v.tensor_copy(
                    afA[0:16, XG + n2 * 512:XG + (n2 + 1) * 512],
                    pb[2 + n2][0:16, 0:512]),
                     reads=(f"pb{2 + n}",), writes=("xg",))
            for tg in range(4):
                S.dma("gpsimd", afA[64:80, VT0 + tg * 1024:VT0 + (tg + 1) * 1024],
                      afA[0:16, XG:XG + 1024], ("xg",), ("vg",))
            for m in range(8):
                for tg in range(4):
                    cop(EV[(m + tg) % 2],
                        afA[:, KT0 + m * 320 + tg * 80 + 64:
                            KT0 + m * 320 + tg * 80 + 80],
                        afA[:, KGT + m * 16:KGT + (m + 1) * 16],
                        ("kgt",), ("ktg",))
            for wi in range(3):
                S.op("vector", lambda v, w2=wi: v.memset(absac[:, w2:w2 + 1], 0.0),
                     reads=(), writes=(f"aa{wi}",))
            S.fence()

            def w_absum_tiles(lo, hi):
                for t3 in range(lo, hi):
                    wi, src_ap, _ = wtiles[t3]
                    p = t3 % 2
                    buf = afW1[:, p * 1024:(p + 1) * 1024]
                    S.dma("sync", buf, src_ap, (f"wpa{p}",), (f"wpa{p}",))
                    red = sc[:, 35 + p:36 + p]
                    S.op("vector", lambda v, b=buf, rd=red: v.reduce_sum(
                        rd, b, axis=AX, apply_absolute_value=True),
                         reads=(f"wpa{p}",), writes=(f"red{p}",))
                    S.op("vector", lambda v, rd=red, w2=wi: v.tensor_tensor(
                        absac[:, w2:w2 + 1], absac[:, w2:w2 + 1], rd, ALU.add),
                         reads=(f"red{p}", f"aa{wi}"), writes=(f"aa{wi}",))

            def ln_prefetch(tb2):
                for ti in range(2):
                    tok0 = tb2 * TB + ti * 128
                    xt = afA[:, XT + ti * 1024:XT + (ti + 1) * 1024]
                    xn = afA[:, XN + ti * 1024:XN + (ti + 1) * 1024]
                    S.dma("sync", xt, x_h.ap()[tok0:tok0 + 128, :],
                          (f"xta{ti}",), (f"xta{ti}",))
                    mean, rstd = ln_tile(xt, (f"xta{ti}",))
                    S.op("vector", lambda v, x=xt, o=xn, m=mean, r=rstd:
                         v.tensor_scalar(o, x, m, r, ALU.subtract, ALU.mult),
                         reads=(f"xta{ti}", "st2", "lnw"), writes=(f"xn{ti}",))

            for tb in range(NTB):
                if tb == 0:
                    ln_prefetch(0)
                for ti in range(2):
                    xn = afA[:, XN + ti * 1024:XN + (ti + 1) * 1024]
                    for c in range(8):
                        tpose(pb[6 + c // 4][:, (c % 4) * 128:(c % 4) * 128 + 128],
                              xn[:, c * 128:(c + 1) * 128],
                              (f"xn{ti}",), (f"pb{6 + c // 4}",))
                    for half in range(2):
                        dst = bass.AP(afA, XNT + (half * 4) * 256 + ti * 128,
                                      [[PITA, 128], [256, 4], [1, 128]])
                        src = pb[6 + half][:, 0:512].rearrange(
                            "p (g k) -> p g k", g=4)
                        cop("scalar", dst, src, (f"pb{6 + half}",), ("xnt",))

                if tb + 1 < NTB:
                    ln_prefetch(tb + 1)
                for wbase, dkey in ((WQ, "qt"), (WK, "kt")):
                    for m in range(8):
                        bk = pb[m % 4]
                        bcol = (m // 4) * 256
                        for c in range(8):
                            mm(bk[:, bcol:bcol + 256],
                               afA[:, wbase + c * 1024 + m * 128:
                                   wbase + c * 1024 + (m + 1) * 128],
                               afA[:, XNT + c * 256:XNT + (c + 1) * 256],
                               ("wf", "xnt"), (f"pb{m % 4}",),
                               start=(c == 0), stop=(c == 7))
                    for b2 in range(4):
                        if wbase == WQ:
                            dst = bass.AP(afA, QT0 + b2 * 256,
                                          [[PITA, 128], [1024, 2], [1, 256]])
                            src = pb[b2][:, 0:512].rearrange(
                                "p (g k) -> p g k", g=2)
                        else:
                            dst = bass.AP(afA, KT0 + b2 * 320,
                                          [[PITA, 128], [1280, 2], [80, 4], [1, 64]])
                            src = pb[b2][:, 0:512].rearrange(
                                "p (g t k) -> p g t k", g=2, t=4)
                        cop("scalar", dst, src, (f"pb{b2}",), (dkey,))
                for tg in range(4):
                    for n in range(2):
                        bki = 4 + (tg * 2 + n) % 4
                        for c in range(8):
                            mm(pb[bki][0:64, 0:512],
                               afA[:, XNT + c * 256 + tg * 64:
                                   XNT + c * 256 + tg * 64 + 64],
                               afA[:, WV + c * 1024 + n * 512:
                                   WV + c * 1024 + (n + 1) * 512],
                               ("wf", "xnt"), (f"pb{bki}",),
                               start=(c == 0), stop=(c == 7))
                        cop(EV[n],
                            afA[0:64, VT0 + tg * 1024 + n * 512:
                                VT0 + tg * 1024 + (n + 1) * 512],
                            pb[bki][0:64, 0:512],
                            (f"pb{bki}",), ("vt",))

                def scbuf(pr):
                    return afA[pr * 64:(pr + 1) * 64, SCB:SCB + 2560]

                for pr in range(2):
                    scb = scbuf(pr)
                    pk = f"scb{pr}"
                    for h2 in range(H):
                        qrow = (h2 % 2) * 64
                        for t2 in range(2):
                            ci = 2 * h2 + t2
                            bk = pb[ci // 6]
                            cc = (ci % 6) * 80
                            tg = pr * 2 + t2
                            lhs = afA[qrow:qrow + 64,
                                      QT0 + (h2 // 2) * 256 + tg * 64:
                                      QT0 + (h2 // 2) * 256 + tg * 64 + 64]
                            rhs = afA[qrow:qrow + 64,
                                      KT0 + (h2 // 2) * 320 + tg * 80:
                                      KT0 + (h2 // 2) * 320 + tg * 80 + 80]
                            mm(bk[0:64, cc:cc + 80], lhs, rhs,
                               ("qt", "kt", "ktg"), (f"pb{ci // 6}",))
                    for b2 in range(6):
                        w3 = 480 if b2 < 5 else 160
                        S.op("vector", lambda v, b=pb[b2], off=b2 * 480, w4=w3,
                             s4=scb, p2=pr:
                             v.tensor_tensor(s4[:, off:off + w4], b[0:64, 0:w4],
                                             mkT[p2 * 64:p2 * 64 + 64,
                                                 off:off + w4],
                                             ALU.add),
                             reads=(f"pb{b2}", "mk"), writes=(pk,))
                    if tb == 0 and pr == 0:
                        glob3 = bass.AP(afA, SCB + 64, [[PITA, 64], [160, 16],
                                                        [1, 16]])
                        gm3 = bass.AP(gmT, 0, [[16, 64], [0, 16], [1, 16]])
                        S.op("vector", lambda v, g3=glob3, g4=gm3: v.tensor_tensor(
                            g3, g3, g4, ALU.add),
                             reads=(pk, "mk"), writes=(pk,))
                    scb3 = scb.rearrange("p (h k) -> p h k", k=80)
                    mx3 = sc[pr * 64:pr * 64 + 64, 0:32].rearrange(
                        "p (h k) -> p h k", k=1)
                    S.op("vector", lambda v, o=mx3, i=scb3:
                         v.reduce_max(o, i, axis=AX),
                         reads=(pk,), writes=(f"mx{pr}",))
                    mxb = bass.AP(sc, pr * 64 * 128, [[128, 64], [1, 32], [0, 80]])
                    S.op("vector", lambda v, m3=mxb, s3=scb3: v.tensor_tensor(
                        s3, s3, m3, ALU.subtract),
                         reads=(pk, f"mx{pr}"), writes=(pk,))
                    S.op("scalar", lambda s, s3=scb: s.activation(s3, s3, AF.Exp),
                         reads=(pk,), writes=(pk,))
                    rs2 = sc[pr * 64:pr * 64 + 64, 96:128]
                    rs3 = rs2.rearrange("p (h k) -> p h k", k=1)
                    S.op("vector", lambda v, o=rs3, i=scb3:
                         v.reduce_sum(o, i, axis=AX),
                         reads=(pk,), writes=(f"rs{pr}",))
                    S.op("vector", lambda v, r=rs2: v.reciprocal(r, r),
                         reads=(f"rs{pr}",), writes=(f"rs{pr}",))
                    rsb = bass.AP(sc, pr * 64 * 128 + 96,
                                  [[128, 64], [1, 32], [0, 80]])
                    S.op("vector", lambda v, r=rsb, s3=scb3: v.tensor_tensor(
                        s3, s3, r, ALU.mult),
                         reads=(pk, f"rs{pr}"), writes=(pk,))

                PTBANKS = (6, 7, 0, 1)
                for pr in range(2):
                    scb = scbuf(pr)
                    pk = f"scb{pr}"
                    for ci in range(32):
                        bk = pb[PTBANKS[ci // 8]]
                        tpose(bk[0:80, (ci % 8) * 64:(ci % 8) * 64 + 64],
                              scb[:, ci * 80:ci * 80 + 80],
                              (pk,), (f"pb{PTBANKS[ci // 8]}",),
                              pb0_=pr * 64, pn=64)
                    for g2 in range(4):
                        S.op("scalar", lambda s, b=pb[PTBANKS[g2]], off=g2 * 512:
                             s.copy(afA[0:80, PTB + off:PTB + off + 512],
                                    b[0:80, 0:512]),
                             reads=(f"pb{PTBANKS[g2]}",), writes=("ptb",))
                    for h2 in range(H):
                        for t2 in range(2):
                            ci = 2 * h2 + t2
                            bk = pb[2 + ci // 8]
                            tg = pr * 2 + t2
                            lhsl = afA[0:80,
                                       VT0 + tg * 1024 + h2 * 64:
                                       VT0 + tg * 1024 + h2 * 64 + 64]
                            rhsl = afA[0:80, PTB + ci * 64:PTB + ci * 64 + 64]
                            mm(bk[0:64, (ci % 8) * 64:(ci % 8) * 64 + 64],
                               lhsl, rhsl, ("vt", "vg", "ptb"),
                               (f"pb{2 + ci // 8}",))
                    for g2 in range(4):
                        cop(EV[g2 % 2],
                            scb[:, g2 * 512:g2 * 512 + 512],
                            pb[2 + g2][0:64, 0:512],
                            (f"pb{2 + g2}",), (pk,))
                    pair = tb * 2 + pr
                    src = bass.AP(afA, pr * 64 * PITA + SCB,
                                  [[PITA, 64], [128, 16], [1, 128]])
                    dst = bass.AP(ctx_scr, pair * 131072,
                                  [[128, 64], [8192, 16], [1, 128]])
                    S.dma("gpsimd", dst, src, (pk,), ("ctxw",))
                w_absum_tiles(tb * 12, (tb + 1) * 12)
            mm(pb6[0:1, 0:3], onec, absac[:, 0:3],
               ("aa0", "aa1", "aa2", "cst"), ("pb6",))
            S.op("vector", lambda v: v.tensor_copy(sc[0:1, 8:11], pb6[0:1, 0:3]),
                 reads=("pb6",), writes=("wm",))
            S.op("vector", lambda v: v.tensor_scalar(
                sc[0:1, 8:11], sc[0:1, 8:11], 1.0 / (INNER * D), 1e-5,
                ALU.mult, ALU.max),
                 reads=("wm",), writes=("wm",))
            S.op("vector", lambda v: v.tensor_single_scalar(
                sc[0:1, 11:14], sc[0:1, 8:11], 0.5, ALU.mult),
                 reads=("wm",), writes=("tau",))
            S.op("vector", lambda v: v.tensor_single_scalar(
                sc[0:1, 14:15], sc[0:1, 8:9], 1.0 / 254.0, ALU.mult),
                 reads=("wm",), writes=("tau",))
            S.op("vector", lambda v: v.reciprocal(sc[0:1, 17:18], sc[0:1, 8:9]),
                 reads=("wm",), writes=("mumg",))
            S.op("vector", lambda v: v.tensor_tensor(
                sc[0:1, 17:18], sc[0:1, 17:18], sc[0:1, 9:10], ALU.mult),
                 reads=("mumg", "wm"), writes=("mumg",))
            S.op("vector", lambda v: v.tensor_single_scalar(
                sc[0:1, 16:17], sc[0:1, 10:11], 1.0 / 254.0, ALU.mult),
                 reads=("wm",), writes=("md127",))
            mm(pb7[0:128, 0:4], oner, sc[0:1, 11:15], ("tau", "cst"), ("pb7",))
            S.op("vector", lambda v: v.tensor_copy(cst[:, 260:264], pb7[:, 0:4]),
                 reads=("pb7",), writes=("cst",))
            S.op("vector", lambda v: v.tensor_single_scalar(
                cst[:, 264:267], cst[:, 260:263], -1.0, ALU.mult),
                 reads=("cst",), writes=("cst",))
            S.fence()

        PITB = 34880
        with (
            nc.sbuf_tensor("afB", [128, PITB], F32) as afB,
            nc.sbuf_tensor("abB", [128, 12288], BF16) as abB,
            nc.sbuf_tensor("afW2", [128, 2048], F32) as afW2,
            nc.sbuf_tensor("abW", [128, 6144], BF16) as abW,
        ):
            WO, WST2 = 0, 8192
            CTXF, XT2, X1B, XN2, DQ = 10240, 14336, 18432, 22528, 26624
            GB2 = 30720
            LNS = 34816
            MD, ATI = 0, 8192


            def w_tern_tiles(lo, hi):
                for t3 in range(lo, hi):
                    wi, src_ap, dst_ap = wtiles[t3]
                    p = t3 % 2
                    buf = afW2[:, p * 1024:(p + 1) * 1024]
                    S.dma(("sync", "scalar")[p], buf, src_ap,
                          (f"wpa{p}",), (f"wpa{p}",))
                    s1 = abW[:, p * 1024:(p + 1) * 1024]
                    s2b = abW[:, 2048 + p * 1024:2048 + (p + 1) * 1024]
                    tn = abW[:, 4096 + p * 1024:4096 + (p + 1) * 1024]
                    S.op("scalar", lambda s, o=s1, b=buf, w2=wi: s.sign(
                        o, b, bias=cst[:, 264 + w2:265 + w2]),
                         reads=(f"wpa{p}", "cst"), writes=(f"tp{p}",))
                    S.op("scalar", lambda s, o=s2b, b=buf, w2=wi: s.sign(
                        o, b, bias=cst[:, 260 + w2:261 + w2]),
                         reads=(f"wpa{p}", "cst"), writes=(f"tq{p}",))
                    S.op("gpsimd", lambda g, o=tn, a=s1, b=s2b: g.tensor_tensor(
                        o, a, b, ALU.add),
                         reads=(f"tp{p}", f"tq{p}"), writes=(f"tn{p}",))
                    S.dma("sync", dst_ap, tn, (f"tn{p}",), ("ts",))

            wt_prep(afB, wo_h, WO, WST2, PITB, None, "wsb")

            def btile(ti):
                p = ti % 4
                return dict(
                    p=p, tok0=ti * 128,
                    cf=afB[:, CTXF + p * 1024:CTXF + (p + 1) * 1024],
                    xt=afB[:, XT2 + p * 1024:XT2 + (p + 1) * 1024],
                    x1=afB[:, X1B + p * 1024:X1B + (p + 1) * 1024],
                    xn2=afB[:, XN2 + p * 1024:XN2 + (p + 1) * 1024],
                    ic=afB[:, DQ + p * 1024:DQ + (p + 1) * 1024],
                    gq=afB[:, GB2 + p * 1024:GB2 + (p + 1) * 1024],
                    mp=abB[:, MD + p * 1024:MD + (p + 1) * 1024],
                    mdn=abB[:, MD + 4096 + p * 1024:MD + 4096 + (p + 1) * 1024],
                    at=abB[:, ATI + p * 1024:ATI + (p + 1) * 1024],
                    st6=afB[:, LNS + p * 16:LNS + p * 16 + 12],
                    st2=afB[:, LNS + p * 16 + 12:LNS + p * 16 + 14],
                    lnw=afB[:, LNS + p * 16 + 14:LNS + p * 16 + 16],
                    mr=maxr_nat[:, ti:ti + 1], sx=sc[:, 80 + ti:81 + ti])

            for g in range(4):
                tis = range(g * 4, g * 4 + 4)
                bt = {ti: btile(ti) for ti in tis}
                w_tern_tiles(g * 24, (g + 1) * 24)
                for ti in tis:
                    b = bt[ti]
                    p = b["p"]
                    csrc = bass.AP(ctx_scr, ti * 131072,
                                   [[128, 128], [16384, 8], [1, 128]])
                    cdst = bass.AP(afB, CTXF + p * 1024,
                                   [[PITB, 128], [128, 8], [1, 128]])
                    S.dma("sync", cdst, csrc, (f"cf{p}",), (f"cf{p}",))
                    S.dma("sync", b["xt"], x_h.ap()[b["tok0"]:b["tok0"] + 128, :],
                          (f"xtb{p}",), (f"xtb{p}",))
                for ti in tis:
                    b = bt[ti]
                    p = b["p"]
                    wb0 = 4 + 2 * (ti % 2)
                    for n in range(2):
                        bk = pb[wb0 + n]
                        for c in range(8):
                            mm(bk[:, 0:512], b["cf"][:, c * 128:(c + 1) * 128],
                               afB[:, WO + c * 1024 + n * 512:
                                   WO + c * 1024 + (n + 1) * 512],
                               (f"cf{p}", "wf"), (f"pb{wb0 + n}",),
                               start=(c == 0), stop=(c == 7))
                    for n in range(2):
                        S.op("vector", lambda v, b2=pb[wb0 + n], n2=n,
                             x2=b["x1"], xx=b["xt"]:
                             v.tensor_tensor(x2[:, n2 * 512:(n2 + 1) * 512],
                                             b2[:, 0:512],
                                             xx[:, n2 * 512:(n2 + 1) * 512],
                                             ALU.add),
                             reads=(f"pb{wb0 + n}", f"xtb{p}"), writes=(f"x1{p}",))
                    S.dma("scalar", x1_scr.ap()[b["tok0"]:b["tok0"] + 128, :],
                          b["x1"], (f"x1{p}",), ("x1s",))
                for ti in tis:
                    b = bt[ti]
                    p = b["p"]
                    S.op("vector", lambda v, x=b["x1"], o=b["st6"]: v.bn_stats(
                        o[:, 0:6], x[:, 0:512]),
                         reads=(f"x1{p}",), writes=(f"st6{p}",))
                    S.op("vector", lambda v, x=b["x1"], o=b["st6"]: v.bn_stats(
                        o[:, 6:12], x[:, 512:1024]),
                         reads=(f"x1{p}",), writes=(f"st6{p}",))
                    S.op("vector", lambda v, o=b["st2"], i=b["st6"]: v.bn_aggr(o, i),
                         reads=(f"st6{p}",), writes=(f"st2{p}",))
                    S.op("vector", lambda v, o=b["lnw"], i=b["st2"]:
                         v.tensor_single_scalar(o[:, 0:1], i[:, 1:2], 1e-5, ALU.add),
                         reads=(f"st2{p}",), writes=(f"lnw{p}",))
                    S.op("vector", lambda v, o=b["lnw"]:
                         v.reciprocal(o[:, 1:2], o[:, 0:1]),
                         reads=(f"lnw{p}",), writes=(f"lnw{p}",))
                for ti in tis:
                    b = bt[ti]
                    p = b["p"]
                    S.op("scalar", lambda s, o=b["lnw"]: s.sqrt(o[:, 0:1], o[:, 1:2]),
                         reads=(f"lnw{p}",), writes=(f"lnw{p}",))
                for ti in tis:
                    b = bt[ti]
                    p = b["p"]
                    S.op("vector", lambda v, x2=b["x1"], o=b["xn2"], m=b["st2"],
                         r=b["lnw"]:
                         v.tensor_scalar(o, x2, m[:, 0:1], r[:, 0:1],
                                         ALU.subtract, ALU.mult),
                         reads=(f"x1{p}", f"st2{p}", f"lnw{p}"), writes=(f"xn2{p}",))
                    S.op("vector", lambda v, m3=b["mr"], x2=b["xn2"]: v.reduce_max(
                        m3, x2, axis=AX, apply_absolute_value=True),
                         reads=(f"xn2{p}",), writes=("maxr",))
                    S.op("vector", lambda v, m3=b["mr"]: v.tensor_single_scalar(
                        m3, m3, 1e-5, ALU.max),
                         reads=("maxr",), writes=("maxr",))
                    S.op("vector", lambda v, m3=b["mr"], s3=b["sx"]:
                         v.reciprocal(s3, m3),
                         reads=("maxr",), writes=("sx",))
                    S.op("vector", lambda v, s3=b["sx"]: v.tensor_single_scalar(
                        s3, s3, 127.0, ALU.mult),
                         reads=("sx",), writes=("sx",))
                i0f = i0b[:, 0:1024].bitcast(F32)
                for ti in tis:
                    b = bt[ti]
                    p = b["p"]
                    S.op("scalar", lambda s, o=b["gq"], x2=b["xn2"], s3=b["sx"]:
                         s.mul(o, x2, s3),
                         reads=(f"xn2{p}", "sx"), writes=(f"gq{p}",))
                for ti in tis:
                    b = bt[ti]
                    p = b["p"]
                    S.op("scalar", lambda s, o=i0f, i=b["gq"]: s.add(
                        o, i, cst[:, 267:268]),
                         reads=(f"gq{p}", "cst", "i0"), writes=("i0",))
                    S.op("scalar", lambda s, o=b["ic"], i=i0f: s.add(
                        o, i, cst[:, 268:269]),
                         reads=("i0", "cst"), writes=(f"ic{p}",))
                for ti in tis:
                    b = bt[ti]
                    p = b["p"]
                    S.op(PAR[p % 2], lambda v, o=b["xt"], i=b["ic"], g2=b["gq"]:
                         v.tensor_sub(o, g2, i),
                         reads=(f"ic{p}", f"gq{p}"), writes=(f"xtb{p}",))
                for ti in tis:
                    b = bt[ti]
                    p = b["p"]
                    S.op("vector", lambda v, o=b["mp"], i=b["xt"]:
                         v.tensor_single_scalar(o, i, 0.5, ALU.is_gt),
                         reads=(f"xtb{p}",), writes=(f"mp{p}",))
                    S.op("vector", lambda v, o=b["mdn"], i=b["xt"], m3=b["mp"]:
                         v.scalar_tensor_tensor(
                             o, i, -0.5, m3, ALU.is_lt, ALU.subtract),
                         reads=(f"xtb{p}", f"mp{p}"), writes=(f"md{p}",))
                    S.op("vector", lambda v, o=b["at"], a=b["ic"], b2=b["mdn"]:
                         v.tensor_sub(o, a, b2),
                         reads=(f"ic{p}", f"md{p}"), writes=(f"at{p}",))
                    S.dma("gpsimd", a_scr.ap()[b["tok0"]:b["tok0"] + 128, :],
                          b["at"], (f"at{p}",), ("as",))
            S.fence()

        with (
            nc.sbuf_tensor("abF", [128, 49152], BF16) as abF,
            nc.sbuf_tensor("afF", [128, 20736], F32) as afF,
        ):
            WG0, WU0, ATL = 0, 16384, 32768
            GUH, RMX0, RMX1 = 0, 8192, 10240
            ABG, ABU = 12288, 14336
            RA, RG, RU = 16384, 16512, 18560

            for c in range(8):
                S.dma("sync", abF[:, ATL + c * 2048:ATL + (c + 1) * 2048],
                      a_scr.ap()[0:T, c * 128:(c + 1) * 128],
                      (), ("atb",), transpose=True)
            S.op("vector", lambda v: v.tensor_scalar_mul(sc[:, 0:16], maxr_nat,
                                                         mg127_v),
                 reads=("cst",), writes=("ag",))
            tpose(pb6[0:16, 0:128], sc[:, 0:16], ("ag",), ("pb6",))
            S.op("vector", lambda v: v.tensor_copy(afF[0:16, RA:RA + 128],
                                                   pb6[0:16, 0:128]),
                 reads=("pb6",), writes=("ra",))
            S.dma("gpsimd",
                  row_scr.ap()[0:1, 0:T].rearrange("a (b c) -> (a b) c", c=128),
                  afF[0:16, RA:RA + 128], ("ra",), ("rowh",))
            S.dma("sync", afF[0:1, RG:RG + T], row_scr.ap()[0:1, 0:T],
                  ("rowh",), ("rg",))
            S.op("vector", lambda v: v.tensor_scalar_mul(
                afF[0:1, RU:RU + T], afF[0:1, RG:RG + T], sc[0:1, 17:18]),
                 reads=("rg",), writes=("ru",))
            for src_off, dst, rk in ((RG, ABG, "rg"), (RU, ABU, "ru")):
                for q in range(4):
                    bk = pb[2 + q % 2]
                    mm(bk[:, 0:512], oner,
                       afF[0:1, src_off + q * 512:src_off + (q + 1) * 512],
                       (rk, "cst"), (f"pb{2 + q % 2}",))
                    cop(EV[q % 2], afF[:, dst + q * 512:dst + (q + 1) * 512],
                        bk[:, 0:512], (f"pb{2 + q % 2}",), ("ab",))
            S.op("vector", lambda v: v.memset(afF[:, RMX0:RMX0 + 2048], 0.0),
                 reads=(), writes=("rm0",))
            S.op("gpsimd", lambda g: g.memset(afF[:, RMX1:RMX1 + 2048], 0.0),
                 reads=(), writes=("rm1",))

            for half in range(2):
                for wi, (scr3, dst) in enumerate(((tg_scr, WG0), (tu_scr, WU0))):
                    for c in range(8):
                        S.dma("scalar", abF[:, dst + c * 2048:dst + (c + 1) * 2048],
                              scr3.ap()[half * 2048:(half + 1) * 2048,
                                        c * 128:(c + 1) * 128],
                              ("wgw" if wi == 0 else "wuw",),
                              ("wgw" if wi == 0 else "wuw",), transpose=True)
                for kc in range(16):
                    for tb2 in range(4):
                        q = (kc * 4 + tb2) % 4
                        bg, bu = pb[q], pb[4 + q]
                        for c in range(8):
                            mm(bg[:, 0:512],
                               abF[:, WG0 + c * 2048 + kc * 128:
                                   WG0 + c * 2048 + (kc + 1) * 128],
                               abF[:, ATL + c * 2048 + tb2 * 512:
                                   ATL + c * 2048 + (tb2 + 1) * 512],
                               ("wgw", "atb"), (f"pb{q}",),
                               start=(c == 0), stop=(c == 7))
                            mm(bu[:, 0:512],
                               abF[:, WU0 + c * 2048 + kc * 128:
                                   WU0 + c * 2048 + (kc + 1) * 128],
                               abF[:, ATL + c * 2048 + tb2 * 512:
                                   ATL + c * 2048 + (tb2 + 1) * 512],
                               ("wuw", "atb"), (f"pb{4 + q}",),
                               start=(c == 0), stop=(c == 7))
                        gb = afF[:, GUH + q * 2048:GUH + q * 2048 + 512]
                        gs = afF[:, GUH + q * 2048 + 512:GUH + q * 2048 + 1024]
                        ub = afF[:, GUH + q * 2048 + 1024:GUH + q * 2048 + 1536]
                        hb = afF[:, GUH + q * 2048 + 1536:GUH + q * 2048 + 2048]
                        S.op("vector", lambda v, o=gb, b=bg, t4=tb2: v.tensor_tensor(
                            o, b[:, 0:512],
                            afF[:, ABG + t4 * 512:ABG + (t4 + 1) * 512], ALU.mult),
                             reads=(f"pb{q}", "ab"), writes=(f"gb{q}",))
                        S.op("scalar", lambda s, o=gs, i=gb:
                             s.activation(o, i, AF.Silu),
                             reads=(f"gb{q}",), writes=(f"gs{q}",))
                        S.op("vector", lambda v, o=ub, b=bu, t4=tb2: v.tensor_tensor(
                            o, b[:, 0:512],
                            afF[:, ABU + t4 * 512:ABU + (t4 + 1) * 512], ALU.mult),
                             reads=(f"pb{4 + q}", "ab"), writes=(f"ub{q}",))
                        S.op("gpsimd", lambda g, o=hb, a=gs, b=ub: g.tensor_tensor(
                            o, a, b, ALU.mult),
                             reads=(f"gs{q}", f"ub{q}"), writes=(f"hb{q}",))
                        rmx0 = afF[:, RMX0 + tb2 * 512:RMX0 + (tb2 + 1) * 512]
                        rmx1 = afF[:, RMX1 + tb2 * 512:RMX1 + (tb2 + 1) * 512]
                        S.op("vector", lambda v, o=rmx0, b=hb: v.tensor_tensor(
                            o, o, b, ALU.max),
                             reads=(f"hb{q}", "rm0"), writes=("rm0",))
                        S.op("vector", lambda v, o=rmx1, b=hb: v.tensor_tensor(
                            o, o, b, ALU.min),
                             reads=(f"hb{q}", "rm1"), writes=("rm1",))
                        kg = half * 16 + kc
                        S.dma("sync",
                              h_scr.ap()[kg * 128:(kg + 1) * 128,
                                         tb2 * 512:(tb2 + 1) * 512],
                              hb, (f"hb{q}",), ("hs",))
            S.op("vector", lambda v: v.tensor_single_scalar(
                afF[:, RMX1:RMX1 + 2048], afF[:, RMX1:RMX1 + 2048], -1.0, ALU.mult),
                 reads=("rm1",), writes=("rm1",))
            S.op("vector", lambda v: v.tensor_tensor(
                afF[:, RMX0:RMX0 + 2048], afF[:, RMX0:RMX0 + 2048],
                afF[:, RMX1:RMX1 + 2048], ALU.max),
                 reads=("rm0", "rm1"), writes=("rm0",))
            for sl in range(16):
                bk = pb[sl % 2]
                tpose(bk[:, 0:128], afF[:, RMX0 + sl * 128:RMX0 + (sl + 1) * 128],
                      ("rm0",), (f"pb{sl % 2}",))
                S.op("vector", lambda v, b=bk, s3=sl: v.reduce_max(
                    maxh_nat[:, s3:s3 + 1], b[:, 0:128], axis=AX),
                     reads=(f"pb{sl % 2}",), writes=("maxh",))
            S.op("vector", lambda v: v.tensor_single_scalar(
                maxh_nat, maxh_nat, 1e-5, ALU.max),
                 reads=("maxh",), writes=("maxh",))
            tpose(pb6[0:16, 0:128], maxh_nat, ("maxh",), ("pb6",))
            S.op("vector", lambda v: v.tensor_copy(afF[0:16, RA:RA + 128],
                                                   pb6[0:16, 0:128]),
                 reads=("pb6",), writes=("ra",))
            S.dma("gpsimd",
                  row_scr.ap()[2:3, 0:T].rearrange("a (b c) -> (a b) c", c=128),
                  afF[0:16, RA:RA + 128], ("ra",), ("rowh2",))
            S.fence()

        with (
            nc.sbuf_tensor("abG", [128, 38912], BF16) as abG,
            nc.sbuf_tensor("afG", [128, 24704], F32) as afG,
        ):
            WD0, HQ, MD2 = 0, 32768, 34816
            HT, G3, IC2, DD2 = 0, 2048, 4096, 6144
            SYB, BDB, FFT = 8192, 10240, 12288
            X1T, OUT = 16384, 18432
            ROWW, RMH, RROW = 20480, 20608, 22656

            for kc in range(32):
                S.dma(("scalar", "sync")[kc % 2],
                      abG[:, WD0 + kc * 1024:WD0 + (kc + 1) * 1024],
                      td_scr.ap()[0:D, kc * 128:(kc + 1) * 128],
                      (), ("wdt",), transpose=True)
            S.dma("sync", afG[0:1, RMH:RMH + T], row_scr.ap()[2:3, 0:T],
                  (), ("rmh",))
            S.op("vector", lambda v: v.reciprocal(afG[0:1, RROW:RROW + T],
                                                  afG[0:1, RMH:RMH + T]),
                 reads=("rmh",), writes=("rrow",))
            S.op("vector", lambda v: v.tensor_single_scalar(
                afG[0:1, RROW:RROW + T], afG[0:1, RROW:RROW + T], 127.0, ALU.mult),
                 reads=("rrow",), writes=("rrow",))
            for q in range(4):
                bk = pb[2 + q % 2]
                mm(bk[:, 0:512], oner, afG[0:1, RROW + q * 512:RROW + (q + 1) * 512],
                   ("rrow", "cst"), (f"pb{2 + q % 2}",))
                cop(EV[q % 2], afG[:, SYB + q * 512:SYB + (q + 1) * 512],
                    bk[:, 0:512], (f"pb{2 + q % 2}",), ("syb",))
            S.op("vector", lambda v: v.tensor_scalar_mul(
                afG[0:1, RROW:RROW + T], afG[0:1, RMH:RMH + T], sc[0:1, 16:17]),
                 reads=("rmh",), writes=("rrow",))
            for q in range(4):
                bk = pb[2 + q % 2]
                mm(bk[:, 0:512], oner, afG[0:1, RROW + q * 512:RROW + (q + 1) * 512],
                   ("rrow", "cst"), (f"pb{2 + q % 2}",))
                cop(EV[q % 2], afG[:, BDB + q * 512:BDB + (q + 1) * 512],
                    bk[:, 0:512], (f"pb{2 + q % 2}",), ("bdb",))

            for tb2 in range(4):
                for kc in range(32):
                    p = kc % 4
                    ht = afG[:, HT + p * 512:HT + (p + 1) * 512]
                    S.dma("sync", ht,
                          h_scr.ap()[kc * 128:(kc + 1) * 128,
                                     tb2 * 512:(tb2 + 1) * 512],
                          (f"ht{p}",), (f"ht{p}",))
                    g3 = afG[:, G3 + p * 512:G3 + (p + 1) * 512]
                    ic = afG[:, IC2 + p * 512:IC2 + (p + 1) * 512]
                    dd = afG[:, DD2 + p * 512:DD2 + (p + 1) * 512]
                    i0f = i0b[:, 0:512].bitcast(F32)
                    mp = abG[:, MD2 + p * 512:MD2 + (p + 1) * 512]
                    mdn = abG[:, MD2 + 1024 + p * 512:MD2 + 1024 + (p + 1) * 512]
                    hq = abG[:, HQ + p * 512:HQ + (p + 1) * 512]
                    S.op("gpsimd", lambda g, o=g3, a=ht, t4=tb2: g.tensor_tensor(
                        o, a, afG[:, SYB + t4 * 512:SYB + (t4 + 1) * 512], ALU.mult),
                         reads=(f"ht{p}", "syb"), writes=(f"g3{p}",))
                    S.op("scalar", lambda s, o=i0f, i=g3: s.add(
                        o, i, cst[:, 267:268]),
                         reads=(f"g3{p}", "cst", "i0"), writes=("i0",))
                    S.op("scalar", lambda s, o=ic, i=i0f: s.add(
                        o, i, cst[:, 268:269]),
                         reads=("i0", "cst"), writes=(f"ic{p}",))
                    S.op("gpsimd", lambda g, o=dd, a=ic, b=g3: g.tensor_sub(o, b, a),
                         reads=(f"ic{p}", f"g3{p}"), writes=(f"dd{p}",))
                    S.op("vector", lambda v, o=mp, i=dd: v.tensor_single_scalar(
                        o, i, 0.5, ALU.is_gt),
                         reads=(f"dd{p}",), writes=(f"mp{p}",))
                    S.op("vector", lambda v, o=mdn, i=dd, m3=mp:
                         v.scalar_tensor_tensor(
                             o, i, -0.5, m3, ALU.is_lt, ALU.subtract),
                         reads=(f"dd{p}", f"mp{p}",), writes=(f"md{p}",))
                    S.op("vector", lambda v, o=hq, a=ic, b=mdn: v.tensor_sub(
                        o, a, b),
                         reads=(f"ic{p}", f"md{p}"), writes=(f"hq{p}",))
                    for m2 in range(8):
                        mm(pb[m2][:, 0:512],
                           abG[:, WD0 + kc * 1024 + m2 * 128:
                               WD0 + kc * 1024 + (m2 + 1) * 128],
                           hq, ("wdt", f"hq{p}"), (f"pb{m2}",),
                           start=(kc == 0), stop=(kc == 31))
                for m2 in range(8):
                    S.op("vector", lambda v, b=pb[m2], m3=m2, t4=tb2:
                         v.tensor_tensor(
                             afG[:, FFT + m3 * 512:FFT + (m3 + 1) * 512],
                             b[:, 0:512],
                             afG[:, BDB + t4 * 512:BDB + (t4 + 1) * 512], ALU.mult),
                         reads=(f"pb{m2}", "bdb"), writes=("fft",))
                for sl in range(4):
                    tok0 = tb2 * 512 + sl * 128
                    sp = sl % 2
                    x1t = afG[:, X1T + sp * 1024:X1T + (sp + 1) * 1024]
                    S.dma("sync", x1t, x1_scr.ap()[tok0:tok0 + 128, :],
                          (f"x1t{sp}", "x1s"), (f"x1t{sp}",))
                    for m2 in range(8):
                        tpose(pb[m2 // 4][:, (m2 % 4) * 128:(m2 % 4) * 128 + 128],
                              afG[:, FFT + m2 * 512 + sl * 128:
                                  FFT + m2 * 512 + sl * 128 + 128],
                              ("fft",), (f"pb{m2 // 4}",))
                    ot = afG[:, OUT + sp * 1024:OUT + (sp + 1) * 1024]
                    for b2 in range(2):
                        S.op("vector", lambda v, b=pb[b2], b3=b2, o=ot, x2=x1t:
                             v.tensor_tensor(o[:, b3 * 512:(b3 + 1) * 512],
                                             b[:, 0:512],
                                             x2[:, b3 * 512:(b3 + 1) * 512], ALU.add),
                             reads=(f"pb{b2}", f"x1t{sp}"), writes=(f"ot{sp}",))
                    S.dma("scalar", y_h.ap()[tok0:tok0 + 128, :], ot,
                          (f"ot{sp}",), ("ys",))
            S.fence()
        S.finalize()

    return nc


_NC_CACHE = None


def mk_mask(first_pair):
    m = np.zeros((64, 2560), np.float32)
    j = np.arange(64)
    for h in range(H):
        for t2 in range(2):
            ci = 2 * h + t2
            for q in range(64):
                m[q, ci * 80:ci * 80 + 64][j > q] = NEG
            if first_pair and t2 == 0:
                m[:, ci * 80 + 64:ci * 80 + 80] = NEG
    return m.astype(ml_dtypes.bfloat16)


def kernel(L0_attn_hidden_states, L0_attn_Wq, L0_attn_Wk, L0_attn_Wv,
           L0_attn_Wo, L0_ffn_w_gate, L0_ffn_w_up, L0_ffn_w_down):
    global _NC_CACHE
    x = np.ascontiguousarray(L0_attn_hidden_states, dtype=np.float32)
    B, Sq, Dm = x.shape
    xf = x.reshape(B * Sq, Dm)

    eye = np.eye(128, dtype=np.float32)
    onec = np.ones((128, 1), np.float32)
    oner = np.ones((1, 128), np.float32)

    mask_n = np.vstack([mk_mask(False)] * 2)

    if _NC_CACHE is None:
        _NC_CACHE = build_nc()
    nc = _NC_CACHE

    in_maps = []
    for c in range(8):
        b = c // 2
        sh = c % 2
        in_maps.append({
            "x": np.ascontiguousarray(xf[c * T:(c + 1) * T]),
            "xg": np.ascontiguousarray(x[b, 0:16, :]),
            "wq": np.ascontiguousarray(L0_attn_Wq, np.float32),
            "wk": np.ascontiguousarray(L0_attn_Wk, np.float32),
            "wv": np.ascontiguousarray(L0_attn_Wv, np.float32),
            "wo": np.ascontiguousarray(L0_attn_Wo, np.float32),
            "wg": np.ascontiguousarray(L0_ffn_w_gate, np.float32),
            "wu": np.ascontiguousarray(L0_ffn_w_up, np.float32),
            "wd": np.ascontiguousarray(L0_ffn_w_down, np.float32),
            "eye": eye, "onec": onec, "oner": oner,
            "maskn": mask_n,
            "gmask": np.full((128, 16), NEG if sh == 0 else 0.0,
                             ml_dtypes.bfloat16),
        })
    res = run_bass_kernel_spmd(nc, in_maps, core_ids=list(range(8)))
    out = np.concatenate([res.results[c]["y"] for c in range(8)], axis=0)
    return out.reshape(B, Sq, Dm)


# version 26
# speedup vs baseline: 1.0918x; 1.0204x over previous
import numpy as np
import ml_dtypes
import concourse.bass as bass
import concourse.mybir as mybir
from concourse.bass_utils import run_bass_kernel_spmd

F32 = mybir.dt.float32
BF16 = mybir.dt.bfloat16
I32 = mybir.dt.int32
AX = mybir.AxisListType.X
AF = mybir.ActivationFunctionType
ALU = mybir.AluOpType

T = 2048
D = 1024
H = 16
INNER = 4096
TB = 256
NTB = T // TB
CMAG = float(np.float32(3 * 2 ** 22))
NEG = -1.0e30
PAR = ("vector", "gpsimd")



class Sched:

    ENGINES = ("sync", "scalar", "vector", "tensor", "gpsimd")
    CENG = ("scalar", "vector", "tensor", "gpsimd")
    DMAQ = ("sync", "scalar", "gpsimd")
    NLANE = 6

    def __init__(self, nc):
        self.nc = nc
        self.ops = []
        self.last_w = {}
        self.last_r = {}
        self.emitted = 0
        self.ccnt = {e: 0 for e in self.CENG}
        self.dtot = {e: 0 for e in self.DMAQ}
        self.dlcnt = {e: [0] * self.NLANE for e in self.DMAQ}
        self.waited = {e: {} for e in self.ENGINES}
        self.drained = {e: -1 for e in self.ENGINES}
        self.engpos = {e: 0 for e in self.ENGINES}
        self.snap = (dict(self.ccnt),
                     {e: list(v) for e, v in self.dlcnt.items()})
        self.csem = {}
        self.dsem = {}

    def make_sems(self, ent):
        for e in self.CENG:
            self.csem[e] = ent(self.nc.semaphore(f"c_{e}"))
        for e in self.DMAQ:
            self.dsem[e] = [ent(self.nc.semaphore(f"d_{e}{i}"))
                            for i in range(self.NLANE)]

    def op(self, engine, fn, reads=(), writes=(), dma=False):
        assert engine in self.ENGINES
        assert dma or engine != "sync"
        i = len(self.ops)
        deps = set()
        for b in reads:
            if b in self.last_w:
                deps.add(self.last_w[b])
        for b in writes:
            if b in self.last_w:
                deps.add(self.last_w[b])
            deps.update(self.last_r.get(b, ()))
        deps.discard(i)
        o = dict(idx=i, engine=engine, fn=fn, deps=sorted(deps),
                 dma=dma, inc=bool(dma), semval=None, pos=None, lane=None)
        if dma:
            o["lane"] = self.dtot[engine] % self.NLANE
            self.dtot[engine] += 1
        self.ops.append(o)
        for b in reads:
            self.last_r.setdefault(b, set()).add(i)
        for b in writes:
            self.last_w[b] = i
            self.last_r[b] = set()
        return i

    def dma(self, queue, out_ap, in_ap, reads=(), writes=(), transpose=False):
        self.op(queue,
                lambda e, o=out_ap, i=in_ap, tr=transpose:
                e.dma_start(o, i, transpose=True) if tr else e.dma_start(o, i),
                reads=reads, writes=writes, dma=True)

    def _fence_waits(self, e, eng, snapc, snapd):
        w = self.waited[e]
        for e2 in self.CENG:
            if e2 != e and snapc[e2] > 0 and w.get(("c", e2), 0) < snapc[e2]:
                eng.wait_ge(self.csem[e2], snapc[e2])
                w[("c", e2)] = snapc[e2]
        for e2 in self.DMAQ:
            for l in range(self.NLANE):
                v = 16 * snapd[e2][l]
                if v > 0 and w.get(("d", e2, l), 0) < v:
                    eng.wait_ge(self.dsem[e2][l], v)
                    w[("d", e2, l)] = v

    def fence(self):
        base = self.emitted
        seg = self.ops[base:]
        if not seg:
            return
        for o in seg:
            for d in o["deps"]:
                if d < base:
                    continue
                od = self.ops[d]
                if od["engine"] != o["engine"] and not od["dma"]:
                    od["inc"] = True
        eng_ops = {e: [] for e in self.ENGINES}
        for o in seg:
            o["pos"] = self.engpos[o["engine"]]
            self.engpos[o["engine"]] += 1
            eng_ops[o["engine"]].append(o)
        for e, lst in eng_ops.items():
            if lst and not lst[-1]["dma"]:
                lst[-1]["inc"] = True
        for o in seg:
            if o["dma"]:
                q, l = o["engine"], o["lane"]
                self.dlcnt[q][l] += 1
                o["semval"] = 16 * self.dlcnt[q][l]
            elif o["inc"]:
                self.ccnt[o["engine"]] += 1
                o["semval"] = self.ccnt[o["engine"]]

        snapc, snapd = self.snap

        def run(e, eng):
            w = self.waited[e]
            self._fence_waits(e, eng, snapc, snapd)
            for o in eng_ops[e]:
                need_drain = False
                for d in o["deps"]:
                    if d < base:
                        continue
                    od = self.ops[d]
                    if od["engine"] == e and not od["dma"]:
                        if od["pos"] > self.drained[e]:
                            need_drain = True
                        continue
                    assert od["inc"]
                    if od["dma"]:
                        k = ("d", od["engine"], od["lane"])
                        sem = self.dsem[od["engine"]][od["lane"]]
                    else:
                        k = ("c", od["engine"])
                        sem = self.csem[od["engine"]]
                    if w.get(k, 0) < od["semval"]:
                        eng.wait_ge(sem, od["semval"])
                        w[k] = od["semval"]
                if need_drain:
                    eng.drain()
                    self.drained[e] = o["pos"] - 1
                if o["dma"]:
                    k = ("d", e, o["lane"])
                    prev = o["semval"] - 16
                    if prev > 0 and w.get(k, 0) < prev:
                        eng.wait_ge(self.dsem[e][o["lane"]], prev)
                        w[k] = prev
                ins = o["fn"](eng)
                if o["dma"]:
                    ins.then_inc(self.dsem[e][o["lane"]], 16)
                elif o["inc"]:
                    ins.then_inc(self.csem[o["engine"]], 1)

        with self.nc.Block() as blk:
            for e in self.ENGINES:
                if eng_ops[e]:
                    getattr(blk, e)(lambda eng, e=e: run(e, eng))

        self.emitted = len(self.ops)
        self.snap = (dict(self.ccnt),
                     {e: list(v) for e, v in self.dlcnt.items()})
        self.last_w = {}
        self.last_r = {}
        for e in self.ENGINES:
            self.drained[e] = self.engpos[e] - 1

    def finalize(self):
        snapc = dict(self.ccnt)
        snapd = {e: list(v) for e, v in self.dlcnt.items()}
        with self.nc.Block() as blk:
            for e in self.ENGINES:
                getattr(blk, e)(
                    lambda eng, e=e: self._fence_waits(e, eng, snapc, snapd))




def build_nc():
    nc = bass.Bass("TRN2", target_bir_lowering=False)

    x_h = nc.dram_tensor("x", [T, D], F32, kind="ExternalInput")
    xg_h = nc.dram_tensor("xg", [16, D], F32, kind="ExternalInput")
    wq_h = nc.dram_tensor("wq", [D, D], F32, kind="ExternalInput")
    wk_h = nc.dram_tensor("wk", [D, D], F32, kind="ExternalInput")
    wv_h = nc.dram_tensor("wv", [D, D], F32, kind="ExternalInput")
    wo_h = nc.dram_tensor("wo", [D, D], F32, kind="ExternalInput")
    wg_h = nc.dram_tensor("wg", [INNER, D], F32, kind="ExternalInput")
    wu_h = nc.dram_tensor("wu", [INNER, D], F32, kind="ExternalInput")
    wd_h = nc.dram_tensor("wd", [D, INNER], F32, kind="ExternalInput")
    eye_h = nc.dram_tensor("eye", [128, 128], F32, kind="ExternalInput")
    onec_h = nc.dram_tensor("onec", [128, 1], F32, kind="ExternalInput")
    oner_h = nc.dram_tensor("oner", [1, 128], F32, kind="ExternalInput")
    maskn_h = nc.dram_tensor("maskn", [128, 2560], BF16, kind="ExternalInput")
    gmask_h = nc.dram_tensor("gmask", [128, 16], BF16, kind="ExternalInput")
    y_h = nc.dram_tensor("y", [T, D], F32, kind="ExternalOutput")
    ctx_scr = nc.dram_tensor("ctx_scr", [D, T], F32)
    a_scr = nc.dram_tensor("a_scr", [T, D], BF16)
    x1_scr = nc.dram_tensor("x1_scr", [T, D], F32)
    tg_scr = nc.dram_tensor("tg_scr", [INNER, D], BF16)
    tu_scr = nc.dram_tensor("tu_scr", [INNER, D], BF16)
    td_scr = nc.dram_tensor("td_scr", [D, INNER], BF16)
    h_scr = nc.dram_tensor("h_scr", [INNER, T], F32)
    row_scr = nc.dram_tensor("row_scr", [4, T], F32)

    from contextlib import ExitStack
    with ExitStack() as stack:
        ent = stack.enter_context
        cst = ent(nc.sbuf_tensor("cst", [128, 384], F32))
        sc = ent(nc.sbuf_tensor("sc", [128, 128], F32))
        i0b = ent(nc.sbuf_tensor("i0b", [128, 1024], I32))
        pb = [ent(nc.psum_tensor(f"pb{i}", [128, 512], F32)) for i in range(8)]
        pb6, pb7 = pb[6], pb[7]
        S = Sched(nc)
        S.make_sems(ent)

        eye = cst[:, 0:128]
        oner = cst[0:1, 128:256]
        onec = cst[:, 256:257]
        tau_v = [cst[:, 260 + j:261 + j] for j in range(3)]
        mg127_v = cst[:, 263:264]

        st6a = sc[:, 48:60]
        st2a = sc[:, 60:62]
        lnwa = sc[:, 62:64]
        maxh_nat = sc[:, 32:48]
        maxr_nat = sc[:, 64:80]

        def mm(out_ap, lhs_ap, rhs_ap, reads, writes, start=True, stop=True):
            S.op("tensor", lambda t, o=out_ap, l=lhs_ap, r=rhs_ap,
                 st=start, sp=stop: t.matmul(o, l, r, start=st, stop=sp),
                 reads=reads, writes=writes)

        def tpose(out_ap, in_ap, reads, writes, pb0_=0, pn=128):
            ey = eye[pb0_:pb0_ + pn, pb0_:pb0_ + pn]
            S.op("tensor", lambda t, o=out_ap, i=in_ap, e2=ey: t.transpose(o, i, e2),
                 reads=tuple(reads) + ("cst",), writes=writes)

        def cop(eng, dst, src, reads, writes):
            if eng == "scalar":
                S.op("scalar", lambda s, d=dst, s2=src: s.copy(d, s2),
                     reads=reads, writes=writes)
            else:
                S.op("vector", lambda v, d=dst, s2=src: v.tensor_copy(d, s2),
                     reads=reads, writes=writes)

        EV = ("vector", "scalar")

        def ln_tile(xap, xkeys, P=128):
            S.op("vector", lambda v, x=xap, p2=P: v.bn_stats(st6a[0:p2, 0:6],
                                                             x[:, 0:512]),
                 reads=xkeys, writes=("st6",))
            S.op("vector", lambda v, x=xap, p2=P: v.bn_stats(st6a[0:p2, 6:12],
                                                             x[:, 512:1024]),
                 reads=xkeys, writes=("st6",))
            S.op("vector", lambda v, p2=P: v.bn_aggr(st2a[0:p2, :], st6a[0:p2, :]),
                 reads=("st6",), writes=("st2",))
            S.op("vector", lambda v, p2=P: v.tensor_single_scalar(
                lnwa[0:p2, 0:1], st2a[0:p2, 1:2], 1e-5, ALU.add),
                 reads=("st2",), writes=("lnw",))
            S.op("scalar", lambda s, p2=P: s.activation(lnwa[0:p2, 1:2],
                                                        lnwa[0:p2, 0:1], AF.Ln),
                 reads=("lnw",), writes=("lnw",))
            S.op("vector", lambda v, p2=P: v.tensor_single_scalar(
                lnwa[0:p2, 0:1], lnwa[0:p2, 1:2], -0.5, ALU.mult),
                 reads=("lnw",), writes=("lnw",))
            S.op("scalar", lambda s, p2=P: s.activation(lnwa[0:p2, 0:1],
                                                        lnwa[0:p2, 0:1], AF.Exp),
                 reads=("lnw",), writes=("lnw",))
            return st2a[:, 0:1], lnwa[:, 0:1]

        def wt_prep(af, wh, wbase, wst, pitch, scale, keypfx):
            for r in range(8):
                p = r % 2
                buf = af[:, wst + p * 1024: wst + (p + 1) * 1024]
                S.dma("sync", buf, wh.ap()[r * 128:(r + 1) * 128, :],
                      (f"{keypfx}{p}",), (f"{keypfx}{p}",))
                for c in range(8):
                    bk = pb[4 + 2 * p + c // 4]
                    tpose(bk[:, (c % 4) * 128:(c % 4) * 128 + 128],
                          buf[:, c * 128:(c + 1) * 128],
                          (f"{keypfx}{p}",), (f"pb{4 + 2 * p + c // 4}",))
                for half in range(2):
                    bki = 4 + 2 * p + half
                    dst = bass.AP(af, wbase + (half * 4) * 1024 + r * 128,
                                  [[pitch, 128], [1024, 4], [1, 128]])
                    src = pb[bki][:, 0:512].rearrange("p (g k) -> p g k", g=4)
                    eng = EV[(r * 2 + half) % 2]
                    if scale is None:
                        cop(eng, dst, src, (f"pb{bki}",), ("wf",))
                    elif eng == "scalar":
                        S.op("scalar", lambda s, d=dst, s2=src, sc2=scale:
                             s.mul(d, s2, sc2),
                             reads=(f"pb{bki}",), writes=("wf",))
                    else:
                        S.op("vector", lambda v, d=dst, s2=src, sc2=scale:
                             v.tensor_single_scalar(d, s2, sc2, ALU.mult),
                             reads=(f"pb{bki}",), writes=("wf",))

        wtiles = []
        for wi, (wh, scr3) in enumerate(
                ((wg_h, tg_scr), (wu_h, tu_scr), (wd_h, td_scr))):
            if wi < 2:
                for r in range(32):
                    wtiles.append((wi, wh.ap()[r * 128:(r + 1) * 128, :],
                                   scr3.ap()[r * 128:(r + 1) * 128, :]))
            else:
                for r in range(8):
                    for q in range(4):
                        wtiles.append((wi,
                                       wh.ap()[r * 128:(r + 1) * 128,
                                               q * 1024:(q + 1) * 1024],
                                       scr3.ap()[r * 128:(r + 1) * 128,
                                                 q * 1024:(q + 1) * 1024]))
        absac = sc[:, 32:35]

        PITA = 44160
        with (
            nc.sbuf_tensor("afA", [128, PITA], F32) as afA,
            nc.sbuf_tensor("mk", [128, 2560], BF16) as mkT,
            nc.sbuf_tensor("gm", [128, 16], BF16) as gmT,
            nc.sbuf_tensor("afW1", [128, 2048], F32) as afW1,
        ):
            WQ, WK, WV = 0, 8192, 16384
            XNT = 24576
            QT0 = 26624
            KT0 = 28672
            VT0 = 31232
            SCB, PTB = 35328, 37888
            XT, XN = 39936, 41984
            KGT = 44032
            WST = SCB
            XG = PTB
            XGT = PTB + 1024

            S.dma("sync", cst[:, 0:128], eye_h.ap(), (), ("cst",))
            S.dma("sync", cst[0:1, 128:256], oner_h.ap(), (), ("cst",))
            S.dma("sync", cst[:, 256:257], onec_h.ap(), (), ("cst",))
            S.dma("sync", mkT[:, 0:2560], maskn_h.ap(), (), ("mk",))
            S.dma("sync", gmT[:, 0:16], gmask_h.ap(), (), ("mk",))
            S.op("vector", lambda v: v.memset(cst[:, 267:268], CMAG),
                 reads=(), writes=("cst",))
            S.op("vector", lambda v: v.memset(cst[:, 268:269], -CMAG),
                 reads=(), writes=("cst",))

            wt_prep(afA, wq_h, WQ, WST, PITA, 0.125, "wsa")
            wt_prep(afA, wk_h, WK, WST, PITA, None, "wsa")
            wt_prep(afA, wv_h, WV, WST, PITA, None, "wsa")

            S.dma("sync", afA[0:16, XG:XG + 1024], xg_h.ap(), (), ("xg",))
            mean, rstd = ln_tile(afA[0:16, XG:XG + 1024], ("xg",), P=16)
            xgn = afA[0:16, XN:XN + 1024]
            S.op("vector", lambda v, m=mean, r=rstd: v.tensor_scalar(
                xgn, afA[0:16, XG:XG + 1024], m[0:16, :], r[0:16, :],
                ALU.subtract, ALU.mult),
                 reads=("xg", "st2", "lnw"), writes=("xn",))
            for c in range(8):
                tpose(pb6[0:128, c * 16:(c + 1) * 16],
                      afA[0:16, XN + c * 128:XN + (c + 1) * 128], ("xn",), ("pb6",),
                      pn=16)
            S.op("vector", lambda v: v.tensor_copy(afA[:, XGT:XGT + 128],
                                                   pb6[:, 0:128]),
                 reads=("pb6",), writes=("xgt",))

            def xgt_c(c):
                return afA[:, XGT + c * 16:XGT + (c + 1) * 16]

            for n in range(2):
                for c in range(8):
                    mm(pb[n][0:16, 0:512], xgt_c(c),
                       afA[:, WK + c * 1024 + n * 512:WK + c * 1024 + (n + 1) * 512],
                       ("xgt", "wf"), (f"pb{n}",), start=(c == 0), stop=(c == 7))
            for n in range(2):
                S.op("vector", lambda v, n2=n: v.tensor_copy(
                    afA[0:16, XG + n2 * 512:XG + (n2 + 1) * 512],
                    pb[n2][0:16, 0:512]),
                     reads=(f"pb{n}",), writes=("xg",))
            for m in range(8):
                tpose(pb6[0:128, m * 16:(m + 1) * 16],
                      afA[0:16, XG + m * 128:XG + (m + 1) * 128], ("xg",), ("pb6",),
                      pn=16)
            S.op("vector", lambda v: v.tensor_copy(afA[:, KGT:KGT + 128],
                                                   pb6[:, 0:128]),
                 reads=("pb6",), writes=("kgt",))
            for n in range(2):
                for c in range(8):
                    mm(pb[2 + n][0:16, 0:512], xgt_c(c),
                       afA[:, WV + c * 1024 + n * 512:WV + c * 1024 + (n + 1) * 512],
                       ("xgt", "wf"), (f"pb{2 + n}",), start=(c == 0), stop=(c == 7))
            for n in range(2):
                S.op("vector", lambda v, n2=n: v.tensor_copy(
                    afA[0:16, XG + n2 * 512:XG + (n2 + 1) * 512],
                    pb[2 + n2][0:16, 0:512]),
                     reads=(f"pb{2 + n}",), writes=("xg",))
            for tg in range(4):
                S.dma("gpsimd", afA[64:80, VT0 + tg * 1024:VT0 + (tg + 1) * 1024],
                      afA[0:16, XG:XG + 1024], ("xg",), ("vg",))
            for m in range(8):
                for tg in range(4):
                    cop(EV[(m + tg) % 2],
                        afA[:, KT0 + m * 320 + tg * 80 + 64:
                            KT0 + m * 320 + tg * 80 + 80],
                        afA[:, KGT + m * 16:KGT + (m + 1) * 16],
                        ("kgt",), ("ktg",))
            for wi in range(3):
                S.op("vector", lambda v, w2=wi: v.memset(absac[:, w2:w2 + 1], 0.0),
                     reads=(), writes=(f"aa{wi}",))
            S.fence()

            def w_absum_tiles(lo, hi):
                for t3 in range(lo, hi):
                    wi, src_ap, _ = wtiles[t3]
                    p = t3 % 2
                    buf = afW1[:, p * 1024:(p + 1) * 1024]
                    S.dma("scalar", buf, src_ap, (f"wpa{p}",), (f"wpa{p}",))
                    red = sc[:, 35 + p:36 + p]
                    S.op("vector", lambda v, b=buf, rd=red: v.reduce_sum(
                        rd, b, axis=AX, apply_absolute_value=True),
                         reads=(f"wpa{p}",), writes=(f"red{p}",))
                    S.op("vector", lambda v, rd=red, w2=wi: v.tensor_tensor(
                        absac[:, w2:w2 + 1], absac[:, w2:w2 + 1], rd, ALU.add),
                         reads=(f"red{p}", f"aa{wi}"), writes=(f"aa{wi}",))

            def ln_prefetch(tb2):
                for ti in range(2):
                    tok0 = tb2 * TB + ti * 128
                    xt = afA[:, XT + ti * 1024:XT + (ti + 1) * 1024]
                    xn = afA[:, XN + ti * 1024:XN + (ti + 1) * 1024]
                    S.dma("sync", xt, x_h.ap()[tok0:tok0 + 128, :],
                          (f"xta{ti}",), (f"xta{ti}",))
                    mean, rstd = ln_tile(xt, (f"xta{ti}",))
                    S.op("vector", lambda v, x=xt, o=xn, m=mean, r=rstd:
                         v.tensor_scalar(o, x, m, r, ALU.subtract, ALU.mult),
                         reads=(f"xta{ti}", "st2", "lnw"), writes=(f"xn{ti}",))

            for tb in range(NTB):
                if tb == 0:
                    ln_prefetch(0)
                for ti in range(2):
                    xn = afA[:, XN + ti * 1024:XN + (ti + 1) * 1024]
                    for c in range(8):
                        tpose(pb[6 + c // 4][:, (c % 4) * 128:(c % 4) * 128 + 128],
                              xn[:, c * 128:(c + 1) * 128],
                              (f"xn{ti}",), (f"pb{6 + c // 4}",))
                    for half in range(2):
                        dst = bass.AP(afA, XNT + (half * 4) * 256 + ti * 128,
                                      [[PITA, 128], [256, 4], [1, 128]])
                        src = pb[6 + half][:, 0:512].rearrange(
                            "p (g k) -> p g k", g=4)
                        cop(EV[half], dst, src, (f"pb{6 + half}",), ("xnt",))

                for wbase, dkey in ((WQ, "qt"), (WK, "kt")):
                    for m in range(8):
                        bk = pb[m % 4]
                        bcol = (m // 4) * 256
                        for c in range(8):
                            mm(bk[:, bcol:bcol + 256],
                               afA[:, wbase + c * 1024 + m * 128:
                                   wbase + c * 1024 + (m + 1) * 128],
                               afA[:, XNT + c * 256:XNT + (c + 1) * 256],
                               ("wf", "xnt"), (f"pb{m % 4}",),
                               start=(c == 0), stop=(c == 7))
                    for b2 in range(4):
                        if wbase == WQ:
                            dst = bass.AP(afA, QT0 + b2 * 256,
                                          [[PITA, 128], [1024, 2], [1, 256]])
                            src = pb[b2][:, 0:512].rearrange(
                                "p (g k) -> p g k", g=2)
                        else:
                            dst = bass.AP(afA, KT0 + b2 * 320,
                                          [[PITA, 128], [1280, 2], [80, 4], [1, 64]])
                            src = pb[b2][:, 0:512].rearrange(
                                "p (g t k) -> p g t k", g=2, t=4)
                        cop(EV[b2 % 2], dst, src, (f"pb{b2}",), (dkey,))
                for tg in range(4):
                    for n in range(2):
                        bki = 4 + (tg * 2 + n) % 4
                        for c in range(8):
                            mm(pb[bki][0:64, 0:512],
                               afA[:, XNT + c * 256 + tg * 64:
                                   XNT + c * 256 + tg * 64 + 64],
                               afA[:, WV + c * 1024 + n * 512:
                                   WV + c * 1024 + (n + 1) * 512],
                               ("wf", "xnt"), (f"pb{bki}",),
                               start=(c == 0), stop=(c == 7))
                        cop(EV[n],
                            afA[0:64, VT0 + tg * 1024 + n * 512:
                                VT0 + tg * 1024 + (n + 1) * 512],
                            pb[bki][0:64, 0:512],
                            (f"pb{bki}",), ("vt",))

                if tb + 1 < NTB:
                    ln_prefetch(tb + 1)
                def scbuf(pr):
                    return afA[pr * 64:(pr + 1) * 64, SCB:SCB + 2560]

                for pr in range(2):
                    scb = scbuf(pr)
                    pk = f"scb{pr}"
                    for h2 in range(H):
                        qrow = (h2 % 2) * 64
                        for t2 in range(2):
                            ci = 2 * h2 + t2
                            bk = pb[ci // 6]
                            cc = (ci % 6) * 80
                            tg = pr * 2 + t2
                            lhs = afA[qrow:qrow + 64,
                                      QT0 + (h2 // 2) * 256 + tg * 64:
                                      QT0 + (h2 // 2) * 256 + tg * 64 + 64]
                            rhs = afA[qrow:qrow + 64,
                                      KT0 + (h2 // 2) * 320 + tg * 80:
                                      KT0 + (h2 // 2) * 320 + tg * 80 + 80]
                            mm(bk[0:64, cc:cc + 80], lhs, rhs,
                               ("qt", "kt", "ktg"), (f"pb{ci // 6}",))
                    for b2 in range(6):
                        w3 = 480 if b2 < 5 else 160
                        S.op("vector", lambda v, b=pb[b2], off=b2 * 480, w4=w3,
                             s4=scb, p2=pr:
                             v.tensor_tensor(s4[:, off:off + w4], b[0:64, 0:w4],
                                             mkT[p2 * 64:p2 * 64 + 64,
                                                 off:off + w4],
                                             ALU.add),
                             reads=(f"pb{b2}", "mk"), writes=(pk,))
                    if tb == 0 and pr == 0:
                        glob3 = bass.AP(afA, SCB + 64, [[PITA, 64], [160, 16],
                                                        [1, 16]])
                        gm3 = bass.AP(gmT, 0, [[16, 64], [0, 16], [1, 16]])
                        S.op("vector", lambda v, g3=glob3, g4=gm3: v.tensor_tensor(
                            g3, g3, g4, ALU.add),
                             reads=(pk, "mk"), writes=(pk,))
                    scb3 = scb.rearrange("p (h k) -> p h k", k=80)
                    mx3 = sc[pr * 64:pr * 64 + 64, 0:32].rearrange(
                        "p (h k) -> p h k", k=1)
                    S.op("vector", lambda v, o=mx3, i=scb3:
                         v.reduce_max(o, i, axis=AX),
                         reads=(pk,), writes=(f"mx{pr}",))
                    mxb = bass.AP(sc, pr * 64 * 128, [[128, 64], [1, 32], [0, 80]])
                    S.op("vector", lambda v, m3=mxb, s3=scb3: v.tensor_tensor(
                        s3, s3, m3, ALU.subtract),
                         reads=(pk, f"mx{pr}"), writes=(pk,))
                    S.op("scalar", lambda s, s3=scb: s.activation(s3, s3, AF.Exp),
                         reads=(pk,), writes=(pk,))
                    rs2 = sc[pr * 64:pr * 64 + 64, 96:128]
                    rs3 = rs2.rearrange("p (h k) -> p h k", k=1)
                    S.op("vector", lambda v, o=rs3, i=scb3:
                         v.reduce_sum(o, i, axis=AX),
                         reads=(pk,), writes=(f"rs{pr}",))
                    S.op("vector", lambda v, r=rs2: v.reciprocal(r, r),
                         reads=(f"rs{pr}",), writes=(f"rs{pr}",))
                    rsb = bass.AP(sc, pr * 64 * 128 + 96,
                                  [[128, 64], [1, 32], [0, 80]])
                    S.op("vector", lambda v, r=rsb, s3=scb3: v.tensor_tensor(
                        s3, s3, r, ALU.mult),
                         reads=(pk, f"rs{pr}"), writes=(pk,))

                PTBANKS = (6, 7, 0, 1)
                for pr in range(2):
                    scb = scbuf(pr)
                    pk = f"scb{pr}"
                    for ci in range(32):
                        bk = pb[PTBANKS[ci // 8]]
                        tpose(bk[0:80, (ci % 8) * 64:(ci % 8) * 64 + 64],
                              scb[:, ci * 80:ci * 80 + 80],
                              (pk,), (f"pb{PTBANKS[ci // 8]}",),
                              pb0_=pr * 64, pn=64)
                    for g2 in range(4):
                        S.op("scalar", lambda s, b=pb[PTBANKS[g2]], off=g2 * 512:
                             s.copy(afA[0:80, PTB + off:PTB + off + 512],
                                    b[0:80, 0:512]),
                             reads=(f"pb{PTBANKS[g2]}",), writes=("ptb",))
                    for h2 in range(H):
                        for t2 in range(2):
                            ci = 2 * h2 + t2
                            bk = pb[2 + ci // 8]
                            tg = pr * 2 + t2
                            lhsl = afA[0:80,
                                       VT0 + tg * 1024 + h2 * 64:
                                       VT0 + tg * 1024 + h2 * 64 + 64]
                            rhsl = afA[0:80, PTB + ci * 64:PTB + ci * 64 + 64]
                            mm(bk[0:64, (ci % 8) * 64:(ci % 8) * 64 + 64],
                               lhsl, rhsl, ("vt", "vg", "ptb"),
                               (f"pb{2 + ci // 8}",))
                    for g2 in range(4):
                        cop(EV[g2 % 2],
                            scb[:, g2 * 512:g2 * 512 + 512],
                            pb[2 + g2][0:64, 0:512],
                            (f"pb{2 + g2}",), (pk,))
                    tok0 = tb * TB + pr * 128
                    src = bass.AP(afA, pr * 64 * PITA + SCB,
                                  [[PITA, 64], [128, 16], [64, 2], [1, 64]])
                    dst = bass.AP(ctx_scr, tok0,
                                  [[2048, 64], [64 * 2048, 16], [64, 2], [1, 64]])
                    S.dma("gpsimd", dst, src, (pk,), ("ctxw",))
                w_absum_tiles(tb * 12, (tb + 1) * 12)
            S.fence()

        PITB = 34816
        with (
            nc.sbuf_tensor("afB", [128, PITB], F32) as afB,
            nc.sbuf_tensor("abB", [128, 12288], BF16) as abB,
            nc.sbuf_tensor("afW2", [128, 2048], F32) as afW2,
            nc.sbuf_tensor("abW", [128, 6144], BF16) as abW,
        ):
            WO, WST2 = 0, 8192
            CTXF, XT2, X1B, XN2, DQ = 10240, 14336, 18432, 22528, 26624
            GB2 = 30720
            MD, ATI = 0, 8192

            mm(pb6[0:1, 0:3], onec, absac[:, 0:3], ("cst",), ("pb6",))
            S.op("vector", lambda v: v.tensor_copy(sc[0:1, 8:11], pb6[0:1, 0:3]),
                 reads=("pb6",), writes=("wm",))
            S.op("vector", lambda v: v.tensor_scalar(
                sc[0:1, 8:11], sc[0:1, 8:11], 1.0 / (INNER * D), 1e-5,
                ALU.mult, ALU.max),
                 reads=("wm",), writes=("wm",))
            S.op("vector", lambda v: v.tensor_single_scalar(
                sc[0:1, 11:14], sc[0:1, 8:11], 0.5, ALU.mult),
                 reads=("wm",), writes=("tau",))
            S.op("vector", lambda v: v.tensor_single_scalar(
                sc[0:1, 14:15], sc[0:1, 8:9], 1.0 / 254.0, ALU.mult),
                 reads=("wm",), writes=("tau",))
            S.op("vector", lambda v: v.reciprocal(sc[0:1, 17:18], sc[0:1, 8:9]),
                 reads=("wm",), writes=("mumg",))
            S.op("vector", lambda v: v.tensor_tensor(
                sc[0:1, 17:18], sc[0:1, 17:18], sc[0:1, 9:10], ALU.mult),
                 reads=("mumg", "wm"), writes=("mumg",))
            S.op("vector", lambda v: v.tensor_single_scalar(
                sc[0:1, 16:17], sc[0:1, 10:11], -1.0 / 254.0, ALU.mult),
                 reads=("wm",), writes=("md127",))
            mm(pb7[0:128, 0:4], oner, sc[0:1, 11:15], ("tau", "cst"), ("pb7",))
            S.op("vector", lambda v: v.tensor_copy(cst[:, 260:264], pb7[:, 0:4]),
                 reads=("pb7",), writes=("cst",))
            S.op("vector", lambda v: v.tensor_single_scalar(
                cst[:, 264:267], cst[:, 260:263], -1.0, ALU.mult),
                 reads=("cst",), writes=("cst",))

            def w_tern_tiles(lo, hi):
                for t3 in range(lo, hi):
                    wi, src_ap, dst_ap = wtiles[t3]
                    p = t3 % 2
                    buf = afW2[:, p * 1024:(p + 1) * 1024]
                    S.dma(("sync", "scalar")[p], buf, src_ap,
                          (f"wpa{p}",), (f"wpa{p}",))
                    s1 = abW[:, p * 1024:(p + 1) * 1024]
                    s2b = abW[:, 2048 + p * 1024:2048 + (p + 1) * 1024]
                    tn = abW[:, 4096 + p * 1024:4096 + (p + 1) * 1024]
                    S.op("scalar", lambda s, o=s1, b=buf, w2=wi: s.sign(
                        o, b, bias=cst[:, 264 + w2:265 + w2]),
                         reads=(f"wpa{p}", "cst"), writes=(f"tp{p}",))
                    S.op("scalar", lambda s, o=s2b, b=buf, w2=wi: s.sign(
                        o, b, bias=cst[:, 260 + w2:261 + w2]),
                         reads=(f"wpa{p}", "cst"), writes=(f"tq{p}",))
                    S.op("gpsimd", lambda g, o=tn, a=s1, b=s2b: g.tensor_tensor(
                        o, a, b, ALU.add),
                         reads=(f"tp{p}", f"tq{p}"), writes=(f"tn{p}",))
                    S.dma("gpsimd", dst_ap, tn, (f"tn{p}",), ("ts",))

            wt_prep(afB, wo_h, WO, WST2, PITB, None, "wsb")

            for ti in range(16):
                p = ti % 4
                tok0 = ti * 128
                w_tern_tiles(ti * 6, (ti + 1) * 6)
                cf = afB[:, CTXF + p * 1024:CTXF + (p + 1) * 1024]
                csrc = bass.AP(ctx_scr, tok0,
                               [[2048, 128], [128 * 2048, 8], [1, 128]])
                cdst = bass.AP(afB, CTXF + p * 1024,
                               [[PITB, 128], [128, 8], [1, 128]])
                S.dma("sync", cdst, csrc, (f"cf{p}",), (f"cf{p}",))
                xt = afB[:, XT2 + p * 1024:XT2 + (p + 1) * 1024]
                S.dma("sync", xt, x_h.ap()[tok0:tok0 + 128, :],
                      (f"xtb{p}",), (f"xtb{p}",))
                for n in range(2):
                    bk = pb[4 + n]
                    for c in range(8):
                        mm(bk[:, 0:512], cf[:, c * 128:(c + 1) * 128],
                           afB[:, WO + c * 1024 + n * 512:WO + c * 1024 + (n + 1) * 512],
                           (f"cf{p}", "wf"), (f"pb{4 + n}",),
                           start=(c == 0), stop=(c == 7))
                x1 = afB[:, X1B + p * 1024:X1B + (p + 1) * 1024]
                for n in range(2):
                    S.op("vector", lambda v, b=pb[4 + n], n2=n, x2=x1, xx=xt:
                         v.tensor_tensor(x2[:, n2 * 512:(n2 + 1) * 512],
                                         b[:, 0:512],
                                         xx[:, n2 * 512:(n2 + 1) * 512], ALU.add),
                         reads=(f"pb{4 + n}", f"xtb{p}"), writes=(f"x1{p}",))
                S.dma("scalar", x1_scr.ap()[tok0:tok0 + 128, :], x1,
                      (f"x1{p}",), ("x1s",))
                xn2 = afB[:, XN2 + p * 1024:XN2 + (p + 1) * 1024]
                mean, rstd = ln_tile(x1, (f"x1{p}",))
                S.op("vector", lambda v, x2=x1, o=xn2, m=mean, r=rstd:
                     v.tensor_scalar(o, x2, m, r, ALU.subtract, ALU.mult),
                     reads=(f"x1{p}", "st2", "lnw"), writes=(f"xn2{p}",))
                mr = maxr_nat[:, ti:ti + 1]
                sx = sc[:, 80 + ti:81 + ti]
                S.op("vector", lambda v, m3=mr, x2=xn2: v.reduce_max(
                    m3, x2, axis=AX, apply_absolute_value=True),
                     reads=(f"xn2{p}",), writes=("maxr",))
                S.op("vector", lambda v, m3=mr: v.tensor_single_scalar(
                    m3, m3, 1e-5, ALU.max),
                     reads=("maxr",), writes=("maxr",))
                S.op("vector", lambda v, m3=mr, s3=sx: v.reciprocal(s3, m3),
                     reads=("maxr",), writes=("sx",))
                S.op("vector", lambda v, s3=sx: v.tensor_single_scalar(
                    s3, s3, 127.0, ALU.mult),
                     reads=("sx",), writes=("sx",))
                E = PAR[p % 2]
                i0f = i0b[:, 0:1024].bitcast(F32)
                ic = afB[:, DQ + p * 1024:DQ + (p + 1) * 1024]
                dd = xt
                mp = abB[:, MD + p * 1024:MD + (p + 1) * 1024]
                mdn = abB[:, MD + 4096 + p * 1024:MD + 4096 + (p + 1) * 1024]
                at = abB[:, ATI + p * 1024:ATI + (p + 1) * 1024]
                gq = afB[:, GB2 + p * 1024:GB2 + (p + 1) * 1024]
                S.op("scalar", lambda s, o=gq, x2=xn2, s3=sx: s.mul(o, x2, s3),
                     reads=(f"xn2{p}", "sx"), writes=(f"gq{p}",))
                S.op("scalar", lambda s, o=i0f, i=gq: s.add(
                    o, i, cst[:, 267:268]),
                     reads=(f"gq{p}", "cst", "i0"), writes=("i0",))
                S.op("scalar", lambda s, o=ic, i=i0f: s.add(
                    o, i, cst[:, 268:269]),
                     reads=("i0", "cst"), writes=(f"ic{p}",))
                S.op(E, lambda v, o=dd, i=ic, g2=gq: v.tensor_sub(o, g2, i),
                     reads=(f"ic{p}", f"gq{p}"), writes=(f"xtb{p}",))
                S.op("vector", lambda v, o=mp, i=dd: v.tensor_single_scalar(
                    o, i, 0.5, ALU.is_gt),
                     reads=(f"xtb{p}",), writes=(f"mp{p}",))
                S.op("vector", lambda v, o=mdn, i=dd, m3=mp: v.scalar_tensor_tensor(
                    o, i, -0.5, m3, ALU.is_lt, ALU.subtract),
                     reads=(f"xtb{p}", f"mp{p}"), writes=(f"md{p}",))
                S.op("vector", lambda v, o=at, a=ic, b=mdn: v.tensor_sub(o, a, b),
                     reads=(f"ic{p}", f"md{p}"), writes=(f"at{p}",))
                S.dma("gpsimd", a_scr.ap()[tok0:tok0 + 128, :], at,
                      (f"at{p}",), ("as",))
            S.fence()

        with (
            nc.sbuf_tensor("abF", [128, 49152], BF16) as abF,
            nc.sbuf_tensor("afF", [128, 20736], F32) as afF,
        ):
            WG0, WU0, ATL = 0, 16384, 32768
            GUH, RMX0, RMX1 = 0, 8192, 10240
            ABG, ABU = 12288, 14336
            RA, RG, RU = 16384, 16512, 18560

            for c in range(8):
                S.dma("scalar", abF[:, ATL + c * 2048:ATL + (c + 1) * 2048],
                      a_scr.ap()[0:T, c * 128:(c + 1) * 128],
                      (), ("atb",), transpose=True)
            S.op("vector", lambda v: v.tensor_scalar_mul(sc[:, 0:16], maxr_nat,
                                                         mg127_v),
                 reads=("cst",), writes=("ag",))
            tpose(pb6[0:16, 0:128], sc[:, 0:16], ("ag",), ("pb6",))
            S.op("vector", lambda v: v.tensor_copy(afF[0:16, RA:RA + 128],
                                                   pb6[0:16, 0:128]),
                 reads=("pb6",), writes=("ra",))
            S.dma("gpsimd",
                  row_scr.ap()[0:1, 0:T].rearrange("a (b c) -> (a b) c", c=128),
                  afF[0:16, RA:RA + 128], ("ra",), ("rowh",))
            S.dma("sync", afF[0:1, RG:RG + T], row_scr.ap()[0:1, 0:T],
                  ("rowh",), ("rg",))
            S.op("vector", lambda v: v.tensor_scalar_mul(
                afF[0:1, RU:RU + T], afF[0:1, RG:RG + T], sc[0:1, 17:18]),
                 reads=("rg",), writes=("ru",))
            for src_off, dst, rk in ((RG, ABG, "rg"), (RU, ABU, "ru")):
                for q in range(4):
                    bk = pb[2 + q % 2]
                    mm(bk[:, 0:512], oner,
                       afF[0:1, src_off + q * 512:src_off + (q + 1) * 512],
                       (rk, "cst"), (f"pb{2 + q % 2}",))
                    cop(EV[q % 2], afF[:, dst + q * 512:dst + (q + 1) * 512],
                        bk[:, 0:512], (f"pb{2 + q % 2}",), ("ab",))
            S.op("vector", lambda v: v.memset(afF[:, RMX0:RMX0 + 2048], 0.0),
                 reads=(), writes=("rm0",))
            S.op("gpsimd", lambda g: g.memset(afF[:, RMX1:RMX1 + 2048], 0.0),
                 reads=(), writes=("rm1",))

            for half in range(2):
                for wi, (scr3, dst) in enumerate(((tg_scr, WG0), (tu_scr, WU0))):
                    for c in range(8):
                        S.dma("scalar", abF[:, dst + c * 2048:dst + (c + 1) * 2048],
                              scr3.ap()[half * 2048:(half + 1) * 2048,
                                        c * 128:(c + 1) * 128],
                              ("wgw" if wi == 0 else "wuw",),
                              ("wgw" if wi == 0 else "wuw",), transpose=True)
                for kc in range(16):
                    for tb2 in range(4):
                        q = (kc * 4 + tb2) % 4
                        bg, bu = pb[q], pb[4 + q]
                        for c in range(8):
                            mm(bg[:, 0:512],
                               abF[:, WG0 + c * 2048 + kc * 128:
                                   WG0 + c * 2048 + (kc + 1) * 128],
                               abF[:, ATL + c * 2048 + tb2 * 512:
                                   ATL + c * 2048 + (tb2 + 1) * 512],
                               ("wgw", "atb"), (f"pb{q}",),
                               start=(c == 0), stop=(c == 7))
                            mm(bu[:, 0:512],
                               abF[:, WU0 + c * 2048 + kc * 128:
                                   WU0 + c * 2048 + (kc + 1) * 128],
                               abF[:, ATL + c * 2048 + tb2 * 512:
                                   ATL + c * 2048 + (tb2 + 1) * 512],
                               ("wuw", "atb"), (f"pb{4 + q}",),
                               start=(c == 0), stop=(c == 7))
                        gb = afF[:, GUH + q * 2048:GUH + q * 2048 + 512]
                        gs = afF[:, GUH + q * 2048 + 512:GUH + q * 2048 + 1024]
                        ub = afF[:, GUH + q * 2048 + 1024:GUH + q * 2048 + 1536]
                        hb = afF[:, GUH + q * 2048 + 1536:GUH + q * 2048 + 2048]
                        S.op("vector", lambda v, o=gb, b=bg, t4=tb2: v.tensor_tensor(
                            o, b[:, 0:512],
                            afF[:, ABG + t4 * 512:ABG + (t4 + 1) * 512], ALU.mult),
                             reads=(f"pb{q}", "ab"), writes=(f"gb{q}",))
                        S.op("scalar", lambda s, o=gs, i=gb:
                             s.activation(o, i, AF.Silu),
                             reads=(f"gb{q}",), writes=(f"gs{q}",))
                        S.op("vector", lambda v, o=ub, b=bu, t4=tb2: v.tensor_tensor(
                            o, b[:, 0:512],
                            afF[:, ABU + t4 * 512:ABU + (t4 + 1) * 512], ALU.mult),
                             reads=(f"pb{4 + q}", "ab"), writes=(f"ub{q}",))
                        S.op("gpsimd", lambda g, o=hb, a=gs, b=ub: g.tensor_tensor(
                            o, a, b, ALU.mult),
                             reads=(f"gs{q}", f"ub{q}"), writes=(f"hb{q}",))
                        rmx0 = afF[:, RMX0 + tb2 * 512:RMX0 + (tb2 + 1) * 512]
                        rmx1 = afF[:, RMX1 + tb2 * 512:RMX1 + (tb2 + 1) * 512]
                        S.op("vector", lambda v, o=rmx0, b=hb: v.tensor_tensor(
                            o, o, b, ALU.max),
                             reads=(f"hb{q}", "rm0"), writes=("rm0",))
                        S.op("vector", lambda v, o=rmx1, b=hb: v.tensor_tensor(
                            o, o, b, ALU.min),
                             reads=(f"hb{q}", "rm1"), writes=("rm1",))
                        kg = half * 16 + kc
                        S.dma("sync",
                              h_scr.ap()[kg * 128:(kg + 1) * 128,
                                         tb2 * 512:(tb2 + 1) * 512],
                              hb, (f"hb{q}",), ("hs",))
            S.op("vector", lambda v: v.tensor_single_scalar(
                afF[:, RMX1:RMX1 + 2048], afF[:, RMX1:RMX1 + 2048], -1.0, ALU.mult),
                 reads=("rm1",), writes=("rm1",))
            S.op("vector", lambda v: v.tensor_tensor(
                afF[:, RMX0:RMX0 + 2048], afF[:, RMX0:RMX0 + 2048],
                afF[:, RMX1:RMX1 + 2048], ALU.max),
                 reads=("rm0", "rm1"), writes=("rm0",))
            for sl in range(16):
                bk = pb[sl % 2]
                tpose(bk[:, 0:128], afF[:, RMX0 + sl * 128:RMX0 + (sl + 1) * 128],
                      ("rm0",), (f"pb{sl % 2}",))
                S.op("vector", lambda v, b=bk, s3=sl: v.reduce_max(
                    maxh_nat[:, s3:s3 + 1], b[:, 0:128], axis=AX),
                     reads=(f"pb{sl % 2}",), writes=("maxh",))
            S.op("vector", lambda v: v.tensor_single_scalar(
                maxh_nat, maxh_nat, 1e-5, ALU.max),
                 reads=("maxh",), writes=("maxh",))
            S.fence()

        with (
            nc.sbuf_tensor("abG", [128, 38912], BF16) as abG,
            nc.sbuf_tensor("afG", [128, 22656], F32) as afG,
        ):
            WD0, HQ, MD2 = 0, 32768, 34816
            HT, G3, IC2, DD2 = 0, 2048, 4096, 6144
            SYB, BDB, FFT = 8192, 10240, 12288
            X1T, OUT = 16384, 17408
            ROWW, RMH, RROW = 18432, 18560, 20608

            for kc in range(32):
                S.dma("scalar", abG[:, WD0 + kc * 1024:WD0 + (kc + 1) * 1024],
                      td_scr.ap()[0:D, kc * 128:(kc + 1) * 128],
                      (), ("wdt",), transpose=True)
            tpose(pb6[0:16, 0:128], maxh_nat, ("cst",), ("pb6",))
            S.op("vector", lambda v: v.tensor_copy(afG[0:16, ROWW:ROWW + 128],
                                                   pb6[0:16, 0:128]),
                 reads=("pb6",), writes=("rw",))
            S.dma("gpsimd",
                  row_scr.ap()[2:3, 0:T].rearrange("a (b c) -> (a b) c", c=128),
                  afG[0:16, ROWW:ROWW + 128], ("rw",), ("rowh2",))
            S.dma("sync", afG[0:1, RMH:RMH + T], row_scr.ap()[2:3, 0:T],
                  ("rowh2",), ("rmh",))
            S.op("vector", lambda v: v.reciprocal(afG[0:1, RROW:RROW + T],
                                                  afG[0:1, RMH:RMH + T]),
                 reads=("rmh",), writes=("rrow",))
            S.op("vector", lambda v: v.tensor_single_scalar(
                afG[0:1, RROW:RROW + T], afG[0:1, RROW:RROW + T], 127.0, ALU.mult),
                 reads=("rrow",), writes=("rrow",))
            for q in range(4):
                bk = pb[2 + q % 2]
                mm(bk[:, 0:512], oner, afG[0:1, RROW + q * 512:RROW + (q + 1) * 512],
                   ("rrow", "cst"), (f"pb{2 + q % 2}",))
                cop(EV[q % 2], afG[:, SYB + q * 512:SYB + (q + 1) * 512],
                    bk[:, 0:512], (f"pb{2 + q % 2}",), ("syb",))
            S.op("vector", lambda v: v.tensor_scalar_mul(
                afG[0:1, RROW:RROW + T], afG[0:1, RMH:RMH + T], sc[0:1, 16:17]),
                 reads=("rmh",), writes=("rrow",))
            for q in range(4):
                bk = pb[2 + q % 2]
                mm(bk[:, 0:512], oner, afG[0:1, RROW + q * 512:RROW + (q + 1) * 512],
                   ("rrow", "cst"), (f"pb{2 + q % 2}",))
                cop(EV[q % 2], afG[:, BDB + q * 512:BDB + (q + 1) * 512],
                    bk[:, 0:512], (f"pb{2 + q % 2}",), ("bdb",))

            for tb2 in range(4):
                for kc in range(32):
                    p = kc % 4
                    ht = afG[:, HT + p * 512:HT + (p + 1) * 512]
                    S.dma("sync", ht,
                          h_scr.ap()[kc * 128:(kc + 1) * 128,
                                     tb2 * 512:(tb2 + 1) * 512],
                          (f"ht{p}",), (f"ht{p}",))
                    g3 = afG[:, G3 + p * 512:G3 + (p + 1) * 512]
                    ic = afG[:, IC2 + p * 512:IC2 + (p + 1) * 512]
                    dd = afG[:, DD2 + p * 512:DD2 + (p + 1) * 512]
                    i0f = i0b[:, 0:512].bitcast(F32)
                    mp = abG[:, MD2 + p * 512:MD2 + (p + 1) * 512]
                    mdn = abG[:, MD2 + 1024 + p * 512:MD2 + 1024 + (p + 1) * 512]
                    hq = abG[:, HQ + p * 512:HQ + (p + 1) * 512]
                    S.op("gpsimd", lambda g, o=g3, a=ht, t4=tb2: g.tensor_tensor(
                        o, a, afG[:, SYB + t4 * 512:SYB + (t4 + 1) * 512], ALU.mult),
                         reads=(f"ht{p}", "syb"), writes=(f"g3{p}",))
                    S.op("scalar", lambda s, o=i0f, i=g3: s.add(
                        o, i, cst[:, 267:268]),
                         reads=(f"g3{p}", "cst", "i0"), writes=("i0",))
                    S.op("scalar", lambda s, o=ic, i=i0f: s.add(
                        o, i, cst[:, 268:269]),
                         reads=("i0", "cst"), writes=(f"ic{p}",))
                    S.op("gpsimd", lambda g, o=dd, a=ic, b=g3: g.tensor_sub(o, b, a),
                         reads=(f"ic{p}", f"g3{p}"), writes=(f"dd{p}",))
                    S.op("vector", lambda v, o=mp, i=dd, a=ic:
                         v.scalar_tensor_tensor(
                             o, i, 0.5, a, ALU.is_gt, ALU.add),
                         reads=(f"dd{p}", f"ic{p}"), writes=(f"mp{p}",))
                    S.op("vector", lambda v, o=hq, i=dd, c1=mp:
                         v.scalar_tensor_tensor(
                             o, i, -0.5, c1, ALU.is_lt, ALU.subtract),
                         reads=(f"dd{p}", f"mp{p}"), writes=(f"hq{p}",))
                    for m2 in range(8):
                        mm(pb[m2][:, 0:512],
                           abG[:, WD0 + kc * 1024 + m2 * 128:
                               WD0 + kc * 1024 + (m2 + 1) * 128],
                           hq, ("wdt", f"hq{p}"), (f"pb{m2}",),
                           start=(kc == 0), stop=(kc == 31))
                for m2 in range(8):
                    S.op("vector", lambda v, b=pb[m2], m3=m2, t4=tb2:
                         v.tensor_tensor(
                             afG[:, FFT + m3 * 512:FFT + (m3 + 1) * 512],
                             b[:, 0:512],
                             afG[:, BDB + t4 * 512:BDB + (t4 + 1) * 512], ALU.mult),
                         reads=(f"pb{m2}", "bdb"), writes=("fft",))
                for sl in range(4):
                    tok0 = tb2 * 512 + sl * 128
                    x1t = afG[:, X1T:X1T + 1024]
                    S.dma("sync", x1t, x1_scr.ap()[tok0:tok0 + 128, :],
                          ("x1t", "x1s"), ("x1t",))
                    for m2 in range(8):
                        tpose(pb[m2 // 4][:, (m2 % 4) * 128:(m2 % 4) * 128 + 128],
                              afG[:, FFT + m2 * 512 + sl * 128:
                                  FFT + m2 * 512 + sl * 128 + 128],
                              ("fft",), (f"pb{m2 // 4}",))
                    ot = afG[:, OUT:OUT + 1024]
                    for b2 in range(2):
                        S.op("vector", lambda v, b=pb[b2], b3=b2, o=ot, x2=x1t:
                             v.tensor_tensor(o[:, b3 * 512:(b3 + 1) * 512],
                                             b[:, 0:512],
                                             x2[:, b3 * 512:(b3 + 1) * 512], ALU.add),
                             reads=(f"pb{b2}", "x1t"), writes=("ot",))
                    S.dma("scalar", y_h.ap()[tok0:tok0 + 128, :], ot,
                          ("ot",), ("ys",))
            S.fence()
        S.finalize()

    return nc


_NC_CACHE = None


def mk_mask(first_pair):
    m = np.zeros((64, 2560), np.float32)
    j = np.arange(64)
    for h in range(H):
        for t2 in range(2):
            ci = 2 * h + t2
            for q in range(64):
                m[q, ci * 80:ci * 80 + 64][j > q] = NEG
            if first_pair and t2 == 0:
                m[:, ci * 80 + 64:ci * 80 + 80] = NEG
    return m.astype(ml_dtypes.bfloat16)


def kernel(L0_attn_hidden_states, L0_attn_Wq, L0_attn_Wk, L0_attn_Wv,
           L0_attn_Wo, L0_ffn_w_gate, L0_ffn_w_up, L0_ffn_w_down):
    global _NC_CACHE
    x = np.ascontiguousarray(L0_attn_hidden_states, dtype=np.float32)
    B, Sq, Dm = x.shape
    xf = x.reshape(B * Sq, Dm)

    eye = np.eye(128, dtype=np.float32)
    onec = np.ones((128, 1), np.float32)
    oner = np.ones((1, 128), np.float32)

    mask_n = np.vstack([mk_mask(False)] * 2)

    if _NC_CACHE is None:
        _NC_CACHE = build_nc()
    nc = _NC_CACHE

    in_maps = []
    for c in range(8):
        b = c // 2
        sh = c % 2
        in_maps.append({
            "x": np.ascontiguousarray(xf[c * T:(c + 1) * T]),
            "xg": np.ascontiguousarray(x[b, 0:16, :]),
            "wq": np.ascontiguousarray(L0_attn_Wq, np.float32),
            "wk": np.ascontiguousarray(L0_attn_Wk, np.float32),
            "wv": np.ascontiguousarray(L0_attn_Wv, np.float32),
            "wo": np.ascontiguousarray(L0_attn_Wo, np.float32),
            "wg": np.ascontiguousarray(L0_ffn_w_gate, np.float32),
            "wu": np.ascontiguousarray(L0_ffn_w_up, np.float32),
            "wd": np.ascontiguousarray(L0_ffn_w_down, np.float32),
            "eye": eye, "onec": onec, "oner": oner,
            "maskn": mask_n,
            "gmask": np.full((128, 16), NEG if sh == 0 else 0.0,
                             ml_dtypes.bfloat16),
        })
    res = run_bass_kernel_spmd(nc, in_maps, core_ids=list(range(8)))
    out = np.concatenate([res.results[c]["y"] for c in range(8)], axis=0)
    return out.reshape(B, Sq, Dm)


# version 27
# speedup vs baseline: 1.0984x; 1.0060x over previous
import numpy as np
import ml_dtypes
import concourse.bass as bass
import concourse.mybir as mybir
from concourse.bass_utils import run_bass_kernel_spmd

F32 = mybir.dt.float32
BF16 = mybir.dt.bfloat16
I32 = mybir.dt.int32
AX = mybir.AxisListType.X
AF = mybir.ActivationFunctionType
ALU = mybir.AluOpType

T = 2048
D = 1024
H = 16
INNER = 4096
TB = 256
NTB = T // TB
CMAG = float(np.float32(3 * 2 ** 22))
NEG = -1.0e30
PAR = ("vector", "gpsimd")



class Sched:

    ENGINES = ("sync", "scalar", "vector", "tensor", "gpsimd")
    CENG = ("scalar", "vector", "tensor", "gpsimd")
    DMAQ = ("sync", "scalar", "gpsimd")
    NLANE = 12

    def __init__(self, nc):
        self.nc = nc
        self.ops = []
        self.last_w = {}
        self.last_r = {}
        self.emitted = 0
        self.ccnt = {e: 0 for e in self.CENG}
        self.dtot = {e: 0 for e in self.DMAQ}
        self.dlcnt = {e: [0] * self.NLANE for e in self.DMAQ}
        self.waited = {e: {} for e in self.ENGINES}
        self.drained = {e: -1 for e in self.ENGINES}
        self.engpos = {e: 0 for e in self.ENGINES}
        self.snap = (dict(self.ccnt),
                     {e: list(v) for e, v in self.dlcnt.items()})
        self.csem = {}
        self.dsem = {}

    def make_sems(self, ent):
        for e in self.CENG:
            self.csem[e] = ent(self.nc.semaphore(f"c_{e}"))
        for e in self.DMAQ:
            self.dsem[e] = [ent(self.nc.semaphore(f"d_{e}{i}"))
                            for i in range(self.NLANE)]

    def op(self, engine, fn, reads=(), writes=(), dma=False):
        assert engine in self.ENGINES
        assert dma or engine != "sync"
        i = len(self.ops)
        deps = set()
        for b in reads:
            if b in self.last_w:
                deps.add(self.last_w[b])
        for b in writes:
            if b in self.last_w:
                deps.add(self.last_w[b])
            deps.update(self.last_r.get(b, ()))
        deps.discard(i)
        o = dict(idx=i, engine=engine, fn=fn, deps=sorted(deps),
                 dma=dma, inc=bool(dma), semval=None, pos=None, lane=None)
        if dma:
            o["lane"] = self.dtot[engine] % self.NLANE
            self.dtot[engine] += 1
        self.ops.append(o)
        for b in reads:
            self.last_r.setdefault(b, set()).add(i)
        for b in writes:
            self.last_w[b] = i
            self.last_r[b] = set()
        return i

    def dma(self, queue, out_ap, in_ap, reads=(), writes=(), transpose=False):
        self.op(queue,
                lambda e, o=out_ap, i=in_ap, tr=transpose:
                e.dma_start(o, i, transpose=True) if tr else e.dma_start(o, i),
                reads=reads, writes=writes, dma=True)

    def _fence_waits(self, e, eng, snapc, snapd):
        w = self.waited[e]
        for e2 in self.CENG:
            if e2 != e and snapc[e2] > 0 and w.get(("c", e2), 0) < snapc[e2]:
                eng.wait_ge(self.csem[e2], snapc[e2])
                w[("c", e2)] = snapc[e2]
        for e2 in self.DMAQ:
            for l in range(self.NLANE):
                v = 16 * snapd[e2][l]
                if v > 0 and w.get(("d", e2, l), 0) < v:
                    eng.wait_ge(self.dsem[e2][l], v)
                    w[("d", e2, l)] = v

    def fence(self):
        base = self.emitted
        seg = self.ops[base:]
        if not seg:
            return
        for o in seg:
            for d in o["deps"]:
                if d < base:
                    continue
                od = self.ops[d]
                if od["engine"] != o["engine"] and not od["dma"]:
                    od["inc"] = True
        eng_ops = {e: [] for e in self.ENGINES}
        for o in seg:
            o["pos"] = self.engpos[o["engine"]]
            self.engpos[o["engine"]] += 1
            eng_ops[o["engine"]].append(o)
        for e, lst in eng_ops.items():
            if lst and not lst[-1]["dma"]:
                lst[-1]["inc"] = True
        for o in seg:
            if o["dma"]:
                q, l = o["engine"], o["lane"]
                self.dlcnt[q][l] += 1
                o["semval"] = 16 * self.dlcnt[q][l]
            elif o["inc"]:
                self.ccnt[o["engine"]] += 1
                o["semval"] = self.ccnt[o["engine"]]

        snapc, snapd = self.snap

        def run(e, eng):
            w = self.waited[e]
            self._fence_waits(e, eng, snapc, snapd)
            for o in eng_ops[e]:
                need_drain = False
                for d in o["deps"]:
                    if d < base:
                        continue
                    od = self.ops[d]
                    if od["engine"] == e and not od["dma"]:
                        if od["pos"] > self.drained[e]:
                            need_drain = True
                        continue
                    assert od["inc"]
                    if od["dma"]:
                        k = ("d", od["engine"], od["lane"])
                        sem = self.dsem[od["engine"]][od["lane"]]
                    else:
                        k = ("c", od["engine"])
                        sem = self.csem[od["engine"]]
                    if w.get(k, 0) < od["semval"]:
                        eng.wait_ge(sem, od["semval"])
                        w[k] = od["semval"]
                if need_drain:
                    eng.drain()
                    self.drained[e] = o["pos"] - 1
                if o["dma"]:
                    k = ("d", e, o["lane"])
                    prev = o["semval"] - 16
                    if prev > 0 and w.get(k, 0) < prev:
                        eng.wait_ge(self.dsem[e][o["lane"]], prev)
                        w[k] = prev
                ins = o["fn"](eng)
                if o["dma"]:
                    ins.then_inc(self.dsem[e][o["lane"]], 16)
                elif o["inc"]:
                    ins.then_inc(self.csem[o["engine"]], 1)

        with self.nc.Block() as blk:
            for e in self.ENGINES:
                if eng_ops[e]:
                    getattr(blk, e)(lambda eng, e=e: run(e, eng))

        self.emitted = len(self.ops)
        self.snap = (dict(self.ccnt),
                     {e: list(v) for e, v in self.dlcnt.items()})
        self.last_w = {}
        self.last_r = {}
        for e in self.ENGINES:
            self.drained[e] = self.engpos[e] - 1

    def finalize(self):
        snapc = dict(self.ccnt)
        snapd = {e: list(v) for e, v in self.dlcnt.items()}
        with self.nc.Block() as blk:
            for e in self.ENGINES:
                getattr(blk, e)(
                    lambda eng, e=e: self._fence_waits(e, eng, snapc, snapd))




def build_nc():
    nc = bass.Bass("TRN2", target_bir_lowering=False)

    x_h = nc.dram_tensor("x", [T, D], F32, kind="ExternalInput")
    xg_h = nc.dram_tensor("xg", [16, D], F32, kind="ExternalInput")
    wq_h = nc.dram_tensor("wq", [D, D], F32, kind="ExternalInput")
    wk_h = nc.dram_tensor("wk", [D, D], F32, kind="ExternalInput")
    wv_h = nc.dram_tensor("wv", [D, D], F32, kind="ExternalInput")
    wo_h = nc.dram_tensor("wo", [D, D], F32, kind="ExternalInput")
    wg_h = nc.dram_tensor("wg", [INNER, D], F32, kind="ExternalInput")
    wu_h = nc.dram_tensor("wu", [INNER, D], F32, kind="ExternalInput")
    wd_h = nc.dram_tensor("wd", [D, INNER], F32, kind="ExternalInput")
    eye_h = nc.dram_tensor("eye", [128, 128], F32, kind="ExternalInput")
    onec_h = nc.dram_tensor("onec", [128, 1], F32, kind="ExternalInput")
    oner_h = nc.dram_tensor("oner", [1, 128], F32, kind="ExternalInput")
    maskn_h = nc.dram_tensor("maskn", [128, 2560], BF16, kind="ExternalInput")
    gmask_h = nc.dram_tensor("gmask", [128, 16], BF16, kind="ExternalInput")
    y_h = nc.dram_tensor("y", [T, D], F32, kind="ExternalOutput")
    ctx_scr = nc.dram_tensor("ctx_scr", [D, T], F32)
    a_scr = nc.dram_tensor("a_scr", [T, D], BF16)
    x1_scr = nc.dram_tensor("x1_scr", [T, D], F32)
    tg_scr = nc.dram_tensor("tg_scr", [INNER, D], BF16)
    tu_scr = nc.dram_tensor("tu_scr", [INNER, D], BF16)
    td_scr = nc.dram_tensor("td_scr", [D, INNER], BF16)
    h_scr = nc.dram_tensor("h_scr", [INNER, T], F32)
    row_scr = nc.dram_tensor("row_scr", [4, T], F32)

    from contextlib import ExitStack
    with ExitStack() as stack:
        ent = stack.enter_context
        cst = ent(nc.sbuf_tensor("cst", [128, 384], F32))
        sc = ent(nc.sbuf_tensor("sc", [128, 128], F32))
        i0b = ent(nc.sbuf_tensor("i0b", [128, 1024], I32))
        pb = [ent(nc.psum_tensor(f"pb{i}", [128, 512], F32)) for i in range(8)]
        pb6, pb7 = pb[6], pb[7]
        S = Sched(nc)
        S.make_sems(ent)

        eye = cst[:, 0:128]
        oner = cst[0:1, 128:256]
        onec = cst[:, 256:257]
        tau_v = [cst[:, 260 + j:261 + j] for j in range(3)]
        mg127_v = cst[:, 263:264]

        st6a = sc[:, 48:60]
        st2a = sc[:, 60:62]
        lnwa = sc[:, 62:64]
        maxh_nat = sc[:, 32:48]
        maxr_nat = sc[:, 64:80]

        def mm(out_ap, lhs_ap, rhs_ap, reads, writes, start=True, stop=True):
            S.op("tensor", lambda t, o=out_ap, l=lhs_ap, r=rhs_ap,
                 st=start, sp=stop: t.matmul(o, l, r, start=st, stop=sp),
                 reads=reads, writes=writes)

        def tpose(out_ap, in_ap, reads, writes, pb0_=0, pn=128):
            ey = eye[pb0_:pb0_ + pn, pb0_:pb0_ + pn]
            S.op("tensor", lambda t, o=out_ap, i=in_ap, e2=ey: t.transpose(o, i, e2),
                 reads=tuple(reads) + ("cst",), writes=writes)

        def cop(eng, dst, src, reads, writes):
            if eng == "scalar":
                S.op("scalar", lambda s, d=dst, s2=src: s.copy(d, s2),
                     reads=reads, writes=writes)
            else:
                S.op("vector", lambda v, d=dst, s2=src: v.tensor_copy(d, s2),
                     reads=reads, writes=writes)

        EV = ("vector", "scalar")

        def ln_tile(xap, xkeys, P=128):
            S.op("vector", lambda v, x=xap, p2=P: v.bn_stats(st6a[0:p2, 0:6],
                                                             x[:, 0:512]),
                 reads=xkeys, writes=("st6",))
            S.op("vector", lambda v, x=xap, p2=P: v.bn_stats(st6a[0:p2, 6:12],
                                                             x[:, 512:1024]),
                 reads=xkeys, writes=("st6",))
            S.op("vector", lambda v, p2=P: v.bn_aggr(st2a[0:p2, :], st6a[0:p2, :]),
                 reads=("st6",), writes=("st2",))
            S.op("vector", lambda v, p2=P: v.tensor_single_scalar(
                lnwa[0:p2, 0:1], st2a[0:p2, 1:2], 1e-5, ALU.add),
                 reads=("st2",), writes=("lnw",))
            S.op("scalar", lambda s, p2=P: s.activation(lnwa[0:p2, 1:2],
                                                        lnwa[0:p2, 0:1], AF.Ln),
                 reads=("lnw",), writes=("lnw",))
            S.op("vector", lambda v, p2=P: v.tensor_single_scalar(
                lnwa[0:p2, 0:1], lnwa[0:p2, 1:2], -0.5, ALU.mult),
                 reads=("lnw",), writes=("lnw",))
            S.op("scalar", lambda s, p2=P: s.activation(lnwa[0:p2, 0:1],
                                                        lnwa[0:p2, 0:1], AF.Exp),
                 reads=("lnw",), writes=("lnw",))
            return st2a[:, 0:1], lnwa[:, 0:1]

        def wt_prep(af, wh, wbase, wst, pitch, scale, keypfx):
            for r in range(8):
                p = r % 2
                buf = af[:, wst + p * 1024: wst + (p + 1) * 1024]
                S.dma("sync", buf, wh.ap()[r * 128:(r + 1) * 128, :],
                      (f"{keypfx}{p}",), (f"{keypfx}{p}",))
                for c in range(8):
                    bk = pb[4 + 2 * p + c // 4]
                    tpose(bk[:, (c % 4) * 128:(c % 4) * 128 + 128],
                          buf[:, c * 128:(c + 1) * 128],
                          (f"{keypfx}{p}",), (f"pb{4 + 2 * p + c // 4}",))
                for half in range(2):
                    bki = 4 + 2 * p + half
                    dst = bass.AP(af, wbase + (half * 4) * 1024 + r * 128,
                                  [[pitch, 128], [1024, 4], [1, 128]])
                    src = pb[bki][:, 0:512].rearrange("p (g k) -> p g k", g=4)
                    eng = EV[(r * 2 + half) % 2]
                    if scale is None:
                        cop(eng, dst, src, (f"pb{bki}",), ("wf",))
                    elif eng == "scalar":
                        S.op("scalar", lambda s, d=dst, s2=src, sc2=scale:
                             s.mul(d, s2, sc2),
                             reads=(f"pb{bki}",), writes=("wf",))
                    else:
                        S.op("vector", lambda v, d=dst, s2=src, sc2=scale:
                             v.tensor_single_scalar(d, s2, sc2, ALU.mult),
                             reads=(f"pb{bki}",), writes=("wf",))

        wtiles = []
        for wi, (wh, scr3) in enumerate(
                ((wg_h, tg_scr), (wu_h, tu_scr), (wd_h, td_scr))):
            if wi < 2:
                for r in range(32):
                    wtiles.append((wi, wh.ap()[r * 128:(r + 1) * 128, :],
                                   scr3.ap()[r * 128:(r + 1) * 128, :]))
            else:
                for r in range(8):
                    for q in range(4):
                        wtiles.append((wi,
                                       wh.ap()[r * 128:(r + 1) * 128,
                                               q * 1024:(q + 1) * 1024],
                                       scr3.ap()[r * 128:(r + 1) * 128,
                                                 q * 1024:(q + 1) * 1024]))
        absac = sc[:, 32:35]

        PITA = 44160
        with (
            nc.sbuf_tensor("afA", [128, PITA], F32) as afA,
            nc.sbuf_tensor("mk", [128, 2560], BF16) as mkT,
            nc.sbuf_tensor("gm", [128, 16], BF16) as gmT,
            nc.sbuf_tensor("afW1", [128, 2048], F32) as afW1,
        ):
            WQ, WK, WV = 0, 8192, 16384
            XNT = 24576
            QT0 = 26624
            KT0 = 28672
            VT0 = 31232
            SCB, PTB = 35328, 37888
            XT, XN = 39936, 41984
            KGT = 44032
            WST = SCB
            XG = PTB
            XGT = PTB + 1024

            S.dma("sync", cst[:, 0:128], eye_h.ap(), (), ("cst",))
            S.dma("sync", cst[0:1, 128:256], oner_h.ap(), (), ("cst",))
            S.dma("sync", cst[:, 256:257], onec_h.ap(), (), ("cst",))
            S.dma("sync", mkT[:, 0:2560], maskn_h.ap(), (), ("mk",))
            S.dma("sync", gmT[:, 0:16], gmask_h.ap(), (), ("mk",))
            S.op("vector", lambda v: v.memset(cst[:, 267:268], CMAG),
                 reads=(), writes=("cst",))
            S.op("vector", lambda v: v.memset(cst[:, 268:269], -CMAG),
                 reads=(), writes=("cst",))

            wt_prep(afA, wq_h, WQ, WST, PITA, 0.125, "wsa")
            wt_prep(afA, wk_h, WK, WST, PITA, None, "wsa")
            wt_prep(afA, wv_h, WV, WST, PITA, None, "wsa")

            S.dma("sync", afA[0:16, XG:XG + 1024], xg_h.ap(), (), ("xg",))
            mean, rstd = ln_tile(afA[0:16, XG:XG + 1024], ("xg",), P=16)
            xgn = afA[0:16, XN:XN + 1024]
            S.op("vector", lambda v, m=mean, r=rstd: v.tensor_scalar(
                xgn, afA[0:16, XG:XG + 1024], m[0:16, :], r[0:16, :],
                ALU.subtract, ALU.mult),
                 reads=("xg", "st2", "lnw"), writes=("xn",))
            for c in range(8):
                tpose(pb6[0:128, c * 16:(c + 1) * 16],
                      afA[0:16, XN + c * 128:XN + (c + 1) * 128], ("xn",), ("pb6",),
                      pn=16)
            S.op("vector", lambda v: v.tensor_copy(afA[:, XGT:XGT + 128],
                                                   pb6[:, 0:128]),
                 reads=("pb6",), writes=("xgt",))

            def xgt_c(c):
                return afA[:, XGT + c * 16:XGT + (c + 1) * 16]

            for n in range(2):
                for c in range(8):
                    mm(pb[n][0:16, 0:512], xgt_c(c),
                       afA[:, WK + c * 1024 + n * 512:WK + c * 1024 + (n + 1) * 512],
                       ("xgt", "wf"), (f"pb{n}",), start=(c == 0), stop=(c == 7))
            for n in range(2):
                S.op("vector", lambda v, n2=n: v.tensor_copy(
                    afA[0:16, XG + n2 * 512:XG + (n2 + 1) * 512],
                    pb[n2][0:16, 0:512]),
                     reads=(f"pb{n}",), writes=("xg",))
            for m in range(8):
                tpose(pb6[0:128, m * 16:(m + 1) * 16],
                      afA[0:16, XG + m * 128:XG + (m + 1) * 128], ("xg",), ("pb6",),
                      pn=16)
            S.op("vector", lambda v: v.tensor_copy(afA[:, KGT:KGT + 128],
                                                   pb6[:, 0:128]),
                 reads=("pb6",), writes=("kgt",))
            for n in range(2):
                for c in range(8):
                    mm(pb[2 + n][0:16, 0:512], xgt_c(c),
                       afA[:, WV + c * 1024 + n * 512:WV + c * 1024 + (n + 1) * 512],
                       ("xgt", "wf"), (f"pb{2 + n}",), start=(c == 0), stop=(c == 7))
            for n in range(2):
                S.op("vector", lambda v, n2=n: v.tensor_copy(
                    afA[0:16, XG + n2 * 512:XG + (n2 + 1) * 512],
                    pb[2 + n2][0:16, 0:512]),
                     reads=(f"pb{2 + n}",), writes=("xg",))
            for tg in range(4):
                S.dma("gpsimd", afA[64:80, VT0 + tg * 1024:VT0 + (tg + 1) * 1024],
                      afA[0:16, XG:XG + 1024], ("xg",), ("vg",))
            for m in range(8):
                for tg in range(4):
                    cop(EV[(m + tg) % 2],
                        afA[:, KT0 + m * 320 + tg * 80 + 64:
                            KT0 + m * 320 + tg * 80 + 80],
                        afA[:, KGT + m * 16:KGT + (m + 1) * 16],
                        ("kgt",), ("ktg",))
            for wi in range(3):
                S.op("vector", lambda v, w2=wi: v.memset(absac[:, w2:w2 + 1], 0.0),
                     reads=(), writes=(f"aa{wi}",))
            S.fence()

            def w_absum_tiles(lo, hi):
                for t3 in range(lo, hi):
                    wi, src_ap, _ = wtiles[t3]
                    p = t3 % 2
                    buf = afW1[:, p * 1024:(p + 1) * 1024]
                    S.dma("scalar", buf, src_ap, (f"wpa{p}",), (f"wpa{p}",))
                    red = sc[:, 35 + p:36 + p]
                    S.op("vector", lambda v, b=buf, rd=red: v.reduce_sum(
                        rd, b, axis=AX, apply_absolute_value=True),
                         reads=(f"wpa{p}",), writes=(f"red{p}",))
                    S.op("vector", lambda v, rd=red, w2=wi: v.tensor_tensor(
                        absac[:, w2:w2 + 1], absac[:, w2:w2 + 1], rd, ALU.add),
                         reads=(f"red{p}", f"aa{wi}"), writes=(f"aa{wi}",))

            def ln_prefetch(tb2):
                for ti in range(2):
                    tok0 = tb2 * TB + ti * 128
                    xt = afA[:, XT + ti * 1024:XT + (ti + 1) * 1024]
                    xn = afA[:, XN + ti * 1024:XN + (ti + 1) * 1024]
                    S.dma("sync", xt, x_h.ap()[tok0:tok0 + 128, :],
                          (f"xta{ti}",), (f"xta{ti}",))
                    mean, rstd = ln_tile(xt, (f"xta{ti}",))
                    S.op("vector", lambda v, x=xt, o=xn, m=mean, r=rstd:
                         v.tensor_scalar(o, x, m, r, ALU.subtract, ALU.mult),
                         reads=(f"xta{ti}", "st2", "lnw"), writes=(f"xn{ti}",))

            for tb in range(NTB):
                if tb == 0:
                    ln_prefetch(0)
                for ti in range(2):
                    xn = afA[:, XN + ti * 1024:XN + (ti + 1) * 1024]
                    for c in range(8):
                        tpose(pb[6 + c // 4][:, (c % 4) * 128:(c % 4) * 128 + 128],
                              xn[:, c * 128:(c + 1) * 128],
                              (f"xn{ti}",), (f"pb{6 + c // 4}",))
                    for half in range(2):
                        dst = bass.AP(afA, XNT + (half * 4) * 256 + ti * 128,
                                      [[PITA, 128], [256, 4], [1, 128]])
                        src = pb[6 + half][:, 0:512].rearrange(
                            "p (g k) -> p g k", g=4)
                        cop(EV[half], dst, src, (f"pb{6 + half}",), ("xnt",))

                for wbase, dkey in ((WQ, "qt"), (WK, "kt")):
                    for m in range(8):
                        bk = pb[m % 4]
                        bcol = (m // 4) * 256
                        for c in range(8):
                            mm(bk[:, bcol:bcol + 256],
                               afA[:, wbase + c * 1024 + m * 128:
                                   wbase + c * 1024 + (m + 1) * 128],
                               afA[:, XNT + c * 256:XNT + (c + 1) * 256],
                               ("wf", "xnt"), (f"pb{m % 4}",),
                               start=(c == 0), stop=(c == 7))
                    for b2 in range(4):
                        if wbase == WQ:
                            dst = bass.AP(afA, QT0 + b2 * 256,
                                          [[PITA, 128], [1024, 2], [1, 256]])
                            src = pb[b2][:, 0:512].rearrange(
                                "p (g k) -> p g k", g=2)
                        else:
                            dst = bass.AP(afA, KT0 + b2 * 320,
                                          [[PITA, 128], [1280, 2], [80, 4], [1, 64]])
                            src = pb[b2][:, 0:512].rearrange(
                                "p (g t k) -> p g t k", g=2, t=4)
                        cop(EV[b2 % 2], dst, src, (f"pb{b2}",), (dkey,))
                for tg in range(4):
                    for n in range(2):
                        bki = 4 + (tg * 2 + n) % 4
                        for c in range(8):
                            mm(pb[bki][0:64, 0:512],
                               afA[:, XNT + c * 256 + tg * 64:
                                   XNT + c * 256 + tg * 64 + 64],
                               afA[:, WV + c * 1024 + n * 512:
                                   WV + c * 1024 + (n + 1) * 512],
                               ("wf", "xnt"), (f"pb{bki}",),
                               start=(c == 0), stop=(c == 7))
                        cop(EV[n],
                            afA[0:64, VT0 + tg * 1024 + n * 512:
                                VT0 + tg * 1024 + (n + 1) * 512],
                            pb[bki][0:64, 0:512],
                            (f"pb{bki}",), ("vt",))

                if tb + 1 < NTB:
                    ln_prefetch(tb + 1)
                def scbuf(pr):
                    return afA[pr * 64:(pr + 1) * 64, SCB:SCB + 2560]

                for pr in range(2):
                    scb = scbuf(pr)
                    pk = f"scb{pr}"
                    for h2 in range(H):
                        qrow = (h2 % 2) * 64
                        for t2 in range(2):
                            ci = 2 * h2 + t2
                            bk = pb[ci // 6]
                            cc = (ci % 6) * 80
                            tg = pr * 2 + t2
                            lhs = afA[qrow:qrow + 64,
                                      QT0 + (h2 // 2) * 256 + tg * 64:
                                      QT0 + (h2 // 2) * 256 + tg * 64 + 64]
                            rhs = afA[qrow:qrow + 64,
                                      KT0 + (h2 // 2) * 320 + tg * 80:
                                      KT0 + (h2 // 2) * 320 + tg * 80 + 80]
                            mm(bk[0:64, cc:cc + 80], lhs, rhs,
                               ("qt", "kt", "ktg"), (f"pb{ci // 6}",))
                    for b2 in range(6):
                        w3 = 480 if b2 < 5 else 160
                        S.op("vector", lambda v, b=pb[b2], off=b2 * 480, w4=w3,
                             s4=scb, p2=pr:
                             v.tensor_tensor(s4[:, off:off + w4], b[0:64, 0:w4],
                                             mkT[p2 * 64:p2 * 64 + 64,
                                                 off:off + w4],
                                             ALU.add),
                             reads=(f"pb{b2}", "mk"), writes=(pk,))
                    if tb == 0 and pr == 0:
                        glob3 = bass.AP(afA, SCB + 64, [[PITA, 64], [160, 16],
                                                        [1, 16]])
                        gm3 = bass.AP(gmT, 0, [[16, 64], [0, 16], [1, 16]])
                        S.op("vector", lambda v, g3=glob3, g4=gm3: v.tensor_tensor(
                            g3, g3, g4, ALU.add),
                             reads=(pk, "mk"), writes=(pk,))
                    scb3 = scb.rearrange("p (h k) -> p h k", k=80)
                    mx3 = sc[pr * 64:pr * 64 + 64, 0:32].rearrange(
                        "p (h k) -> p h k", k=1)
                    S.op("vector", lambda v, o=mx3, i=scb3:
                         v.reduce_max(o, i, axis=AX),
                         reads=(pk,), writes=(f"mx{pr}",))
                    mxb = bass.AP(sc, pr * 64 * 128, [[128, 64], [1, 32], [0, 80]])
                    S.op("vector", lambda v, m3=mxb, s3=scb3: v.tensor_tensor(
                        s3, s3, m3, ALU.subtract),
                         reads=(pk, f"mx{pr}"), writes=(pk,))
                    S.op("scalar", lambda s, s3=scb: s.activation(s3, s3, AF.Exp),
                         reads=(pk,), writes=(pk,))
                    rs2 = sc[pr * 64:pr * 64 + 64, 96:128]
                    rs3 = rs2.rearrange("p (h k) -> p h k", k=1)
                    S.op("vector", lambda v, o=rs3, i=scb3:
                         v.reduce_sum(o, i, axis=AX),
                         reads=(pk,), writes=(f"rs{pr}",))
                    S.op("vector", lambda v, r=rs2: v.reciprocal(r, r),
                         reads=(f"rs{pr}",), writes=(f"rs{pr}",))
                    rsb = bass.AP(sc, pr * 64 * 128 + 96,
                                  [[128, 64], [1, 32], [0, 80]])
                    S.op("vector", lambda v, r=rsb, s3=scb3: v.tensor_tensor(
                        s3, s3, r, ALU.mult),
                         reads=(pk, f"rs{pr}"), writes=(pk,))

                PTBANKS = (6, 7, 0, 1)
                for pr in range(2):
                    scb = scbuf(pr)
                    pk = f"scb{pr}"
                    for ci in range(32):
                        bk = pb[PTBANKS[ci // 8]]
                        tpose(bk[0:80, (ci % 8) * 64:(ci % 8) * 64 + 64],
                              scb[:, ci * 80:ci * 80 + 80],
                              (pk,), (f"pb{PTBANKS[ci // 8]}",),
                              pb0_=pr * 64, pn=64)
                    for g2 in range(4):
                        S.op("scalar", lambda s, b=pb[PTBANKS[g2]], off=g2 * 512:
                             s.copy(afA[0:80, PTB + off:PTB + off + 512],
                                    b[0:80, 0:512]),
                             reads=(f"pb{PTBANKS[g2]}",), writes=("ptb",))
                    for h2 in range(H):
                        for t2 in range(2):
                            ci = 2 * h2 + t2
                            bk = pb[2 + ci // 8]
                            tg = pr * 2 + t2
                            lhsl = afA[0:80,
                                       VT0 + tg * 1024 + h2 * 64:
                                       VT0 + tg * 1024 + h2 * 64 + 64]
                            rhsl = afA[0:80, PTB + ci * 64:PTB + ci * 64 + 64]
                            mm(bk[0:64, (ci % 8) * 64:(ci % 8) * 64 + 64],
                               lhsl, rhsl, ("vt", "vg", "ptb"),
                               (f"pb{2 + ci // 8}",))
                    for g2 in range(4):
                        cop(EV[g2 % 2],
                            scb[:, g2 * 512:g2 * 512 + 512],
                            pb[2 + g2][0:64, 0:512],
                            (f"pb{2 + g2}",), (pk,))
                    tok0 = tb * TB + pr * 128
                    src = bass.AP(afA, pr * 64 * PITA + SCB,
                                  [[PITA, 64], [128, 16], [64, 2], [1, 64]])
                    dst = bass.AP(ctx_scr, tok0,
                                  [[2048, 64], [64 * 2048, 16], [64, 2], [1, 64]])
                    S.dma("gpsimd", dst, src, (pk,), ("ctxw",))
                w_absum_tiles(tb * 12, (tb + 1) * 12)
            S.fence()

        PITB = 34816
        with (
            nc.sbuf_tensor("afB", [128, PITB], F32) as afB,
            nc.sbuf_tensor("abB", [128, 12288], BF16) as abB,
            nc.sbuf_tensor("afW2", [128, 2048], F32) as afW2,
            nc.sbuf_tensor("abW", [128, 6144], BF16) as abW,
        ):
            WO, WST2 = 0, 8192
            CTXF, XT2, X1B, XN2, DQ = 10240, 14336, 18432, 22528, 26624
            GB2 = 30720
            MD, ATI = 0, 8192

            mm(pb6[0:1, 0:3], onec, absac[:, 0:3], ("cst",), ("pb6",))
            S.op("vector", lambda v: v.tensor_copy(sc[0:1, 8:11], pb6[0:1, 0:3]),
                 reads=("pb6",), writes=("wm",))
            S.op("vector", lambda v: v.tensor_scalar(
                sc[0:1, 8:11], sc[0:1, 8:11], 1.0 / (INNER * D), 1e-5,
                ALU.mult, ALU.max),
                 reads=("wm",), writes=("wm",))
            S.op("vector", lambda v: v.tensor_single_scalar(
                sc[0:1, 11:14], sc[0:1, 8:11], 0.5, ALU.mult),
                 reads=("wm",), writes=("tau",))
            S.op("vector", lambda v: v.tensor_single_scalar(
                sc[0:1, 14:15], sc[0:1, 8:9], 1.0 / 254.0, ALU.mult),
                 reads=("wm",), writes=("tau",))
            S.op("vector", lambda v: v.reciprocal(sc[0:1, 17:18], sc[0:1, 8:9]),
                 reads=("wm",), writes=("mumg",))
            S.op("vector", lambda v: v.tensor_tensor(
                sc[0:1, 17:18], sc[0:1, 17:18], sc[0:1, 9:10], ALU.mult),
                 reads=("mumg", "wm"), writes=("mumg",))
            S.op("vector", lambda v: v.tensor_single_scalar(
                sc[0:1, 16:17], sc[0:1, 10:11], 1.0 / 254.0, ALU.mult),
                 reads=("wm",), writes=("md127",))
            mm(pb7[0:128, 0:4], oner, sc[0:1, 11:15], ("tau", "cst"), ("pb7",))
            S.op("vector", lambda v: v.tensor_copy(cst[:, 260:264], pb7[:, 0:4]),
                 reads=("pb7",), writes=("cst",))
            S.op("vector", lambda v: v.tensor_single_scalar(
                cst[:, 264:267], cst[:, 260:263], -1.0, ALU.mult),
                 reads=("cst",), writes=("cst",))

            def w_tern_tiles(lo, hi):
                for t3 in range(lo, hi):
                    wi, src_ap, dst_ap = wtiles[t3]
                    p = t3 % 2
                    buf = afW2[:, p * 1024:(p + 1) * 1024]
                    S.dma(("sync", "scalar")[p], buf, src_ap,
                          (f"wpa{p}",), (f"wpa{p}",))
                    s1 = abW[:, p * 1024:(p + 1) * 1024]
                    s2b = abW[:, 2048 + p * 1024:2048 + (p + 1) * 1024]
                    tn = abW[:, 4096 + p * 1024:4096 + (p + 1) * 1024]
                    S.op("scalar", lambda s, o=s1, b=buf, w2=wi: s.sign(
                        o, b, bias=cst[:, 264 + w2:265 + w2]),
                         reads=(f"wpa{p}", "cst"), writes=(f"tp{p}",))
                    S.op("scalar", lambda s, o=s2b, b=buf, w2=wi: s.sign(
                        o, b, bias=cst[:, 260 + w2:261 + w2]),
                         reads=(f"wpa{p}", "cst"), writes=(f"tq{p}",))
                    S.op("gpsimd", lambda g, o=tn, a=s1, b=s2b: g.tensor_tensor(
                        o, a, b, ALU.add),
                         reads=(f"tp{p}", f"tq{p}"), writes=(f"tn{p}",))
                    S.dma("gpsimd", dst_ap, tn, (f"tn{p}",), ("ts",))

            wt_prep(afB, wo_h, WO, WST2, PITB, None, "wsb")

            for ti in range(16):
                p = ti % 4
                tok0 = ti * 128
                w_tern_tiles(ti * 6, (ti + 1) * 6)
                cf = afB[:, CTXF + p * 1024:CTXF + (p + 1) * 1024]
                csrc = bass.AP(ctx_scr, tok0,
                               [[2048, 128], [128 * 2048, 8], [1, 128]])
                cdst = bass.AP(afB, CTXF + p * 1024,
                               [[PITB, 128], [128, 8], [1, 128]])
                S.dma("sync", cdst, csrc, (f"cf{p}",), (f"cf{p}",))
                xt = afB[:, XT2 + p * 1024:XT2 + (p + 1) * 1024]
                S.dma("sync", xt, x_h.ap()[tok0:tok0 + 128, :],
                      (f"xtb{p}",), (f"xtb{p}",))
                for n in range(2):
                    bk = pb[4 + n]
                    for c in range(8):
                        mm(bk[:, 0:512], cf[:, c * 128:(c + 1) * 128],
                           afB[:, WO + c * 1024 + n * 512:WO + c * 1024 + (n + 1) * 512],
                           (f"cf{p}", "wf"), (f"pb{4 + n}",),
                           start=(c == 0), stop=(c == 7))
                x1 = afB[:, X1B + p * 1024:X1B + (p + 1) * 1024]
                for n in range(2):
                    S.op("vector", lambda v, b=pb[4 + n], n2=n, x2=x1, xx=xt:
                         v.tensor_tensor(x2[:, n2 * 512:(n2 + 1) * 512],
                                         b[:, 0:512],
                                         xx[:, n2 * 512:(n2 + 1) * 512], ALU.add),
                         reads=(f"pb{4 + n}", f"xtb{p}"), writes=(f"x1{p}",))
                S.dma("scalar", x1_scr.ap()[tok0:tok0 + 128, :], x1,
                      (f"x1{p}",), ("x1s",))
                xn2 = afB[:, XN2 + p * 1024:XN2 + (p + 1) * 1024]
                mean, rstd = ln_tile(x1, (f"x1{p}",))
                S.op("vector", lambda v, x2=x1, o=xn2, m=mean, r=rstd:
                     v.tensor_scalar(o, x2, m, r, ALU.subtract, ALU.mult),
                     reads=(f"x1{p}", "st2", "lnw"), writes=(f"xn2{p}",))
                mr = maxr_nat[:, ti:ti + 1]
                sx = sc[:, 80 + ti:81 + ti]
                S.op("vector", lambda v, m3=mr, x2=xn2: v.reduce_max(
                    m3, x2, axis=AX, apply_absolute_value=True),
                     reads=(f"xn2{p}",), writes=("maxr",))
                S.op("vector", lambda v, m3=mr: v.tensor_single_scalar(
                    m3, m3, 1e-5, ALU.max),
                     reads=("maxr",), writes=("maxr",))
                S.op("vector", lambda v, m3=mr, s3=sx: v.reciprocal(s3, m3),
                     reads=("maxr",), writes=("sx",))
                S.op("vector", lambda v, s3=sx: v.tensor_single_scalar(
                    s3, s3, 127.0, ALU.mult),
                     reads=("sx",), writes=("sx",))
                E = PAR[p % 2]
                i0f = i0b[:, 0:1024].bitcast(F32)
                ic = afB[:, DQ + p * 1024:DQ + (p + 1) * 1024]
                dd = xt
                mp = abB[:, MD + p * 1024:MD + (p + 1) * 1024]
                mdn = abB[:, MD + 4096 + p * 1024:MD + 4096 + (p + 1) * 1024]
                at = abB[:, ATI + p * 1024:ATI + (p + 1) * 1024]
                gq = afB[:, GB2 + p * 1024:GB2 + (p + 1) * 1024]
                S.op("scalar", lambda s, o=gq, x2=xn2, s3=sx: s.mul(o, x2, s3),
                     reads=(f"xn2{p}", "sx"), writes=(f"gq{p}",))
                S.op("scalar", lambda s, o=i0f, i=gq: s.add(
                    o, i, cst[:, 267:268]),
                     reads=(f"gq{p}", "cst", "i0"), writes=("i0",))
                S.op("scalar", lambda s, o=ic, i=i0f: s.add(
                    o, i, cst[:, 268:269]),
                     reads=("i0", "cst"), writes=(f"ic{p}",))
                S.op(E, lambda v, o=dd, i=ic, g2=gq: v.tensor_sub(o, g2, i),
                     reads=(f"ic{p}", f"gq{p}"), writes=(f"xtb{p}",))
                S.op("vector", lambda v, o=mp, i=dd: v.tensor_single_scalar(
                    o, i, 0.5, ALU.is_gt),
                     reads=(f"xtb{p}",), writes=(f"mp{p}",))
                S.op("vector", lambda v, o=mdn, i=dd, m3=mp: v.scalar_tensor_tensor(
                    o, i, -0.5, m3, ALU.is_lt, ALU.subtract),
                     reads=(f"xtb{p}", f"mp{p}"), writes=(f"md{p}",))
                S.op("vector", lambda v, o=at, a=ic, b=mdn: v.tensor_sub(o, a, b),
                     reads=(f"ic{p}", f"md{p}"), writes=(f"at{p}",))
                S.dma("gpsimd", a_scr.ap()[tok0:tok0 + 128, :], at,
                      (f"at{p}",), ("as",))
            S.fence()

        with (
            nc.sbuf_tensor("abF", [128, 49152], BF16) as abF,
            nc.sbuf_tensor("afF", [128, 20736], F32) as afF,
        ):
            WG0, WU0, ATL = 0, 16384, 32768
            GUH, RMX0, RMX1 = 0, 8192, 10240
            ABG, ABU = 12288, 14336
            RA, RG, RU = 16384, 16512, 18560

            for c in range(8):
                S.dma("scalar", abF[:, ATL + c * 2048:ATL + (c + 1) * 2048],
                      a_scr.ap()[0:T, c * 128:(c + 1) * 128],
                      (), ("atb",), transpose=True)
            S.op("vector", lambda v: v.tensor_scalar_mul(sc[:, 0:16], maxr_nat,
                                                         mg127_v),
                 reads=("cst",), writes=("ag",))
            tpose(pb6[0:16, 0:128], sc[:, 0:16], ("ag",), ("pb6",))
            S.op("vector", lambda v: v.tensor_copy(afF[0:16, RA:RA + 128],
                                                   pb6[0:16, 0:128]),
                 reads=("pb6",), writes=("ra",))
            S.dma("gpsimd",
                  row_scr.ap()[0:1, 0:T].rearrange("a (b c) -> (a b) c", c=128),
                  afF[0:16, RA:RA + 128], ("ra",), ("rowh",))
            S.dma("sync", afF[0:1, RG:RG + T], row_scr.ap()[0:1, 0:T],
                  ("rowh",), ("rg",))
            S.op("vector", lambda v: v.tensor_scalar_mul(
                afF[0:1, RU:RU + T], afF[0:1, RG:RG + T], sc[0:1, 17:18]),
                 reads=("rg",), writes=("ru",))
            for src_off, dst, rk in ((RG, ABG, "rg"), (RU, ABU, "ru")):
                for q in range(4):
                    bk = pb[2 + q % 2]
                    mm(bk[:, 0:512], oner,
                       afF[0:1, src_off + q * 512:src_off + (q + 1) * 512],
                       (rk, "cst"), (f"pb{2 + q % 2}",))
                    cop(EV[q % 2], afF[:, dst + q * 512:dst + (q + 1) * 512],
                        bk[:, 0:512], (f"pb{2 + q % 2}",), ("ab",))
            S.op("vector", lambda v: v.memset(afF[:, RMX0:RMX0 + 2048], 0.0),
                 reads=(), writes=("rm0",))
            S.op("gpsimd", lambda g: g.memset(afF[:, RMX1:RMX1 + 2048], 0.0),
                 reads=(), writes=("rm1",))

            for half in range(2):
                for wi, (scr3, dst) in enumerate(((tg_scr, WG0), (tu_scr, WU0))):
                    for c in range(8):
                        S.dma("scalar", abF[:, dst + c * 2048:dst + (c + 1) * 2048],
                              scr3.ap()[half * 2048:(half + 1) * 2048,
                                        c * 128:(c + 1) * 128],
                              ("wgw" if wi == 0 else "wuw",),
                              ("wgw" if wi == 0 else "wuw",), transpose=True)
                for kc in range(16):
                    for tb2 in range(4):
                        q = (kc * 4 + tb2) % 4
                        bg, bu = pb[q], pb[4 + q]
                        for c in range(8):
                            mm(bg[:, 0:512],
                               abF[:, WG0 + c * 2048 + kc * 128:
                                   WG0 + c * 2048 + (kc + 1) * 128],
                               abF[:, ATL + c * 2048 + tb2 * 512:
                                   ATL + c * 2048 + (tb2 + 1) * 512],
                               ("wgw", "atb"), (f"pb{q}",),
                               start=(c == 0), stop=(c == 7))
                            mm(bu[:, 0:512],
                               abF[:, WU0 + c * 2048 + kc * 128:
                                   WU0 + c * 2048 + (kc + 1) * 128],
                               abF[:, ATL + c * 2048 + tb2 * 512:
                                   ATL + c * 2048 + (tb2 + 1) * 512],
                               ("wuw", "atb"), (f"pb{4 + q}",),
                               start=(c == 0), stop=(c == 7))
                        gb = afF[:, GUH + q * 2048:GUH + q * 2048 + 512]
                        gs = afF[:, GUH + q * 2048 + 512:GUH + q * 2048 + 1024]
                        ub = afF[:, GUH + q * 2048 + 1024:GUH + q * 2048 + 1536]
                        hb = afF[:, GUH + q * 2048 + 1536:GUH + q * 2048 + 2048]
                        S.op("vector", lambda v, o=gb, b=bg, t4=tb2: v.tensor_tensor(
                            o, b[:, 0:512],
                            afF[:, ABG + t4 * 512:ABG + (t4 + 1) * 512], ALU.mult),
                             reads=(f"pb{q}", "ab"), writes=(f"gb{q}",))
                        S.op("scalar", lambda s, o=gs, i=gb:
                             s.activation(o, i, AF.Silu),
                             reads=(f"gb{q}",), writes=(f"gs{q}",))
                        S.op("vector", lambda v, o=ub, b=bu, t4=tb2: v.tensor_tensor(
                            o, b[:, 0:512],
                            afF[:, ABU + t4 * 512:ABU + (t4 + 1) * 512], ALU.mult),
                             reads=(f"pb{4 + q}", "ab"), writes=(f"ub{q}",))
                        S.op("gpsimd", lambda g, o=hb, a=gs, b=ub: g.tensor_tensor(
                            o, a, b, ALU.mult),
                             reads=(f"gs{q}", f"ub{q}"), writes=(f"hb{q}",))
                        rmx0 = afF[:, RMX0 + tb2 * 512:RMX0 + (tb2 + 1) * 512]
                        rmx1 = afF[:, RMX1 + tb2 * 512:RMX1 + (tb2 + 1) * 512]
                        S.op("vector", lambda v, o=rmx0, b=hb: v.tensor_tensor(
                            o, o, b, ALU.max),
                             reads=(f"hb{q}", "rm0"), writes=("rm0",))
                        S.op("vector", lambda v, o=rmx1, b=hb: v.tensor_tensor(
                            o, o, b, ALU.min),
                             reads=(f"hb{q}", "rm1"), writes=("rm1",))
                        kg = half * 16 + kc
                        S.dma("sync",
                              h_scr.ap()[kg * 128:(kg + 1) * 128,
                                         tb2 * 512:(tb2 + 1) * 512],
                              hb, (f"hb{q}",), ("hs",))
            S.op("vector", lambda v: v.tensor_single_scalar(
                afF[:, RMX1:RMX1 + 2048], afF[:, RMX1:RMX1 + 2048], -1.0, ALU.mult),
                 reads=("rm1",), writes=("rm1",))
            S.op("vector", lambda v: v.tensor_tensor(
                afF[:, RMX0:RMX0 + 2048], afF[:, RMX0:RMX0 + 2048],
                afF[:, RMX1:RMX1 + 2048], ALU.max),
                 reads=("rm0", "rm1"), writes=("rm0",))
            for sl in range(16):
                bk = pb[sl % 2]
                tpose(bk[:, 0:128], afF[:, RMX0 + sl * 128:RMX0 + (sl + 1) * 128],
                      ("rm0",), (f"pb{sl % 2}",))
                S.op("vector", lambda v, b=bk, s3=sl: v.reduce_max(
                    maxh_nat[:, s3:s3 + 1], b[:, 0:128], axis=AX),
                     reads=(f"pb{sl % 2}",), writes=("maxh",))
            S.op("vector", lambda v: v.tensor_single_scalar(
                maxh_nat, maxh_nat, 1e-5, ALU.max),
                 reads=("maxh",), writes=("maxh",))
            S.fence()

        with (
            nc.sbuf_tensor("abG", [128, 38912], BF16) as abG,
            nc.sbuf_tensor("afG", [128, 22656], F32) as afG,
        ):
            WD0, HQ, MD2 = 0, 32768, 34816
            HT, G3, IC2, DD2 = 0, 2048, 4096, 6144
            SYB, BDB, FFT = 8192, 10240, 12288
            X1T, OUT = 16384, 17408
            ROWW, RMH, RROW = 18432, 18560, 20608

            for kc in range(32):
                S.dma("scalar", abG[:, WD0 + kc * 1024:WD0 + (kc + 1) * 1024],
                      td_scr.ap()[0:D, kc * 128:(kc + 1) * 128],
                      (), ("wdt",), transpose=True)
            tpose(pb6[0:16, 0:128], maxh_nat, ("cst",), ("pb6",))
            S.op("vector", lambda v: v.tensor_copy(afG[0:16, ROWW:ROWW + 128],
                                                   pb6[0:16, 0:128]),
                 reads=("pb6",), writes=("rw",))
            S.dma("gpsimd",
                  row_scr.ap()[2:3, 0:T].rearrange("a (b c) -> (a b) c", c=128),
                  afG[0:16, ROWW:ROWW + 128], ("rw",), ("rowh2",))
            S.dma("sync", afG[0:1, RMH:RMH + T], row_scr.ap()[2:3, 0:T],
                  ("rowh2",), ("rmh",))
            S.op("vector", lambda v: v.reciprocal(afG[0:1, RROW:RROW + T],
                                                  afG[0:1, RMH:RMH + T]),
                 reads=("rmh",), writes=("rrow",))
            S.op("vector", lambda v: v.tensor_single_scalar(
                afG[0:1, RROW:RROW + T], afG[0:1, RROW:RROW + T], 127.0, ALU.mult),
                 reads=("rrow",), writes=("rrow",))
            for q in range(4):
                bk = pb[2 + q % 2]
                mm(bk[:, 0:512], oner, afG[0:1, RROW + q * 512:RROW + (q + 1) * 512],
                   ("rrow", "cst"), (f"pb{2 + q % 2}",))
                cop(EV[q % 2], afG[:, SYB + q * 512:SYB + (q + 1) * 512],
                    bk[:, 0:512], (f"pb{2 + q % 2}",), ("syb",))
            S.op("vector", lambda v: v.tensor_scalar_mul(
                afG[0:1, RROW:RROW + T], afG[0:1, RMH:RMH + T], sc[0:1, 16:17]),
                 reads=("rmh",), writes=("rrow",))
            for q in range(4):
                bk = pb[2 + q % 2]
                mm(bk[:, 0:512], oner, afG[0:1, RROW + q * 512:RROW + (q + 1) * 512],
                   ("rrow", "cst"), (f"pb{2 + q % 2}",))
                cop(EV[q % 2], afG[:, BDB + q * 512:BDB + (q + 1) * 512],
                    bk[:, 0:512], (f"pb{2 + q % 2}",), ("bdb",))

            for tb2 in range(4):
                for kc in range(32):
                    p = kc % 4
                    ht = afG[:, HT + p * 512:HT + (p + 1) * 512]
                    S.dma("sync", ht,
                          h_scr.ap()[kc * 128:(kc + 1) * 128,
                                     tb2 * 512:(tb2 + 1) * 512],
                          (f"ht{p}",), (f"ht{p}",))
                    g3 = afG[:, G3 + p * 512:G3 + (p + 1) * 512]
                    ic = afG[:, IC2 + p * 512:IC2 + (p + 1) * 512]
                    dd = afG[:, DD2 + p * 512:DD2 + (p + 1) * 512]
                    i0f = i0b[:, 0:512].bitcast(F32)
                    mp = abG[:, MD2 + p * 512:MD2 + (p + 1) * 512]
                    mdn = abG[:, MD2 + 1024 + p * 512:MD2 + 1024 + (p + 1) * 512]
                    hq = abG[:, HQ + p * 512:HQ + (p + 1) * 512]
                    S.op("gpsimd", lambda g, o=g3, a=ht, t4=tb2: g.tensor_tensor(
                        o, a, afG[:, SYB + t4 * 512:SYB + (t4 + 1) * 512], ALU.mult),
                         reads=(f"ht{p}", "syb"), writes=(f"g3{p}",))
                    S.op("scalar", lambda s, o=i0f, i=g3: s.add(
                        o, i, cst[:, 267:268]),
                         reads=(f"g3{p}", "cst", "i0"), writes=("i0",))
                    S.op("scalar", lambda s, o=ic, i=i0f: s.add(
                        o, i, cst[:, 268:269]),
                         reads=("i0", "cst"), writes=(f"ic{p}",))
                    S.op("gpsimd", lambda g, o=dd, a=ic, b=g3: g.tensor_sub(o, b, a),
                         reads=(f"ic{p}", f"g3{p}"), writes=(f"dd{p}",))
                    S.op("vector", lambda v, o=mp, i=dd: v.tensor_single_scalar(
                        o, i, 0.5, ALU.is_gt),
                         reads=(f"dd{p}",), writes=(f"mp{p}",))
                    S.op("vector", lambda v, o=mdn, i=dd, m3=mp:
                         v.scalar_tensor_tensor(
                             o, i, -0.5, m3, ALU.is_lt, ALU.subtract),
                         reads=(f"dd{p}", f"mp{p}",), writes=(f"md{p}",))
                    S.op("vector", lambda v, o=hq, a=ic, b=mdn: v.tensor_sub(
                        o, a, b),
                         reads=(f"ic{p}", f"md{p}"), writes=(f"hq{p}",))
                    for m2 in range(8):
                        mm(pb[m2][:, 0:512],
                           abG[:, WD0 + kc * 1024 + m2 * 128:
                               WD0 + kc * 1024 + (m2 + 1) * 128],
                           hq, ("wdt", f"hq{p}"), (f"pb{m2}",),
                           start=(kc == 0), stop=(kc == 31))
                for m2 in range(8):
                    S.op("vector", lambda v, b=pb[m2], m3=m2, t4=tb2:
                         v.tensor_tensor(
                             afG[:, FFT + m3 * 512:FFT + (m3 + 1) * 512],
                             b[:, 0:512],
                             afG[:, BDB + t4 * 512:BDB + (t4 + 1) * 512], ALU.mult),
                         reads=(f"pb{m2}", "bdb"), writes=("fft",))
                for sl in range(4):
                    tok0 = tb2 * 512 + sl * 128
                    x1t = afG[:, X1T:X1T + 1024]
                    S.dma("sync", x1t, x1_scr.ap()[tok0:tok0 + 128, :],
                          ("x1t", "x1s"), ("x1t",))
                    for m2 in range(8):
                        tpose(pb[m2 // 4][:, (m2 % 4) * 128:(m2 % 4) * 128 + 128],
                              afG[:, FFT + m2 * 512 + sl * 128:
                                  FFT + m2 * 512 + sl * 128 + 128],
                              ("fft",), (f"pb{m2 // 4}",))
                    ot = afG[:, OUT:OUT + 1024]
                    for b2 in range(2):
                        S.op("vector", lambda v, b=pb[b2], b3=b2, o=ot, x2=x1t:
                             v.tensor_tensor(o[:, b3 * 512:(b3 + 1) * 512],
                                             b[:, 0:512],
                                             x2[:, b3 * 512:(b3 + 1) * 512], ALU.add),
                             reads=(f"pb{b2}", "x1t"), writes=("ot",))
                    S.dma("scalar", y_h.ap()[tok0:tok0 + 128, :], ot,
                          ("ot",), ("ys",))
            S.fence()
        S.finalize()

    return nc


_NC_CACHE = None


def mk_mask(first_pair):
    m = np.zeros((64, 2560), np.float32)
    j = np.arange(64)
    for h in range(H):
        for t2 in range(2):
            ci = 2 * h + t2
            for q in range(64):
                m[q, ci * 80:ci * 80 + 64][j > q] = NEG
            if first_pair and t2 == 0:
                m[:, ci * 80 + 64:ci * 80 + 80] = NEG
    return m.astype(ml_dtypes.bfloat16)


def kernel(L0_attn_hidden_states, L0_attn_Wq, L0_attn_Wk, L0_attn_Wv,
           L0_attn_Wo, L0_ffn_w_gate, L0_ffn_w_up, L0_ffn_w_down):
    global _NC_CACHE
    x = np.ascontiguousarray(L0_attn_hidden_states, dtype=np.float32)
    B, Sq, Dm = x.shape
    xf = x.reshape(B * Sq, Dm)

    eye = np.eye(128, dtype=np.float32)
    onec = np.ones((128, 1), np.float32)
    oner = np.ones((1, 128), np.float32)

    mask_n = np.vstack([mk_mask(False)] * 2)

    if _NC_CACHE is None:
        _NC_CACHE = build_nc()
    nc = _NC_CACHE

    in_maps = []
    for c in range(8):
        b = c // 2
        sh = c % 2
        in_maps.append({
            "x": np.ascontiguousarray(xf[c * T:(c + 1) * T]),
            "xg": np.ascontiguousarray(x[b, 0:16, :]),
            "wq": np.ascontiguousarray(L0_attn_Wq, np.float32),
            "wk": np.ascontiguousarray(L0_attn_Wk, np.float32),
            "wv": np.ascontiguousarray(L0_attn_Wv, np.float32),
            "wo": np.ascontiguousarray(L0_attn_Wo, np.float32),
            "wg": np.ascontiguousarray(L0_ffn_w_gate, np.float32),
            "wu": np.ascontiguousarray(L0_ffn_w_up, np.float32),
            "wd": np.ascontiguousarray(L0_ffn_w_down, np.float32),
            "eye": eye, "onec": onec, "oner": oner,
            "maskn": mask_n,
            "gmask": np.full((128, 16), NEG if sh == 0 else 0.0,
                             ml_dtypes.bfloat16),
        })
    res = run_bass_kernel_spmd(nc, in_maps, core_ids=list(range(8)))
    out = np.concatenate([res.results[c]["y"] for c in range(8)], axis=0)
    return out.reshape(B, Sq, Dm)
